# version 02
# baseline (speedup 1.0000x reference)
import contextlib
import numpy as np
import ml_dtypes
import concourse.bass as bass
import concourse.mybir as mybir
from concourse.bass_utils import run_bass_kernel_spmd

F32 = mybir.dt.float32
BF16 = mybir.dt.bfloat16
AF = mybir.ActivationFunctionType
ALU = mybir.AluOpType
AX = mybir.AxisListType

NCORE = 8
T = 2048
D = 1024
DCH = 8
BIG_NEG = -1.0e30

ALL_PHASES = ("ffn0", "attn0", "ffn0", "ffn1", "attn1", "ffn1")
FUSED = True
SERIALIZE = False

ENG2SECTION = {"pe": "tensor", "act": "scalar", "dve": "vector",
               "pool": "gpsimd", "sp": "sync"}
QUEUES = ("qi", "qw0", "qw1", "qw2", "qw3", "qc0", "qc1", "qsx")


class Prog:
    def __init__(self):
        self.nc = bass.Bass("TRN2", target_bir_lowering=False)
        self.st = contextlib.ExitStack()
        self.ops = {e: [] for e in ENG2SECTION}
        self.count = {s: 0 for s in list(ENG2SECTION) + list(QUEUES)}
        self.writer = {}
        self.readers = {}
        self.marks = {}
        self.scope_ser = False

    def sbuf(self, name, shape, dt):
        return self.st.enter_context(self.nc.sbuf_tensor(name, shape, dt))

    def psum(self, name, shape, dt=F32):
        return self.st.enter_context(self.nc.psum_tensor(name, shape, dt))

    def op(self, eng, emit, reads=(), writes=(), dma=None, sreads=(), fence=False):
        stream = dma if dma else eng
        if dma is None and eng in ("dve", "act", "pool"):
            fence = True
        waits = {}
        if SERIALIZE or self.scope_ser:
            for s, cv in self.count.items():
                if s == eng or cv == 0:
                    continue
                w = self.marks.get((eng, s), 0)
                if cv > w:
                    waits[s] = cv
                    self.marks[(eng, s)] = cv

        def need(pstream, cnt, force=False):
            if cnt <= 0:
                return
            if pstream == eng and dma is None and not force:
                return
            if pstream in QUEUES:
                cnt = self.count[pstream]
            w = self.marks.get((eng, pstream), 0)
            if cnt > w:
                waits[pstream] = max(waits.get(pstream, 0), cnt)
                self.marks[(eng, pstream)] = cnt

        for k in reads:
            lw = self.writer.get(k)
            if lw:
                need(lw[0], lw[1], force=lw[2])
        for k in sreads:
            lw = self.writer.get(k)
            if lw:
                need(lw[0], lw[1], force=True)
        for k in writes:
            lw = self.writer.get(k)
            if lw:
                need(lw[0], lw[1], force=lw[2])
            for r in self.readers.get(k, ()):
                need(*r)
        if dma is not None and self.count[stream] > 0:
            w = self.marks.get((eng, stream), 0)
            if self.count[stream] > w:
                waits[stream] = max(waits.get(stream, 0), self.count[stream])
                self.marks[(eng, stream)] = self.count[stream]
        self.count[stream] += 16 if dma else 1
        cnt = self.count[stream]
        for k in list(reads) + list(sreads):
            self.readers.setdefault(k, []).append((stream, cnt))
        for k in writes:
            self.writer[k] = (stream, cnt, fence)
            self.readers[k] = []
        self.ops[eng].append((sorted(waits.items()), emit, stream, 16 if dma else 1))

    def barrier_waits(self, eng, streams=None):
        waits = []
        for s in (streams or list(self.count)):
            if s == eng or self.count[s] == 0:
                continue
            w = self.marks.get((eng, s), 0)
            if self.count[s] > w:
                waits.append((s, self.count[s]))
                self.marks[(eng, s)] = self.count[s]
        if waits:
            self.ops[eng].append((waits, None, None, 0))

    def finalize(self):
        used = [s for s in self.count if self.count[s] > 0]
        sems = {s: self.st.enter_context(self.nc.semaphore(s)) for s in used}
        blk = self.st.enter_context(self.nc.Block())
        for eng, section_name in ENG2SECTION.items():
            if not self.ops[eng]:
                continue
            section = getattr(blk, section_name)
            eng_ops = self.ops[eng]

            def body(engobj, eng_ops=eng_ops):
                for waits, emit, stream, incv in eng_ops:
                    for s, cval in waits:
                        engobj.wait_ge(sems[s], cval)
                    if emit is not None:
                        ins = emit(engobj)
                        ins.then_inc(sems[stream], incv)
            section(body)
        self.st.close()
        return self.nc


class Buf:
    def __init__(self, v, keys):
        self.v = v
        self.k = list(keys)



def pretile_lhsT(W):
    K, M = W.shape
    return np.ascontiguousarray(
        W.reshape(K // 128, 128, M // 128, 128).transpose(2, 1, 0, 3))


def pretile_rhs(W):
    K, N = W.shape
    return np.ascontiguousarray(W.reshape(K // 128, 128, N))


def resident_rhs(W):
    K, n = W.shape
    return np.ascontiguousarray(W.reshape(K // 128, 128, n).transpose(1, 0, 2))


def bias_pp(b):
    return np.ascontiguousarray(b.reshape(-1, 128).T)


def prep_host(inputs):
    g = {k: np.asarray(v) for k, v in inputs.items()}
    H = {}
    w1s = [g["L0_ffn_ew1"][e] for e in range(8)] + [g["L0_ffn_sw1"]]
    w2s = [g["L0_ffn_ew2"][e] for e in range(8)] + [g["L0_ffn_sw2"]]
    w3s = [g["L0_ffn_ew3"][e] for e in range(8)] + [g["L0_ffn_sw3"]]
    H["F0A"] = np.stack([pretile_lhsT(w) for w in w1s])
    H["F0B"] = np.stack([pretile_lhsT(w) for w in w2s])

    def grp_split(arr, ngrp):
        dc, pp, kc, m = arr.shape
        return np.ascontiguousarray(
            arr.reshape(dc, pp, ngrp, kc // ngrp, m).transpose(2, 0, 1, 3, 4))
    H["F0C"] = np.stack([grp_split(pretile_lhsT(w), 1) for w in w3s])
    H["gw0"] = resident_rhs(g["L0_ffn_gate_w"])
    H["F1A"] = np.stack([pretile_lhsT(g["L1_ffn_W1"][e].T) for e in range(8)])
    H["F1B"] = np.stack([pretile_lhsT(g["L1_ffn_W3"][e].T) for e in range(8)])
    H["F1C"] = np.stack([grp_split(pretile_lhsT(g["L1_ffn_W2"][e].T), 2)
                         for e in range(8)])
    H["gw1"] = resident_rhs(np.ascontiguousarray(g["L1_ffn_Wr"].T))
    Wqk = g["L0_attn_Wqk"]
    H["Wq_l"] = pretile_lhsT(np.ascontiguousarray(Wqk[:, :1024]))
    H["Wk_l"] = pretile_lhsT(np.ascontiguousarray(Wqk[:, 1024:]))
    H["Wk_r"] = pretile_rhs(np.ascontiguousarray(Wqk[:, 1024:]))
    H["Wv_r"] = pretile_rhs(g["L0_attn_Wv"])
    H["Wo_l"] = pretile_lhsT(g["L0_attn_Wo"])
    H["Wn_r"] = resident_rhs(g["L0_attn_Wn"])
    H["bq_pp"] = bias_pp(g["L0_attn_bqk"][:1024])
    H["bk_pp"] = bias_pp(g["L0_attn_bqk"][1024:])
    H["bk_row"] = np.ascontiguousarray(g["L0_attn_bqk"][1024:]).reshape(1, 1024)
    H["bv_row"] = np.ascontiguousarray(g["L0_attn_bv"]).reshape(1, 1024)
    H["bn_pp"] = np.ascontiguousarray(g["L0_attn_bn"]).reshape(16, 1)
    H["bo_pp"] = bias_pp(g["L0_attn_bo"])
    H["Wi_l"] = pretile_lhsT(np.ascontiguousarray(g["L1_attn_W_in"][1024:, :].T))
    H["Wg_l"] = pretile_lhsT(np.ascontiguousarray(g["L1_attn_W_gate"].T))
    H["Bm_r"] = resident_rhs(np.ascontiguousarray(g["L1_attn_Bm"].T))
    H["Cm_r"] = resident_rhs(g["L1_attn_Cm"])
    H["WoT_r"] = pretile_rhs(np.ascontiguousarray(g["L1_attn_W_out"].T))
    H["A_lhs"] = np.ascontiguousarray(g["L1_attn_A"])
    H["AT"] = np.ascontiguousarray(g["L1_attn_A"].T)
    H["bi_pp"] = bias_pp(g["L1_attn_b_in"][1024:])
    H["bg_pp"] = bias_pp(g["L1_attn_b_gate"])
    H["bout_pp"] = bias_pp(g["L1_attn_b_out"])
    H["I128"] = np.eye(128, dtype=np.float32)
    H["I16bf"] = np.eye(16, dtype=np.float32).astype(ml_dtypes.bfloat16)
    H["I16f"] = np.eye(16, dtype=np.float32)
    H["ones_col_bf"] = np.ones((128, 1), dtype=np.float32).astype(ml_dtypes.bfloat16)
    H["ones_row"] = np.ones((1, 128), dtype=np.float32)
    H["ones_row_bf"] = np.ones((1, 128), dtype=np.float32).astype(ml_dtypes.bfloat16)
    H["triuT"] = np.triu(np.ones((128, 128), dtype=np.float32))
    PC = []
    x = g["L0_attn_x"]
    for cid in range(NCORE):
        b, hh = divmod(cid, 2)
        d = {"xin": np.ascontiguousarray(x[b, hh * T:(hh + 1) * T, :].T)}
        sel = np.zeros((128, 16), dtype=np.float32)
        if hh == 1:
            for i in range(16):
                sel[(cid - 1) * 16 + i, i] = 1.0
        d["selmat"] = sel
        PC.append(d)
    return H, PC



PHASE_WEIGHTS = {
    "ffn0": ["F0A", "F0B", "F0C", "gw0"],
    "ffn1": ["F1A", "F1B", "F1C", "gw1"],
    "attn0": ["Wq_l", "Wk_l", "Wk_r", "Wv_r", "Wo_l", "Wn_r", "bq_pp", "bk_pp",
              "bk_row", "bv_row", "bn_pp", "bo_pp", "selmat"],
    "attn1": ["Wi_l", "Wg_l", "Bm_r", "Cm_r", "WoT_r", "A_lhs", "AT", "I16f",
              "bi_pp", "bg_pp", "bout_pp", "selmat"],
}
CONST_NAMES = ["I128", "I16bf", "ones_col_bf", "ones_row", "ones_row_bf", "triuT"]

NSCR = 49
NSCF = 7



def wbk(unit, half=None):
    if half is None:
        return [("wb", unit, 0), ("wb", unit, 1)]
    return [("wb", unit, half)]

class Ctx:
    pass


def alloc(p, phases, host):
    c = Ctx()
    nc = p.nc
    c.din = {}
    names = set(CONST_NAMES)
    for ph in phases:
        names.update(PHASE_WEIGHTS[ph])
    names.add("xin")
    for n in sorted(names):
        arr = host.get(n)
        if arr is None:
            if n == "xin":
                arr = np.zeros((D, T), np.float32)
            elif n == "selmat":
                arr = np.zeros((128, 16), np.float32)
        dt = BF16 if arr.dtype == ml_dtypes.bfloat16 else F32
        c.din[n] = nc.dram_tensor(n, list(arr.shape), dt, kind="ExternalInput").ap()
    c.yout = nc.dram_tensor("y", [D, T], F32, kind="ExternalOutput").ap()
    c.innames = names

    c.x = [p.sbuf(f"x{i}", [128, T], F32) for i in range(DCH)]
    c.xn = [p.sbuf(f"xn{i}", [128, T], BF16) for i in range(DCH)]
    c.SCR = p.sbuf("SCR", [128, NSCR * 512], BF16)
    c.SCF = p.sbuf("SCF", [128, NSCF * 512], F32)
    c.wbuf = [p.sbuf(f"wbuf{i}", [128, 1024], BF16) for i in range(4)]
    c.w2buf = [p.sbuf(f"w2buf{i}", [128, 2048], BF16) for i in range(2)]
    c.ps = [p.psum(f"ps{i}", [128, 512]) for i in range(8)]
    c.wb_sb = p.sbuf("wb_sb", [128, 1024], F32)
    c.w_tm = p.sbuf("w_tm", [128, 128], F32)
    c.wrow = [p.sbuf(f"wrow{i}", [1, 512], F32) for i in range(2)]
    c.stat = p.sbuf("stat", [1, 1536], F32)
    c.rst = p.sbuf("rst", [128, 16], F32)
    c.r8 = [p.sbuf(f"r8_{i}", [128, 8], F32) for i in range(4)]
    c.gS = p.sbuf("gS", [128, 8], F32)
    c.I128 = p.sbuf("cI128", [128, 128], F32)
    c.I16bf = p.sbuf("cI16", [16, 16], BF16)
    c.ones_col_bf = p.sbuf("cOnesCol", [128, 1], BF16)
    c.ones_row = p.sbuf("cOnesRow", [1, 128], F32)
    c.ones_row_bf = p.sbuf("cOnesRowBf", [1, 128], BF16)
    c.triuT = p.sbuf("cTriu", [128, 128], F32)
    c.gw = {}
    if "ffn0" in phases:
        c.gw["ffn0"] = p.sbuf("gw0sb", [128, 8, 8], BF16)
    if "ffn1" in phases:
        c.gw["ffn1"] = p.sbuf("gw1sb", [128, 8, 8], BF16)
    c.dwscr = nc.dram_tensor("dwscr", [16, T], F32).ap()
    need_attn = ("attn0" in phases) or ("attn1" in phases)
    if need_attn:
        c.selm = p.sbuf("selm", [128, 16], F32)
    if "attn0" in phases:
        c.biases0 = {n: p.sbuf("sb_" + n, list(host[n].shape), F32)
                     for n in ("bq_pp", "bk_pp", "bo_pp", "bn_pp")}
        c.negbn = p.sbuf("negbn", [16, 1], F32)
        c.Wn = p.sbuf("WnSb", [128, 8, 16], BF16)
        c.bk_bf = p.sbuf("bk_bf", [1, 1024], BF16)
        c.bv_bf = p.sbuf("bv_bf", [1, 1024], BF16)
        c.nrow = [p.sbuf("nrow0", [1, 512], F32)]
        c.dnscr = nc.dram_tensor("dnscr", [16, 512], F32).ap()
        c.cinL0 = nc.dram_tensor("cinL0", [64, 1024], F32)
        c.coutL0 = nc.dram_tensor("coutL0", [512, 1024], F32)
        c.dSp = nc.dram_tensor("dSp", [64, 1024], F32).ap()
    if "attn1" in phases:
        c.biases1 = {n: p.sbuf("sb_" + n, list(host[n].shape), F32)
                     for n in ("bi_pp", "bg_pp", "bout_pp")}
        c.Bm = p.sbuf("BmSb", [128, 8, 16], BF16)
        c.Cm = p.sbuf("CmSb", [128, 8, 16], BF16)
        c.A_sb = p.sbuf("A_sb", [16, 16], F32)
        c.I16f = p.sbuf("sI16f", [16, 16], F32)
        c.svec = [p.sbuf(f"svec{i}", [16, 1], F32) for i in range(2)]
        c.cinL1 = nc.dram_tensor("cinL1", [16, 1], F32)
        c.coutL1 = nc.dram_tensor("coutL1", [128, 1], F32)
        c.Gl1 = p.sbuf("Gl1", [128, 1], F32)
    return c


def scr(c, g, ngr=1, parts=128, p0=0):
    v = c.SCR[p0:p0 + parts, g * 512:(g + ngr) * 512]
    return Buf(v, [("scr", g + k) for k in range(ngr)])


def scf(c, g, ngr=1, parts=128, p0=0):
    v = c.SCF[p0:p0 + parts, g * 512:(g + ngr) * 512]
    return Buf(v, [("scf", g + k) for k in range(ngr)])


def emit_load_common(p, c, phases):
    for i in range(DCH):
        p.op("sp", lambda s, i=i: s.dma_start(out=c.x[i][:, :], in_=c.din["xin"][i * 128:(i + 1) * 128, :]),
             writes=[("x", i, j) for j in range(4)], dma="qsx")
    loads = [(c.I128, "I128"), (c.I16bf, "I16bf"), (c.ones_col_bf, "ones_col_bf"),
             (c.ones_row, "ones_row"), (c.ones_row_bf, "ones_row_bf"), (c.triuT, "triuT")]
    for t, n in loads:
        p.op("sp", lambda s, t=t, n=n: s.dma_start(out=t[:, :], in_=c.din[n]),
             writes=[("s", n)], dma="qsx")
    for ph, t in c.gw.items():
        n = "gw0" if ph == "ffn0" else "gw1"
        p.op("pool", lambda g, t=t, n=n: g.dma_start(out=t[:, :, :], in_=c.din[n]),
             writes=[("s", "gw_" + ph)], dma="qi")
    if hasattr(c, "selm"):
        p.op("sp", lambda s: s.dma_start(out=c.selm[:, :], in_=c.din["selmat"]),
             writes=[("s", "selm")], dma="qsx")
    if "attn0" in phases:
        for n, t in c.biases0.items():
            p.op("sp", lambda s, t=t, n=n: s.dma_start(out=t[:, :], in_=c.din[n]),
                 writes=[("s", n)], dma="qsx")
        p.op("pool", lambda g: g.dma_start(out=c.Wn[:, :, :], in_=c.din["Wn_r"]),
             writes=[("s", "Wn")], dma="qi")
        p.op("pool", lambda g: g.dma_start(out=c.bk_bf[:, :], in_=c.din["bk_row"]),
             writes=[("s", "bk_bf")], dma="qi")
        p.op("pool", lambda g: g.dma_start(out=c.bv_bf[:, :], in_=c.din["bv_row"]),
             writes=[("s", "bv_bf")], dma="qi")
        p.op("dve", lambda v: v.tensor_scalar(c.negbn[:, :], c.biases0["bn_pp"][:, :], -1.0, None, ALU.mult),
             reads=[("s", "bn_pp")], writes=[("s", "negbn")])
    if "attn1" in phases:
        for n, t in c.biases1.items():
            p.op("sp", lambda s, t=t, n=n: s.dma_start(out=t[:, :], in_=c.din[n]),
                 writes=[("s", n)], dma="qsx")
        p.op("pool", lambda g: g.dma_start(out=c.Bm[:, :, :], in_=c.din["Bm_r"]),
             writes=[("s", "Bm")], dma="qi")
        p.op("pool", lambda g: g.dma_start(out=c.Cm[:, :, :], in_=c.din["Cm_r"]),
             writes=[("s", "Cm")], dma="qi")
        p.op("sp", lambda s: s.dma_start(out=c.A_sb[:, :], in_=c.din["A_lhs"]),
             writes=[("s", "A_sb")], dma="qsx")
        p.op("sp", lambda s: s.dma_start(out=c.I16f[:, :], in_=c.din["I16f"]),
             writes=[("s", "I16f")], dma="qsx")


def emit_store(p, c):
    for i in range(DCH):
        p.op("sp", lambda s, i=i: s.dma_start(out=c.yout[i * 128:(i + 1) * 128, :], in_=c.x[i][:, :]),
             reads=[("x", i, j) for j in range(4)], dma="qsx")
    p.barrier_waits("sp")



def emit_ln(p, c):
    for j in range(4):
        sl = slice(j * 512, (j + 1) * 512)
        for i in range(DCH):
            xb = scr(c, 40 + i % 2)
            xq = scr(c, 42 + i % 2)
            p.op("dve", lambda v, i=i, xb=xb, sl=sl: v.tensor_copy(xb.v, c.x[i][:, sl]),
                 reads=[("x", i, j)], writes=xb.k)
            p.op("pe", lambda t, xb=xb, i=i: t.matmul(
                c.ps[6][0:1, :], c.ones_col_bf[:, :], xb.v,
                start=(i == 0), stop=(i == DCH - 1)),
                reads=xb.k + [("s", "ones_col_bf")], writes=[("ps", 6)])
            p.op("act", lambda s, i=i, xq=xq, sl=sl: s.activation(xq.v, c.x[i][:, sl], AF.Square, bias=0.0, scale=1.0),
                 reads=[("x", i, j)], writes=xq.k)
            p.op("pe", lambda t, xq=xq, i=i: t.matmul(
                c.ps[7][0:1, :], c.ones_col_bf[:, :], xq.v,
                start=(i == 0), stop=(i == DCH - 1)),
                reads=xq.k + [("s", "ones_col_bf")], writes=[("ps", 7)])
        m_row = c.stat[0:1, 0:512]
        rows = Buf(None, [("s", "lnrows")])
        t0 = c.stat[0:1, 512:1024]
        t1 = c.stat[0:1, 1024:1536]
        p.op("dve", lambda v, m_row=m_row: v.tensor_scalar(m_row, c.ps[6][0:1, :], 1.0 / D, None, ALU.mult),
             reads=[("ps", 6)], writes=[("s", "lnm")])
        p.op("dve", lambda v: v.tensor_scalar(t0, c.ps[7][0:1, :], 1.0 / D, None, ALU.mult),
             reads=[("ps", 7)], writes=rows.k)
        p.op("dve", lambda v, m_row=m_row: v.tensor_tensor(t1, m_row, m_row, ALU.mult),
             reads=[("s", "lnm")], writes=rows.k)
        p.op("dve", lambda v: v.tensor_tensor(t0, t0, t1, ALU.subtract),
             reads=rows.k, writes=rows.k)
        p.op("dve", lambda v: v.tensor_scalar(t0, t0, 1e-5, None, ALU.add),
             reads=rows.k, writes=rows.k)
        p.op("act", lambda s: s.activation(t1, t0, AF.Sqrt, bias=0.0, scale=1.0),
             reads=rows.k, writes=rows.k)
        p.op("dve", lambda v: v.reciprocal(t0, t1), reads=rows.k, writes=rows.k, fence=True)
        p.op("dve", lambda v, m_row=m_row: v.tensor_tensor(t1, m_row, t0, ALU.mult),
             reads=[("s", "lnm")] + rows.k, writes=rows.k)
        p.op("dve", lambda v: v.tensor_scalar(t1, t1, -1.0, None, ALU.mult),
             reads=rows.k, writes=rows.k)
        p.op("pe", lambda t: t.matmul(c.ps[4][:, :], c.ones_row[:, :], t0, start=True, stop=True),
             reads=rows.k + [("s", "ones_row")], writes=[("ps", 4)], fence=True)
        p.op("pe", lambda t: t.matmul(c.ps[5][:, :], c.ones_row[:, :], t1, start=True, stop=True),
             reads=rows.k + [("s", "ones_row")], writes=[("ps", 5)])
        tt = scf(c, 4)
        for i in range(DCH):
            p.op("dve", lambda v, i=i, sl=sl, tt=tt: v.tensor_tensor(tt.v, c.x[i][:, sl], c.ps[4][:, :], ALU.mult),
                 reads=[("x", i, j), ("ps", 4)], writes=tt.k)
            p.op("dve", lambda v, i=i, sl=sl, tt=tt: v.tensor_tensor(c.xn[i][:, sl], tt.v, c.ps[5][:, :], ALU.add),
                 reads=tt.k + [("ps", 5)], writes=[("xn", i, j)])



def emit_moe(p, c, kind):
    shared = kind == "ffn0"
    nexp = 8
    H = 2048 if kind == "ffn0" else 4096
    NGRP = (H // 128) // 16
    FA, FB, FC = {"ffn0": ("F0A", "F0B", "F0C"), "ffn1": ("F1A", "F1B", "F1C")}[kind]
    gw = c.gw[kind]

    for tc in range(16):
        j = tc // 4

        def g_mm(t, tc=tc):
            ins = None
            for kc in range(DCH):
                ins = t.matmul(c.ps[6][:, 0:8], c.xn[kc][:, tc * 128:(tc + 1) * 128],
                               gw[:, kc, :], start=(kc == 0), stop=(kc == DCH - 1))
            return ins
        p.op("pe", g_mm, reads=[("xn", kc, j) for kc in range(DCH)] + [("s", "gw_" + kind)],
             writes=[("ps", 6)])
        r0 = c.rst
        e1, e2, pb, tb = c.r8[0], c.r8[1], c.r8[2], c.r8[3]
        gS = c.gS
        wsl = c.w_tm[:, tc * 8:(tc + 1) * 8]
        p.op("dve", lambda v: v.tensor_copy(gS[:, :], c.ps[6][:, 0:8]),
             reads=[("ps", 6)], writes=[("s", "gS")])
        if kind == "ffn0":
            p.op("dve", lambda v: v.reduce_max(r0[:, 0:1], gS[:, :], axis=AX.X),
                 reads=[("s", "gS")], writes=[("s", "r0")], fence=True)
            p.op("dve", lambda v: v.tensor_scalar(r0[:, 1:2], r0[:, 0:1], -1.0, None, ALU.mult),
                 reads=[("s", "r0")], writes=[("s", "r1")])
            p.op("act", lambda s: s.activation(e1[:, :], gS[:, :], AF.Identity, bias=r0[:, 1:2], scale=1.0),
                 reads=[("s", "gS"), ("s", "r1")], writes=[("s", "e1")])
            p.op("dve", lambda v: v.tensor_scalar(e1[:, :], e1[:, :], 0.0, None, ALU.is_ge),
                 reads=[("s", "e1")], writes=[("s", "e1")])
            p.op("dve", lambda v: v.tensor_scalar(tb[:, :], e1[:, :], BIG_NEG, None, ALU.mult),
                 reads=[("s", "e1")], writes=[("s", "tb")])
            p.op("dve", lambda v: v.tensor_tensor(pb[:, :], gS[:, :], tb[:, :], ALU.add),
                 reads=[("s", "gS"), ("s", "tb")], writes=[("s", "pb")])
            p.op("dve", lambda v: v.reduce_max(r0[:, 2:3], pb[:, :], axis=AX.X),
                 reads=[("s", "pb")], writes=[("s", "r2")], fence=True)
            p.op("dve", lambda v: v.tensor_scalar(r0[:, 3:4], r0[:, 2:3], -1.0, None, ALU.mult),
                 reads=[("s", "r2")], writes=[("s", "r3")])
            p.op("act", lambda s: s.activation(e2[:, :], pb[:, :], AF.Identity, bias=r0[:, 3:4], scale=1.0),
                 reads=[("s", "pb"), ("s", "r3")], writes=[("s", "e2")])
            p.op("dve", lambda v: v.tensor_scalar(e2[:, :], e2[:, :], 0.0, None, ALU.is_ge),
                 reads=[("s", "e2")], writes=[("s", "e2")])
            p.op("dve", lambda v: v.tensor_tensor(r0[:, 4:5], r0[:, 0:1], r0[:, 2:3], ALU.subtract),
                 reads=[("s", "r0"), ("s", "r2")], writes=[("s", "r4")])
            p.op("act", lambda s: s.activation(r0[:, 5:6], r0[:, 4:5], AF.Sigmoid, bias=0.0, scale=1.0),
                 reads=[("s", "r4")], writes=[("s", "r5")])
            p.op("act", lambda s: s.activation(r0[:, 6:7], r0[:, 4:5], AF.Sigmoid, bias=0.0, scale=-1.0),
                 reads=[("s", "r4")], writes=[("s", "r6")])
            p.op("dve", lambda v: v.tensor_scalar(r0[:, 7:8], r0[:, 5:6], 0.5, None, ALU.mult),
                 reads=[("s", "r5")], writes=[("s", "r7")])
            p.op("dve", lambda v: v.tensor_scalar(r0[:, 8:9], r0[:, 6:7], 0.5, None, ALU.mult),
                 reads=[("s", "r6")], writes=[("s", "r8")])
            p.op("dve", lambda v: v.tensor_tensor(e1[:, :], e1[:, :], r0[:, 7:8].broadcast_to([128, 8]), ALU.mult),
                 reads=[("s", "e1"), ("s", "r7")], writes=[("s", "e1")])
            p.op("dve", lambda v: v.tensor_tensor(e2[:, :], e2[:, :], r0[:, 8:9].broadcast_to([128, 8]), ALU.mult),
                 reads=[("s", "e2"), ("s", "r8")], writes=[("s", "e2")])
            p.op("dve", lambda v, wsl=wsl: v.tensor_tensor(wsl, e1[:, :], e2[:, :], ALU.add),
                 reads=[("s", "e1"), ("s", "e2")], writes=[("s", "wtm", tc)])
        else:
            p.op("dve", lambda v: v.reduce_max(r0[:, 0:1], gS[:, :], axis=AX.X),
                 reads=[("s", "gS")], writes=[("s", "r0")], fence=True)
            p.op("dve", lambda v: v.tensor_scalar(r0[:, 1:2], r0[:, 0:1], -1.0, None, ALU.mult),
                 reads=[("s", "r0")], writes=[("s", "r1")])
            p.op("act", lambda s: s.activation(e1[:, :], gS[:, :], AF.Exp, bias=r0[:, 1:2], scale=1.0),
                 reads=[("s", "gS"), ("s", "r1")], writes=[("s", "e1")])
            p.op("dve", lambda v: v.reduce_sum(r0[:, 2:3], e1[:, :], axis=AX.X),
                 reads=[("s", "e1")], writes=[("s", "r2")], fence=True)
            p.op("dve", lambda v: v.reciprocal(r0[:, 3:4], r0[:, 2:3]),
                 reads=[("s", "r2")], writes=[("s", "r3")], fence=True)
            p.op("dve", lambda v: v.tensor_tensor(pb[:, :], e1[:, :], r0[:, 3:4].broadcast_to([128, 8]), ALU.mult),
                 reads=[("s", "e1"), ("s", "r3")], writes=[("s", "pb")])
            p.op("dve", lambda v: v.reduce_max(r0[:, 4:5], pb[:, 1:8], axis=AX.X),
                 reads=[("s", "pb")], writes=[("s", "r4")], fence=True)
            p.op("dve", lambda v: v.tensor_scalar(r0[:, 5:6], r0[:, 4:5], -1.0, None, ALU.mult),
                 reads=[("s", "r4")], writes=[("s", "r5")])
            p.op("act", lambda s: s.activation(e2[:, 1:8], pb[:, 1:8], AF.Identity, bias=r0[:, 5:6], scale=1.0),
                 reads=[("s", "pb"), ("s", "r5")], writes=[("s", "e2")])
            p.op("dve", lambda v: v.tensor_scalar(e2[:, 1:8], e2[:, 1:8], 0.0, None, ALU.is_ge),
                 reads=[("s", "e2")], writes=[("s", "e2")])
            p.op("dve", lambda v: v.tensor_tensor(r0[:, 6:7], pb[:, 0:1], r0[:, 4:5], ALU.add),
                 reads=[("s", "pb"), ("s", "r4")], writes=[("s", "r6")])
            p.op("dve", lambda v: v.tensor_scalar(r0[:, 6:7], r0[:, 6:7], 1e-9, None, ALU.max),
                 reads=[("s", "r6")], writes=[("s", "r6")])
            p.op("dve", lambda v: v.reciprocal(r0[:, 7:8], r0[:, 6:7]),
                 reads=[("s", "r6")], writes=[("s", "r7")], fence=True)
            p.op("dve", lambda v: v.tensor_scalar(r0[:, 8:9], r0[:, 7:8], 0.5, None, ALU.mult),
                 reads=[("s", "r7")], writes=[("s", "r8")])
            p.op("dve", lambda v: v.tensor_tensor(r0[:, 9:10], pb[:, 0:1], r0[:, 8:9], ALU.mult),
                 reads=[("s", "pb"), ("s", "r8")], writes=[("s", "r9")])
            p.op("dve", lambda v, wsl=wsl: v.tensor_copy(wsl[:, 0:1], r0[:, 9:10]),
                 reads=[("s", "r9")], writes=[("s", "wtm", tc)])
            p.op("dve", lambda v: v.tensor_tensor(r0[:, 10:11], r0[:, 4:5], r0[:, 8:9], ALU.mult),
                 reads=[("s", "r4"), ("s", "r8")], writes=[("s", "r10")])
            p.op("dve", lambda v, wsl=wsl: v.tensor_tensor(wsl[:, 1:8], e2[:, 1:8], r0[:, 10:11].broadcast_to([128, 7]), ALU.mult),
                 reads=[("s", "e2"), ("s", "r10")], writes=[("s", "wtm", tc)])
        p.op("pe", lambda t, tc=tc, wsl=wsl: t.transpose(
            c.ps[7][0:8, (tc % 4) * 128:(tc % 4 + 1) * 128], wsl, c.I128[:, :]),
            reads=[("s", "wtm", tc), ("s", "I128")], writes=[("ps", 7)])
        if tc % 4 == 3:
            grp = tc // 4
            wfm = scf(c, 5, parts=8)
            p.op("dve", lambda v, wfm=wfm: v.tensor_copy(wfm.v, c.ps[7][0:8, :]),
                 reads=[("ps", 7)], writes=wfm.k)
            p.op("sp", lambda s, grp=grp, wfm=wfm: s.dma_start(
                out=c.dwscr[0:8, grp * 512:(grp + 1) * 512], in_=wfm.v),
                reads=wfm.k, writes=[("d", "wscr", grp)], dma="qsx")

    for half in range(2):
        for e in range(nexp + (1 if shared else 0)):
            is_shared = shared and e == nexp
            if not is_shared:
                for j2 in range(2):
                    j = half * 2 + j2
                    wr = c.wrow[j2]
                    p.op("sp", lambda s, e=e, j=j, wr=wr: s.dma_start(
                        out=wr[:, :], in_=c.dwscr[e:e + 1, j * 512:(j + 1) * 512]),
                        reads=[("d", "wscr", j)], writes=[("s", "wrow", j2)], dma="qsx")
                    p.op("pe", lambda t, wr=wr: t.matmul(
                        c.ps[6][:, :], c.ones_row[:, :], wr[0:1, :], start=True, stop=True),
                        reads=[("s", "wrow", j2), ("s", "ones_row")], writes=[("ps", 6)])
                    p.op("dve", lambda v, j2=j2: v.tensor_copy(c.wb_sb[:, j2 * 512:(j2 + 1) * 512], c.ps[6][:, :]),
                         reads=[("ps", 6)], writes=[("s", "wbsb", j2)])
            for grp in range(NGRP):
                for hc16 in range(16):
                    hc = grp * 16 + hc16
                    bA = c.wbuf[(hc16 % 2) * 2]
                    bB = c.wbuf[(hc16 % 2) * 2 + 1]
                    p.op("pool", lambda g, e=e, hc=hc, bA=bA: g.dma_start(
                        out=bA[:, :].rearrange("p (k m) -> p k m", k=8), in_=c.din[FA][e, hc]),
                        writes=wbk((hc16 % 2) * 2), dma=f"qw{(hc16 % 2) * 2}")
                    p.op("pool", lambda g, e=e, hc=hc, bB=bB: g.dma_start(
                        out=bB[:, :].rearrange("p (k m) -> p k m", k=8), in_=c.din[FB][e, hc]),
                        writes=wbk((hc16 % 2) * 2 + 1), dma=f"qw{(hc16 % 2) * 2 + 1}")
                    for j2 in range(2):
                        j = half * 2 + j2
                        bkA = (hc16 * 2 + j2) % 2
                        bkB = 2 + (hc16 * 2 + j2) % 2

                        def hmm(t, buf, bank, j):
                            ins = None
                            for kc in range(DCH):
                                ins = t.matmul(c.ps[bank][:, :], buf[:, kc * 128:(kc + 1) * 128],
                                               c.xn[kc][:, j * 512:(j + 1) * 512],
                                               start=(kc == 0), stop=(kc == DCH - 1))
                            return ins
                        p.op("pe", lambda t, bA=bA, bkA=bkA, j=j: hmm(t, bA, bkA, j),
                             reads=wbk((hc16 % 2) * 2) + [("xn", kc, j) for kc in range(DCH)],
                             writes=[("ps", bkA)])
                        p.op("pe", lambda t, bB=bB, bkB=bkB, j=j: hmm(t, bB, bkB, j),
                             reads=wbk((hc16 % 2) * 2 + 1) + [("xn", kc, j) for kc in range(DCH)],
                             writes=[("ps", bkB)])
                        sA = scr(c, 44)
                        sB = scr(c, 45)
                        p.op("act", lambda s, bkA=bkA, sA=sA: s.activation(
                            sA.v, c.ps[bkA][:, :], AF.Silu, bias=0.0, scale=1.0),
                            reads=[("ps", bkA)], writes=sA.k)
                        if is_shared:
                            p.op("dve", lambda v, bkB=bkB, sB=sB: v.tensor_scalar(
                                sB.v, c.ps[bkB][:, :], 0.5, None, ALU.mult),
                                reads=[("ps", bkB)], writes=sB.k)
                        else:
                            p.op("dve", lambda v, bkB=bkB, sB=sB, j2=j2: v.tensor_tensor(
                                sB.v, c.ps[bkB][:, :], c.wb_sb[:, j2 * 512:(j2 + 1) * 512], ALU.mult),
                                reads=[("ps", bkB), ("s", "wbsb", j2)], writes=sB.k)
                        hb = scr(c, hc16 * 2 + j2)
                        p.op("dve", lambda v, hb=hb, sA=sA, sB=sB: v.tensor_tensor(hb.v, sA.v, sB.v, ALU.mult),
                             reads=sA.k + sB.k, writes=hb.k)
                for dc in range(DCH):
                    wC = c.w2buf[dc % 2]
                    p.op("pool", lambda g, e=e, dc=dc, grp=grp, wC=wC: g.dma_start(
                        out=wC[:, :].rearrange("p (k m) -> p k m", k=16),
                        in_=c.din[FC][e, grp, dc]),
                        writes=[("s", "w2buf", dc % 2)], dma=f"qc{dc % 2}")
                    for j2 in range(2):
                        j = half * 2 + j2
                        bank = 4 + (dc * 2 + j2) % 2

                        def ymm(t, wC=wC, bank=bank, j2=j2):
                            ins = None
                            for k16 in range(16):
                                ins = t.matmul(c.ps[bank][:, :], wC[:, k16 * 128:(k16 + 1) * 128],
                                               c.SCR[:, (k16 * 2 + j2) * 512:(k16 * 2 + j2 + 1) * 512],
                                               start=(k16 == 0), stop=(k16 == 15))
                            return ins
                        p.op("pe", ymm,
                             reads=[("s", "w2buf", dc % 2)] + [("scr", k16 * 2 + j2) for k16 in range(16)],
                             writes=[("ps", bank)])
                        xv = c.x[dc][:, j * 512:(j + 1) * 512]
                        p.op("dve", lambda v, xv=xv, bank=bank: v.tensor_tensor(xv, xv, c.ps[bank][:, :], ALU.add),
                             reads=[("ps", bank), ("x", dc, j)], writes=[("x", dc, j)])



def proj_ktm_vtm(p, c, sub):
    for which, W, brow, bkey in (("ktm", "Wk_r", c.bk_bf, "bk_bf"),
                                 ("vtm", "Wv_r", c.bv_bf, "bv_bf")):
        gbase = 16 if which == "ktm" else 24
        for kh in range(2):
            for kc in range(DCH):
                u = c.wbuf[kc // 2][:, (kc % 2) * 512:(kc % 2 + 1) * 512]
                p.op("pool", lambda g, W=W, kc=kc, kh=kh, u=u: g.dma_start(
                    out=u, in_=c.din[W][kc, :, kh * 512:(kh + 1) * 512]),
                    writes=wbk(kc // 2, kc % 2), dma=f"qw{kc // 2}")
            for tc in range(4):
                bank = tc % 2
                tok = sub * 512 + tc * 128

                def kmm(t, kh=kh, tok=tok, bank=bank, brow=brow):
                    ins = None
                    for kc in range(DCH):
                        u = c.wbuf[kc // 2][:, (kc % 2) * 512:(kc % 2 + 1) * 512]
                        ins = t.matmul(c.ps[bank][:, :], c.xn[kc][:, tok:tok + 128], u,
                                       start=(kc == 0), stop=False)
                    ins = t.matmul(c.ps[bank][:, :], c.ones_row_bf[0:1, tok % 128:tok % 128 + 128],
                                   brow[0:1, kh * 512:(kh + 1) * 512], start=False, stop=True)
                    return ins
                p.op("pe", kmm,
                     reads=[k for kc in range(DCH) for k in wbk(kc // 2, kc % 2)]
                     + [("xn", kc, sub) for kc in range(DCH)]
                     + [("s", bkey), ("s", "ones_row_bf")],
                     writes=[("ps", bank)])
                dst = scr(c, gbase + tc * 2 + kh)
                p.op("dve", lambda v, dst=dst, bank=bank: v.tensor_copy(dst.v, c.ps[bank][:, :]),
                     reads=[("ps", bank)], writes=dst.k)


def emit_attn0(p, c):
    for sub in range(4):
        proj_ktm_vtm(p, c, sub)
        for tc in range(4):
            gchunk = sub * 4 + tc
            for h in range(16):
                bank = 6 + h // 8
                ktm = scr(c, 16 + tc * 2, 2)
                vtm = scr(c, 24 + tc * 2, 2)
                lv = ktm.v[:, h * 64:(h + 1) * 64]
                rv = vtm.v[:, h * 64:(h + 1) * 64]
                ov = c.ps[bank][0:64, (h % 8) * 64:(h % 8 + 1) * 64]
                st = gchunk == 0 and (h % 8) == 0
                p.op("pe", lambda t, lv=lv, rv=rv, ov=ov, st=st: t.matmul(ov, lv, rv, start=st, stop=True),
                     reads=ktm.k + vtm.k, writes=[("ps", bank)])
    SL = scf(c, 0, 2, parts=64)
    for i2 in range(2):
        p.op("dve", lambda v, i2=i2: v.tensor_copy(SL.v[:, i2 * 512:(i2 + 1) * 512], c.ps[6 + i2][0:64, :]),
             reads=[("ps", 6 + i2)], writes=SL.k)
    p.op("sp", lambda s: s.dma_start(out=c.cinL0.ap(), in_=SL.v),
         reads=SL.k, writes=[("d", "cin0")], dma="qsx")
    p.op("pool", lambda g: g.collective_compute(
        "AllGather", ALU.bypass, replica_groups=[list(range(NCORE))],
        ins=[c.cinL0.ap().opt()], outs=[c.coutL0.ap().opt()]),
        reads=[("d", "cin0")], writes=[("d", "cout0")])
    for rg in range(4):
        G = scf(c, 2, 2)
        for r in range(8):
            p.op("sp", lambda s, rg=rg, G=G, r=r: s.dma_start(
                out=G.v[r * 16:(r + 1) * 16, :],
                in_=c.coutL0.ap()[r * 64 + rg * 16:r * 64 + rg * 16 + 16, :]),
                reads=[("d", "cout0")], writes=G.k, dma="qsx")
        for i2 in range(2):
            p.op("pe", lambda t, G=G, i2=i2: t.matmul(
                c.ps[4][0:16, :], c.selm[:, :], G.v[:, i2 * 512:(i2 + 1) * 512],
                start=True, stop=True),
                reads=G.k + [("s", "selm")], writes=[("ps", 4)])
            stv = scf(c, 6, parts=16)
            p.op("dve", lambda v, stv=stv: v.tensor_copy(stv.v, c.ps[4][0:16, :]),
                 reads=[("ps", 4)], writes=stv.k)
            p.op("sp", lambda s, rg=rg, i2=i2, stv=stv: s.dma_start(
                out=c.dSp[rg * 16:(rg + 1) * 16, i2 * 512:(i2 + 1) * 512], in_=stv.v),
                reads=stv.k, writes=[("d", "dSp", rg, i2)], dma="qsx")
    Sp = scf(c, 0, 2, parts=64)
    p.op("sp", lambda s: s.dma_start(out=Sp.v.rearrange("p (a f) -> p a f", a=2),
                                     in_=c.dSp.rearrange("p (a f) -> p a f", a=2)),
         reads=[("d", "dSp", rg, i2) for rg in range(4) for i2 in range(2)],
         writes=Sp.k, dma="qsx")
    Ssb = scr(c, 48)

    def sview(h):
        pb = (h % 2) * 64
        return Ssb.v[pb:pb + 64, (h // 2) * 64:(h // 2) * 64 + 64]
    for par in range(2):
        p.op("dve", lambda v, par=par: v.tensor_copy(
            Ssb.v[par * 64:(par + 1) * 64, :].rearrange("p (a b) -> p a b", b=64),
            Sp.v.rearrange("p (a b) -> p a b", b=64)[:, par::2, :]),
            reads=Sp.k, writes=Ssb.k)

    for sub in range(4):
        tok = slice(sub * 512, (sub + 1) * 512)
        for which, W, bias, bkey, gbase in (("q", "Wq_l", c.biases0["bq_pp"], "bq_pp", 0),
                                            ("k", "Wk_l", c.biases0["bk_pp"], "bk_pp", 8)):
            for qc in range(DCH):
                wb = c.wbuf[qc % 2]
                p.op("pool", lambda g, W=W, qc=qc, wb=wb: g.dma_start(
                    out=wb[:, :].rearrange("p (k m) -> p k m", k=8), in_=c.din[W][qc]),
                    writes=wbk(qc % 2), dma=f"qw{qc % 2}")
                bank = qc % 2

                def pmm(t, wb=wb, bank=bank, tok=tok):
                    ins = None
                    for kc in range(DCH):
                        ins = t.matmul(c.ps[bank][:, :], wb[:, kc * 128:(kc + 1) * 128],
                                       c.xn[kc][:, tok], start=(kc == 0), stop=(kc == DCH - 1))
                    return ins
                p.op("pe", pmm,
                     reads=wbk(qc % 2) + [("xn", kc, sub) for kc in range(DCH)],
                     writes=[("ps", bank)])
                dst = scr(c, gbase + qc)
                p.op("act", lambda s, dst=dst, bank=bank, bias=bias, qc=qc: s.activation(
                    dst.v, c.ps[bank][:, :], AF.Identity, bias=bias[:, qc:qc + 1], scale=1.0),
                    reads=[("ps", bank), ("s", bkey)], writes=dst.k)
        proj_ktm_vtm(p, c, sub)
        def nmm(t, tok=tok):
            ins = None
            for kc in range(DCH):
                ins = t.matmul(c.ps[4][0:16, :], c.Wn[:, kc, :], c.xn[kc][:, tok],
                               start=(kc == 0), stop=(kc == DCH - 1))
            return ins
        p.op("pe", nmm, reads=[("s", "Wn")] + [("xn", kc, sub) for kc in range(DCH)],
             writes=[("ps", 4)])
        nsc = scf(c, 6, parts=16)
        p.op("act", lambda s, nsc=nsc: s.activation(nsc.v, c.ps[4][0:16, :], AF.Exp,
                                                    bias=c.negbn[:, :], scale=-1.0),
             reads=[("ps", 4), ("s", "negbn")], writes=nsc.k)
        p.op("sp", lambda s, nsc=nsc: s.dma_start(out=c.dnscr, in_=nsc.v),
             reads=nsc.k, writes=[("d", "nscr")], dma="qsx")
        for tc in range(4):
            gchunk = sub * 4 + tc
            ktm = scr(c, 16 + tc * 2, 2)
            vtm = scr(c, 24 + tc * 2, 2)
            for h in range(16):
                qb = scr(c, 0 + h // 2)
                kb = scr(c, 8 + h // 2)
                pb = (h % 2) * 64
                sbank = h % 2
                att = scr(c, 46 + h % 2)
                p.op("pe", lambda t, kb=kb, qb=qb, pb=pb, tc=tc, sbank=sbank: t.matmul(
                    c.ps[sbank][:, 0:128], kb.v[pb:pb + 64, tc * 128:(tc + 1) * 128],
                    qb.v[pb:pb + 64, tc * 128:(tc + 1) * 128], start=True, stop=True),
                    reads=kb.k + qb.k, writes=[("ps", sbank)])
                p.op("dve", lambda v, att=att, sbank=sbank: v.tensor_tensor(
                    att.v[:, 0:128], c.ps[sbank][:, 0:128], c.triuT[:, :], ALU.mult),
                    reads=[("ps", sbank), ("s", "triuT")], writes=att.k)
                cbank = 2 + h % 2
                p.op("pe", lambda t, vtm=vtm, att=att, h=h, cbank=cbank: t.matmul(
                    c.ps[cbank][0:64, 0:128], vtm.v[:, h * 64:(h + 1) * 64], att.v[:, 0:128],
                    start=True, stop=False),
                    reads=vtm.k + att.k, writes=[("ps", cbank)])
                p.op("pe", lambda t, qb=qb, pb=pb, tc=tc, h=h, cbank=cbank: t.matmul(
                    c.ps[cbank][0:64, 0:128], sview(h),
                    qb.v[pb:pb + 64, tc * 128:(tc + 1) * 128], start=False, stop=True),
                    reads=Ssb.k + qb.k, writes=[("ps", cbank)])
                ctx = scr(c, 32 + h // 2)
                p.op("dve", lambda v, ctx=ctx, pb=pb, tc=tc, cbank=cbank: v.tensor_copy(
                    ctx.v[pb:pb + 64, tc * 128:(tc + 1) * 128], c.ps[cbank][0:64, 0:128]),
                    reads=[("ps", cbank)], writes=ctx.k)
            for h in range(16):
                bank = 6 + h // 8
                lv = ktm.v[:, h * 64:(h + 1) * 64]
                rv = vtm.v[:, h * 64:(h + 1) * 64]
                ov = c.ps[bank][0:64, (h % 8) * 64:(h % 8 + 1) * 64]
                st = gchunk == 0 and (h % 8) == 0
                p.op("pe", lambda t, lv=lv, rv=rv, ov=ov, st=st: t.matmul(ov, lv, rv, start=st, stop=True),
                     reads=ktm.k + vtm.k, writes=[("ps", bank)])
                p.op("dve", lambda v, h=h, bank=bank: v.tensor_tensor(
                    sview(h), c.ps[bank][0:64, (h % 8) * 64:(h % 8 + 1) * 64],
                    Sp.v[:, h * 64:(h + 1) * 64], ALU.add),
                    reads=[("ps", bank)] + Sp.k, writes=Ssb.k)
        for h in range(16):
            nr = c.nrow[0]
            p.op("sp", lambda s, h=h, nr=nr: s.dma_start(out=nr[:, :], in_=c.dnscr[h:h + 1, :]),
                 reads=[("d", "nscr")], writes=[("s", "nrow")], dma="qsx")
            p.op("pe", lambda t, nr=nr: t.matmul(c.ps[4][0:64, :], c.ones_row[0:1, 0:64], nr[0:1, :],
                                                 start=True, stop=True),
                 reads=[("s", "nrow"), ("s", "ones_row")], writes=[("ps", 4)])
            ctx = scr(c, 32 + h // 2)
            pb = (h % 2) * 64
            p.op("dve", lambda v, ctx=ctx, pb=pb: v.tensor_tensor(
                ctx.v[pb:pb + 64, :], ctx.v[pb:pb + 64, :], c.ps[4][0:64, :], ALU.mult),
                reads=ctx.k + [("ps", 4)], writes=ctx.k)
        for dc in range(DCH):
            wb = c.wbuf[2 + dc % 2]
            p.op("pool", lambda g, dc=dc, wb=wb: g.dma_start(
                out=wb[:, :].rearrange("p (k m) -> p k m", k=8), in_=c.din["Wo_l"][dc]),
                writes=wbk(2 + dc % 2), dma=f"qw{2 + dc % 2}")

            def omm(t, wb=wb):
                ins = None
                for cc in range(DCH):
                    ctx = scr(c, 32 + cc)
                    ins = t.matmul(c.ps[5][:, :], wb[:, cc * 128:(cc + 1) * 128], ctx.v,
                                   start=(cc == 0), stop=(cc == DCH - 1))
                return ins
            p.op("pe", omm,
                 reads=wbk(2 + dc % 2) + [("scr", 32 + cc) for cc in range(DCH)],
                 writes=[("ps", 5)])
            otmp = scf(c, 5)
            p.op("act", lambda s, otmp=otmp, dc=dc: s.activation(
                otmp.v, c.ps[5][:, :], AF.Identity, bias=c.biases0["bo_pp"][:, dc:dc + 1], scale=1.0),
                reads=[("ps", 5), ("s", "bo_pp")], writes=otmp.k)
            xv = c.x[dc][:, tok]
            p.op("dve", lambda v, xv=xv, otmp=otmp: v.tensor_tensor(xv, xv, otmp.v, ALU.add),
                 reads=otmp.k + [("x", dc, sub)], writes=[("x", dc, sub)])



SER_SEGS = set()


def emit_attn1(p, c):
    def seg(name):
        p.scope_ser = name in SER_SEGS
    uBx = Buf(c.SCR[0:16, 0:2112].rearrange("p (a b) -> p a b", b=33),
              [("scr", g) for g in range(5)])
    hsT = Buf(c.SCR[0:16, 2560:4608], [("scr", g) for g in range(5, 9)])
    PowsBf = Buf(c.SCR[0:16, 4608:5120], [("scr", 9)])
    WCT = Buf(c.SCR[0:16, 5632:6656], [("scr", 11), ("scr", 12)])
    Pows = Buf(c.SCF[0:16, 0:512], [("scf", 0)])
    echn = Buf(c.SCF[0:16, 1024:1088], [("scf", 2)])
    hmid = Buf(c.SCF[0:16, 1536:1537], [("scf", 3)])
    seg("uproj")
    for sub in range(4):
        tok = slice(sub * 512, (sub + 1) * 512)
        for ec in range(DCH):
            wbi = c.wbuf[0 + ec % 2]
            wbg = c.wbuf[2 + ec % 2]
            p.op("pool", lambda g, ec=ec, wbi=wbi: g.dma_start(
                out=wbi[:, :].rearrange("p (k m) -> p k m", k=8), in_=c.din["Wi_l"][ec]),
                writes=wbk(ec % 2), dma=f"qw{ec % 2}")
            p.op("pool", lambda g, ec=ec, wbg=wbg: g.dma_start(
                out=wbg[:, :].rearrange("p (k m) -> p k m", k=8), in_=c.din["Wg_l"][ec]),
                writes=wbk(2 + ec % 2), dma=f"qw{2 + ec % 2}")

            def prmm(t, wb, bank, tok=tok):
                ins = None
                for kc in range(DCH):
                    ins = t.matmul(c.ps[bank][:, :], wb[:, kc * 128:(kc + 1) * 128],
                                   c.xn[kc][:, tok], start=(kc == 0), stop=(kc == DCH - 1))
                return ins
            bank = ec % 2
            p.op("pe", lambda t, wbi=wbi, bank=bank: prmm(t, wbi, bank),
                 reads=wbk(ec % 2) + [("xn", kc, sub) for kc in range(DCH)],
                 writes=[("ps", bank)])
            p.op("pe", lambda t, wbg=wbg, bank=bank: prmm(t, wbg, 2 + bank),
                 reads=wbk(2 + ec % 2) + [("xn", kc, sub) for kc in range(DCH)],
                 writes=[("ps", 2 + bank)])
            xi = scf(c, 5)
            gt = scf(c, 2)
            p.op("act", lambda s, xi=xi, bank=bank, ec=ec: s.activation(
                xi.v, c.ps[bank][:, :], AF.Identity, bias=c.biases1["bi_pp"][:, ec:ec + 1], scale=1.0),
                reads=[("ps", bank), ("s", "bi_pp")], writes=xi.k)
            p.op("act", lambda s, gt=gt, bank=bank, ec=ec: s.activation(
                gt.v, c.ps[2 + bank][:, :], AF.Sigmoid, bias=c.biases1["bg_pp"][:, ec:ec + 1], scale=1.0),
                reads=[("ps", 2 + bank), ("s", "bg_pp")], writes=gt.k)
            ub = scr(c, 16 + ec)
            p.op("dve", lambda v, ub=ub, xi=xi, gt=gt: v.tensor_tensor(ub.v, xi.v, gt.v, ALU.mult),
                 reads=xi.k + gt.k, writes=ub.k)

        def ubmm(t):
            ins = None
            for kc in range(DCH):
                ub = scr(c, 16 + kc)
                ins = t.matmul(c.ps[4][0:16, :], c.Bm[:, kc, :], ub.v,
                               start=(kc == 0), stop=(kc == DCH - 1))
            return ins
        p.op("pe", ubmm, reads=[("s", "Bm")] + [("scr", 16 + kc) for kc in range(DCH)],
             writes=[("ps", 4)])
        p.op("dve", lambda v, sub=sub: v.tensor_copy(
            uBx.v[:, sub * 16:(sub + 1) * 16, 1:33],
            c.ps[4][0:16, :].rearrange("p (a b) -> p a b", b=32)),
            reads=[("ps", 4)], writes=uBx.k)
    seg("powers")
    p.op("dve", lambda v: v.memset(uBx.v[:, :, 0:1], 0.0), writes=uBx.k)
    p.op("sp", lambda s: s.dma_start(out=Pows.v[:, 0:16], in_=c.din["AT"]),
         writes=Pows.k, dma="qsx")
    p.op("dve", lambda v: v.tensor_copy(PowsBf.v[:, 0:16], Pows.v[:, 0:16]),
         reads=Pows.k, writes=PowsBf.k)
    for l in range(2, 33):
        p.op("pe", lambda t, l=l: t.matmul(c.ps[5][0:16, 0:16], c.A_sb[:, :],
                                           Pows.v[:, (l - 2) * 16:(l - 1) * 16],
                                           start=True, stop=True),
             reads=Pows.k + [("s", "A_sb")], writes=[("ps", 5)])
        p.op("dve", lambda v, l=l: v.tensor_copy(Pows.v[:, (l - 1) * 16:l * 16], c.ps[5][0:16, 0:16]),
             reads=[("ps", 5)], writes=Pows.k)
        p.op("dve", lambda v, l=l: v.tensor_copy(PowsBf.v[:, (l - 1) * 16:l * 16], c.ps[5][0:16, 0:16]),
             reads=[("ps", 5)], writes=PowsBf.k)
    seg("lag")
    CG = [(0, 15), (15, 15), (30, 15), (45, 15), (60, 4)]
    for l in range(33):
        for cgi, (c0, ncg) in enumerate(CG):
            lhs = c.I16bf[:, :] if l == 0 else PowsBf.v[:, (l - 1) * 16:l * 16]
            lk = [("s", "I16bf")] if l == 0 else PowsBf.k
            ov = c.ps[cgi][0:16, 0:ncg * 33].rearrange("p (a b) -> p a b", b=33)[:, :, l:33]
            iv = uBx.v[:, c0:c0 + ncg, 0:33 - l]
            p.op("pe", lambda t, lhs=lhs, ov=ov, iv=iv, l=l: t.matmul(
                ov, lhs, iv, start=(l == 0), stop=False),
                reads=lk + uBx.k, writes=[("ps", cgi)])
    seg("scan")
    for cgi, (c0, ncg) in enumerate(CG):
        p.op("dve", lambda v, cgi=cgi, c0=c0, ncg=ncg: v.tensor_copy(
            echn.v[:, c0:c0 + ncg],
            c.ps[cgi][0:16, 0:ncg * 33].rearrange("p (a b) -> p a b", b=33)[:, :, 32]),
            reads=[("ps", cgi)], writes=echn.k)
    p.op("dve", lambda v: v.memset(c.svec[0][:, :], 0.0), writes=[("s", "sv0")])
    for cc in range(64):
        cur, nxt = c.svec[cc % 2], c.svec[(cc + 1) % 2]
        ck, nk = [("s", f"sv{cc % 2}")], [("s", f"sv{(cc + 1) % 2}")]

        def smm(t, cur=cur, cc=cc):
            t.matmul(c.ps[5][0:16, 0:1], Pows.v[:, 31 * 16:32 * 16], cur[:, :], start=True, stop=False)
            return t.matmul(c.ps[5][0:16, 0:1], c.I16f[:, :], echn.v[:, cc:cc + 1], start=False, stop=True)
        p.op("pe", smm, reads=Pows.k + echn.k + ck + [("s", "I16f")], writes=[("ps", 5)])
        p.op("dve", lambda v, nxt=nxt: v.tensor_copy(nxt[:, :], c.ps[5][0:16, 0:1]),
             reads=[("ps", 5)], writes=nk)
    p.op("sp", lambda s: s.dma_start(out=c.cinL1.ap(), in_=c.svec[0][:, :]),
         reads=[("s", "sv0")], writes=[("d", "cin1")], dma="qsx")
    p.op("pool", lambda g: g.collective_compute(
        "AllGather", ALU.bypass, replica_groups=[list(range(NCORE))],
        ins=[c.cinL1.ap().opt()], outs=[c.coutL1.ap().opt()]),
        reads=[("d", "cin1")], writes=[("d", "cout1")])
    p.op("sp", lambda s: s.dma_start(out=c.Gl1[:, :], in_=c.coutL1.ap()),
         reads=[("d", "cout1")], writes=[("s", "Gl1")], dma="qsx")
    p.op("pe", lambda t: t.matmul(c.ps[5][0:16, 0:1], c.selm[:, :], c.Gl1[:, 0:1],
                                  start=True, stop=True),
         reads=[("s", "Gl1"), ("s", "selm")], writes=[("ps", 5)])
    p.op("dve", lambda v: v.tensor_copy(hmid.v, c.ps[5][0:16, 0:1]),
         reads=[("ps", 5)], writes=hmid.k)
    p.op("dve", lambda v: v.tensor_copy(c.svec[0][:, :], hmid.v),
         reads=hmid.k, writes=[("s", "sv0")])
    for cc in range(64):
        cur, nxt = c.svec[cc % 2], c.svec[(cc + 1) % 2]
        ck, nk = [("s", f"sv{cc % 2}")], [("s", f"sv{(cc + 1) % 2}")]
        p.op("dve", lambda v, cur=cur, cc=cc: v.tensor_copy(uBx.v[:, cc, 0:1], cur[:, :]),
             reads=ck, writes=uBx.k)

        def smm2(t, cur=cur, cc=cc):
            t.matmul(c.ps[5][0:16, 0:1], Pows.v[:, 31 * 16:32 * 16], cur[:, :], start=True, stop=False)
            return t.matmul(c.ps[5][0:16, 0:1], c.I16f[:, :], echn.v[:, cc:cc + 1], start=False, stop=True)
        p.op("pe", smm2, reads=Pows.k + echn.k + ck + [("s", "I16f")], writes=[("ps", 5)])
        p.op("dve", lambda v, nxt=nxt: v.tensor_copy(nxt[:, :], c.ps[5][0:16, 0:1]),
             reads=[("ps", 5)], writes=nk)
    seg("impulse")
    for l in range(1, 33):
        for cgi, (c0, ncg) in enumerate(CG):
            ov = c.ps[cgi][0:16, 0:ncg * 33].rearrange("p (a b) -> p a b", b=33)[:, :, l:l + 1]
            iv = uBx.v[:, c0:c0 + ncg, 0:1]
            p.op("pe", lambda t, l=l, ov=ov, iv=iv: t.matmul(
                ov, PowsBf.v[:, (l - 1) * 16:l * 16], iv, start=False, stop=(l == 32)),
                reads=PowsBf.k + uBx.k, writes=[("ps", cgi)])
    for cgi, (c0, ncg) in enumerate(CG):
        p.op("dve", lambda v, cgi=cgi, c0=c0, ncg=ncg: v.tensor_copy(
            hsT.v.rearrange("p (a b) -> p a b", b=32)[:, c0:c0 + ncg, :],
            c.ps[cgi][0:16, 0:ncg * 33].rearrange("p (a b) -> p a b", b=33)[:, :, 1:33]),
            reads=[("ps", cgi)], writes=hsT.k)
    seg("out")
    for kc in range(DCH):
        wb = c.wbuf[kc % 2]
        p.op("pool", lambda g, kc=kc, wb=wb: g.dma_start(out=wb[:, :], in_=c.din["WoT_r"][kc]),
             writes=wbk(kc % 2), dma=f"qw{kc % 2}")
        for i2 in range(2):
            p.op("pe", lambda t, kc=kc, wb=wb, i2=i2: t.matmul(
                c.ps[i2][0:16, :], c.Cm[:, kc, :], wb[:, i2 * 512:(i2 + 1) * 512],
                start=(kc == 0), stop=(kc == DCH - 1)),
                reads=wbk(kc % 2) + [("s", "Cm")], writes=[("ps", i2)])
    for i2 in range(2):
        p.op("dve", lambda v, i2=i2: v.tensor_copy(WCT.v[:, i2 * 512:(i2 + 1) * 512], c.ps[i2][0:16, :]),
             reads=[("ps", i2)], writes=WCT.k)
    for ec in range(DCH):
        for j in range(4):
            bank = 2 + (ec * 4 + j) % 2
            p.op("pe", lambda t, ec=ec, j=j, bank=bank: t.matmul(
                c.ps[bank][:, :], WCT.v[:, ec * 128:(ec + 1) * 128],
                hsT.v[:, j * 512:(j + 1) * 512], start=True, stop=True),
                reads=WCT.k + hsT.k, writes=[("ps", bank)])
            ot = scf(c, 5)
            p.op("act", lambda s, ot=ot, bank=bank, ec=ec: s.activation(
                ot.v, c.ps[bank][:, :], AF.Identity, bias=c.biases1["bout_pp"][:, ec:ec + 1], scale=1.0),
                reads=[("ps", bank), ("s", "bout_pp")], writes=ot.k)
            xv = c.x[ec][:, j * 512:(j + 1) * 512]
            p.op("dve", lambda v, xv=xv, ot=ot: v.tensor_tensor(xv, xv, ot.v, ALU.add),
                 reads=ot.k + [("x", ec, j)], writes=[("x", ec, j)])
    p.scope_ser = False



DEBUG_DUMPS = []


def build(phases, host):
    p = Prog()
    c = alloc(p, phases, host)
    emit_load_common(p, c, phases)
    for ph in phases:
        emit_ln(p, c)
        if ph in ("ffn0", "ffn1"):
            emit_moe(p, c, ph)
        elif ph == "attn0":
            emit_attn0(p, c)
        elif ph == "attn1":
            emit_attn1(p, c)
    for name, getter, shape, keys in DEBUG_DUMPS:
        ap = getter(c)
        dt = BF16 if "bfloat" in str(ap.dtype) else F32
        dout = p.nc.dram_tensor("dbg_" + name, shape, dt, kind="ExternalOutput").ap()
        p.op("sp", lambda s, dout=dout, ap=ap: s.dma_start(out=dout, in_=ap),
             reads=keys, dma="qsx")
    emit_store(p, c)
    return p.finalize(), c.innames


def run_phases(host, PC, groups, xcur):
    built = {}
    for phases in groups:
        key = tuple(phases)
        if key not in built:
            built[key] = build(phases, host)
        nc, names = built[key]
        in_maps = []
        for cid in range(NCORE):
            m = {}
            for n in names:
                if n == "xin":
                    m[n] = xcur[cid]
                elif n == "selmat":
                    m[n] = PC[cid]["selmat"]
                else:
                    m[n] = host[n]
            in_maps.append(m)
        res = run_bass_kernel_spmd(nc, in_maps, core_ids=list(range(NCORE)))
        xcur = [np.ascontiguousarray(res.results[cid]["y"]) for cid in range(NCORE)]
    return xcur


def kernel(**inputs):
    host, PC = prep_host(inputs)
    groups = [ALL_PHASES] if FUSED else [(ph,) for ph in ALL_PHASES]
    xcur = [PC[cid]["xin"] for cid in range(NCORE)]
    xcur = run_phases(host, PC, groups, xcur)
    out = np.empty((4, 4096, 1024), np.float32)
    for cid in range(NCORE):
        b, hh = divmod(cid, 2)
        out[b, hh * T:(hh + 1) * T, :] = xcur[cid].T
    return out


# version 03
# speedup vs baseline: 1.0249x; 1.0249x over previous
import contextlib
import numpy as np
import ml_dtypes
import concourse.bass as bass
import concourse.mybir as mybir
from concourse.bass_utils import run_bass_kernel_spmd

F32 = mybir.dt.float32
BF16 = mybir.dt.bfloat16
AF = mybir.ActivationFunctionType
ALU = mybir.AluOpType
AX = mybir.AxisListType

NCORE = 8
T = 2048
D = 1024
DCH = 8
BIG_NEG = -1.0e30

ALL_PHASES = ("ffn0", "attn0", "ffn0", "ffn1", "attn1", "ffn1")
FUSED = True
SERIALIZE = False

ENG2SECTION = {"pe": "tensor", "act": "scalar", "dve": "vector",
               "pool": "gpsimd", "sp": "sync"}
DMA_ROLES = ("qi", "qw0", "qw1", "qw2", "qw3", "qc0", "qc1", "qsx")
QUEUES = tuple(f"{r}_{i}" for r in DMA_ROLES for i in range(2))


class Prog:
    def __init__(self):
        self.nc = bass.Bass("TRN2", target_bir_lowering=False)
        self.st = contextlib.ExitStack()
        self.ops = {e: [] for e in ENG2SECTION}
        self.count = {s: 0 for s in list(ENG2SECTION) + list(QUEUES)}
        self.writer = {}
        self.readers = {}
        self.marks = {}
        self.scope_ser = False
        self.role_rot = {}

    def sbuf(self, name, shape, dt):
        return self.st.enter_context(self.nc.sbuf_tensor(name, shape, dt))

    def psum(self, name, shape, dt=F32):
        return self.st.enter_context(self.nc.psum_tensor(name, shape, dt))

    def op(self, eng, emit, reads=(), writes=(), dma=None, sreads=(), fence=False):
        if dma is not None:
            i = self.role_rot.get(dma, 0)
            self.role_rot[dma] = i ^ 1
            dma = f"{dma}_{i}"
        stream = dma if dma else eng
        if dma is None and eng in ("dve", "act", "pool"):
            fence = True
        waits = {}
        if SERIALIZE or self.scope_ser:
            for s, cv in self.count.items():
                if s == eng or cv == 0:
                    continue
                w = self.marks.get((eng, s), 0)
                if cv > w:
                    waits[s] = cv
                    self.marks[(eng, s)] = cv

        def need(pstream, cnt, force=False):
            if cnt <= 0:
                return
            if pstream == eng and dma is None and not force:
                return
            if pstream in QUEUES:
                cnt = self.count[pstream]
            w = self.marks.get((eng, pstream), 0)
            if cnt > w:
                waits[pstream] = max(waits.get(pstream, 0), cnt)
                self.marks[(eng, pstream)] = cnt

        for k in reads:
            lw = self.writer.get(k)
            if lw:
                need(lw[0], lw[1], force=lw[2])
        for k in sreads:
            lw = self.writer.get(k)
            if lw:
                need(lw[0], lw[1], force=True)
        for k in writes:
            lw = self.writer.get(k)
            if lw:
                need(lw[0], lw[1], force=lw[2])
            for r in self.readers.get(k, ()):
                need(*r)
        if dma is not None and self.count[stream] > 0:
            w = self.marks.get((eng, stream), 0)
            if self.count[stream] > w:
                waits[stream] = max(waits.get(stream, 0), self.count[stream])
                self.marks[(eng, stream)] = self.count[stream]
        self.count[stream] += 16 if dma else 1
        cnt = self.count[stream]
        for k in list(reads) + list(sreads):
            self.readers.setdefault(k, []).append((stream, cnt))
        for k in writes:
            self.writer[k] = (stream, cnt, fence)
            self.readers[k] = []
        self.ops[eng].append((sorted(waits.items()), emit, stream, 16 if dma else 1))

    def barrier_waits(self, eng, streams=None):
        waits = []
        for s in (streams or list(self.count)):
            if s == eng or self.count[s] == 0:
                continue
            w = self.marks.get((eng, s), 0)
            if self.count[s] > w:
                waits.append((s, self.count[s]))
                self.marks[(eng, s)] = self.count[s]
        if waits:
            self.ops[eng].append((waits, None, None, 0))

    def finalize(self):
        used = [s for s in self.count if self.count[s] > 0]
        sems = {s: self.st.enter_context(self.nc.semaphore(s)) for s in used}
        blk = self.st.enter_context(self.nc.Block())
        for eng, section_name in ENG2SECTION.items():
            if not self.ops[eng]:
                continue
            section = getattr(blk, section_name)
            eng_ops = self.ops[eng]

            def body(engobj, eng_ops=eng_ops):
                for waits, emit, stream, incv in eng_ops:
                    for s, cval in waits:
                        engobj.wait_ge(sems[s], cval)
                    if emit is not None:
                        ins = emit(engobj)
                        ins.then_inc(sems[stream], incv)
            section(body)
        self.st.close()
        return self.nc


class Buf:
    def __init__(self, v, keys):
        self.v = v
        self.k = list(keys)



def pretile_lhsT(W):
    K, M = W.shape
    return np.ascontiguousarray(
        W.reshape(K // 128, 128, M // 128, 128).transpose(2, 1, 0, 3))


def pretile_rhs(W):
    K, N = W.shape
    return np.ascontiguousarray(W.reshape(K // 128, 128, N))


def resident_rhs(W):
    K, n = W.shape
    return np.ascontiguousarray(W.reshape(K // 128, 128, n).transpose(1, 0, 2))


def bias_pp(b):
    return np.ascontiguousarray(b.reshape(-1, 128).T)


def prep_host(inputs):
    g = {k: np.asarray(v) for k, v in inputs.items()}
    H = {}
    w1s = [g["L0_ffn_ew1"][e] for e in range(8)] + [g["L0_ffn_sw1"]]
    w2s = [g["L0_ffn_ew2"][e] for e in range(8)] + [g["L0_ffn_sw2"]]
    w3s = [g["L0_ffn_ew3"][e] for e in range(8)] + [g["L0_ffn_sw3"]]
    H["F0A"] = np.stack([pretile_lhsT(w) for w in w1s])
    H["F0B"] = np.stack([pretile_lhsT(w) for w in w2s])

    def grp_split(arr, ngrp):
        dc, pp, kc, m = arr.shape
        return np.ascontiguousarray(
            arr.reshape(dc, pp, ngrp, kc // ngrp, m).transpose(2, 0, 1, 3, 4))
    H["F0C"] = np.stack([grp_split(pretile_lhsT(w), 1) for w in w3s])
    H["gw0"] = resident_rhs(g["L0_ffn_gate_w"])
    H["F1A"] = np.stack([pretile_lhsT(g["L1_ffn_W1"][e].T) for e in range(8)])
    H["F1B"] = np.stack([pretile_lhsT(g["L1_ffn_W3"][e].T) for e in range(8)])
    H["F1C"] = np.stack([grp_split(pretile_lhsT(g["L1_ffn_W2"][e].T), 2)
                         for e in range(8)])
    H["gw1"] = resident_rhs(np.ascontiguousarray(g["L1_ffn_Wr"].T))
    Wqk = g["L0_attn_Wqk"]
    H["Wq_l"] = pretile_lhsT(np.ascontiguousarray(Wqk[:, :1024]))
    H["Wk_l"] = pretile_lhsT(np.ascontiguousarray(Wqk[:, 1024:]))
    H["Wk_r"] = pretile_rhs(np.ascontiguousarray(Wqk[:, 1024:]))
    H["Wv_r"] = pretile_rhs(g["L0_attn_Wv"])
    H["Wo_l"] = pretile_lhsT(g["L0_attn_Wo"])
    H["Wn_r"] = resident_rhs(g["L0_attn_Wn"])
    H["bq_pp"] = bias_pp(g["L0_attn_bqk"][:1024])
    H["bk_pp"] = bias_pp(g["L0_attn_bqk"][1024:])
    H["bk_row"] = np.ascontiguousarray(g["L0_attn_bqk"][1024:]).reshape(1, 1024)
    H["bv_row"] = np.ascontiguousarray(g["L0_attn_bv"]).reshape(1, 1024)
    H["bn_pp"] = np.ascontiguousarray(g["L0_attn_bn"]).reshape(16, 1)
    H["bo_pp"] = bias_pp(g["L0_attn_bo"])
    H["Wi_l"] = pretile_lhsT(np.ascontiguousarray(g["L1_attn_W_in"][1024:, :].T))
    H["Wg_l"] = pretile_lhsT(np.ascontiguousarray(g["L1_attn_W_gate"].T))
    H["Bm_r"] = resident_rhs(np.ascontiguousarray(g["L1_attn_Bm"].T))
    H["Cm_r"] = resident_rhs(g["L1_attn_Cm"])
    H["WoT_r"] = pretile_rhs(np.ascontiguousarray(g["L1_attn_W_out"].T))
    H["A_lhs"] = np.ascontiguousarray(g["L1_attn_A"])
    H["AT"] = np.ascontiguousarray(g["L1_attn_A"].T)
    H["bi_pp"] = bias_pp(g["L1_attn_b_in"][1024:])
    H["bg_pp"] = bias_pp(g["L1_attn_b_gate"])
    H["bout_pp"] = bias_pp(g["L1_attn_b_out"])
    H["I128"] = np.eye(128, dtype=np.float32)
    H["I16bf"] = np.eye(16, dtype=np.float32).astype(ml_dtypes.bfloat16)
    H["I16f"] = np.eye(16, dtype=np.float32)
    H["ones_col_bf"] = np.ones((128, 1), dtype=np.float32).astype(ml_dtypes.bfloat16)
    H["ones_row"] = np.ones((1, 128), dtype=np.float32)
    H["ones_row_bf"] = np.ones((1, 128), dtype=np.float32).astype(ml_dtypes.bfloat16)
    H["triuT"] = np.triu(np.ones((128, 128), dtype=np.float32))
    eo = np.zeros((8, 8, 128), dtype=np.float32)
    for e in range(8):
        eo[e, e, :] = 1.0
    H["eoneh"] = np.ascontiguousarray(eo.transpose(1, 0, 2).reshape(8, 8 * 128))
    ho = np.zeros((16, 16, 64), dtype=np.float32)
    for h in range(16):
        ho[h, h, :] = 1.0
    H["honehot"] = np.ascontiguousarray(ho.transpose(1, 0, 2).reshape(16, 16 * 64)).astype(ml_dtypes.bfloat16)
    PC = []
    x = g["L0_attn_x"]
    for cid in range(NCORE):
        b, hh = divmod(cid, 2)
        d = {"xin": np.ascontiguousarray(x[b, hh * T:(hh + 1) * T, :].T)}
        sel = np.zeros((128, 16), dtype=np.float32)
        if hh == 1:
            for i in range(16):
                sel[(cid - 1) * 16 + i, i] = 1.0
        d["selmat"] = sel
        PC.append(d)
    return H, PC



PHASE_WEIGHTS = {
    "ffn0": ["F0A", "F0B", "F0C", "gw0"],
    "ffn1": ["F1A", "F1B", "F1C", "gw1"],
    "attn0": ["Wq_l", "Wk_l", "Wk_r", "Wv_r", "Wo_l", "Wn_r", "bq_pp", "bk_pp",
              "bk_row", "bv_row", "bn_pp", "bo_pp", "selmat", "honehot"],
    "attn1": ["Wi_l", "Wg_l", "Bm_r", "Cm_r", "WoT_r", "A_lhs", "AT", "I16f",
              "bi_pp", "bg_pp", "bout_pp", "selmat"],
}
CONST_NAMES = ["I128", "I16bf", "ones_col_bf", "ones_row", "ones_row_bf", "triuT", "eoneh"]

NSCR = 49
NSCF = 7



def wbk(unit, half=None):
    if half is None:
        return [("wb", unit, 0), ("wb", unit, 1)]
    return [("wb", unit, half)]

class Ctx:
    pass


def alloc(p, phases, host):
    c = Ctx()
    nc = p.nc
    c.din = {}
    names = set(CONST_NAMES)
    for ph in phases:
        names.update(PHASE_WEIGHTS[ph])
    names.add("xin")
    for n in sorted(names):
        arr = host.get(n)
        if arr is None:
            if n == "xin":
                arr = np.zeros((D, T), np.float32)
            elif n == "selmat":
                arr = np.zeros((128, 16), np.float32)
        dt = BF16 if arr.dtype == ml_dtypes.bfloat16 else F32
        c.din[n] = nc.dram_tensor(n, list(arr.shape), dt, kind="ExternalInput").ap()
    c.yout = nc.dram_tensor("y", [D, T], F32, kind="ExternalOutput").ap()
    c.innames = names

    c.x = [p.sbuf(f"x{i}", [128, T], F32) for i in range(DCH)]
    c.xn = [p.sbuf(f"xn{i}", [128, T], BF16) for i in range(DCH)]
    c.SCR = p.sbuf("SCR", [128, NSCR * 512], BF16)
    c.SCF = p.sbuf("SCF", [128, NSCF * 512], F32)
    c.wbuf = [p.sbuf(f"wbuf{i}", [128, 1024], BF16) for i in range(4)]
    c.w2buf = [p.sbuf(f"w2buf{i}", [128, 2048], BF16) for i in range(2)]
    c.ps = [p.psum(f"ps{i}", [128, 512]) for i in range(8)]
    c.wb_sb = p.sbuf("wb_sb", [128, 1024], F32)
    c.w_tm = p.sbuf("w_tm", [128, 128], F32)
    c.w_fm = p.sbuf("w_fm", [8, T], F32)
    c.stat = p.sbuf("stat", [1, 1536], F32)
    c.rst = p.sbuf("rst", [128, 16], F32)
    c.r8 = [p.sbuf(f"r8_{i}", [128, 8], F32) for i in range(4)]
    c.gS = p.sbuf("gS", [128, 8], F32)
    c.I128 = p.sbuf("cI128", [128, 128], F32)
    c.I16bf = p.sbuf("cI16", [16, 16], BF16)
    c.ones_col_bf = p.sbuf("cOnesCol", [128, 1], BF16)
    c.ones_row = p.sbuf("cOnesRow", [1, 128], F32)
    c.ones_row_bf = p.sbuf("cOnesRowBf", [1, 128], BF16)
    c.triuT = p.sbuf("cTriu", [128, 128], F32)
    c.eoneh = p.sbuf("cEoneh", [8, 8 * 128], F32)
    c.gw = {}
    if "ffn0" in phases:
        c.gw["ffn0"] = p.sbuf("gw0sb", [128, 8, 8], BF16)
    if "ffn1" in phases:
        c.gw["ffn1"] = p.sbuf("gw1sb", [128, 8, 8], BF16)
    c.dwscr = nc.dram_tensor("dwscr", [16, T], F32).ap()
    need_attn = ("attn0" in phases) or ("attn1" in phases)
    if need_attn:
        c.selm = p.sbuf("selm", [128, 16], F32)
    if "attn0" in phases:
        c.biases0 = {n: p.sbuf("sb_" + n, list(host[n].shape), F32)
                     for n in ("bq_pp", "bk_pp", "bo_pp", "bn_pp")}
        c.negbn = p.sbuf("negbn", [16, 1], F32)
        c.Wn = p.sbuf("WnSb", [128, 8, 16], BF16)
        c.bk_bf = p.sbuf("bk_bf", [1, 1024], BF16)
        c.bv_bf = p.sbuf("bv_bf", [1, 1024], BF16)
        c.honehot = p.sbuf("cHoneh", [16, 16 * 64], BF16)
        c.cinL0 = nc.dram_tensor("cinL0", [64, 1024], F32)
        c.coutL0 = nc.dram_tensor("coutL0", [512, 1024], F32)
        c.dSp = nc.dram_tensor("dSp", [64, 1024], F32).ap()
    if "attn1" in phases:
        c.biases1 = {n: p.sbuf("sb_" + n, list(host[n].shape), F32)
                     for n in ("bi_pp", "bg_pp", "bout_pp")}
        c.Bm = p.sbuf("BmSb", [128, 8, 16], BF16)
        c.Cm = p.sbuf("CmSb", [128, 8, 16], BF16)
        c.A_sb = p.sbuf("A_sb", [16, 16], F32)
        c.I16f = p.sbuf("sI16f", [16, 16], F32)
        c.svec = [p.sbuf(f"svec{i}", [16, 1], F32) for i in range(2)]
        c.cinL1 = nc.dram_tensor("cinL1", [16, 1], F32)
        c.coutL1 = nc.dram_tensor("coutL1", [128, 1], F32)
        c.Gl1 = p.sbuf("Gl1", [128, 1], F32)
    return c


def scr(c, g, ngr=1, parts=128, p0=0):
    v = c.SCR[p0:p0 + parts, g * 512:(g + ngr) * 512]
    return Buf(v, [("scr", g + k) for k in range(ngr)])


def scf(c, g, ngr=1, parts=128, p0=0):
    v = c.SCF[p0:p0 + parts, g * 512:(g + ngr) * 512]
    return Buf(v, [("scf", g + k) for k in range(ngr)])


def emit_load_common(p, c, phases):
    for i in range(DCH):
        p.op("sp", lambda s, i=i: s.dma_start(out=c.x[i][:, :], in_=c.din["xin"][i * 128:(i + 1) * 128, :]),
             writes=[("x", i, j) for j in range(4)], dma="qsx")
    loads = [(c.I128, "I128"), (c.I16bf, "I16bf"), (c.ones_col_bf, "ones_col_bf"),
             (c.ones_row, "ones_row"), (c.ones_row_bf, "ones_row_bf"), (c.triuT, "triuT"),
             (c.eoneh, "eoneh")]
    for t, n in loads:
        p.op("sp", lambda s, t=t, n=n: s.dma_start(out=t[:, :], in_=c.din[n]),
             writes=[("s", n)], dma="qsx")
    for ph, t in c.gw.items():
        n = "gw0" if ph == "ffn0" else "gw1"
        p.op("pool", lambda g, t=t, n=n: g.dma_start(out=t[:, :, :], in_=c.din[n]),
             writes=[("s", "gw_" + ph)], dma="qi")
    if hasattr(c, "selm"):
        p.op("sp", lambda s: s.dma_start(out=c.selm[:, :], in_=c.din["selmat"]),
             writes=[("s", "selm")], dma="qsx")
    if "attn0" in phases:
        for n, t in c.biases0.items():
            p.op("sp", lambda s, t=t, n=n: s.dma_start(out=t[:, :], in_=c.din[n]),
                 writes=[("s", n)], dma="qsx")
        p.op("pool", lambda g: g.dma_start(out=c.Wn[:, :, :], in_=c.din["Wn_r"]),
             writes=[("s", "Wn")], dma="qi")
        p.op("pool", lambda g: g.dma_start(out=c.bk_bf[:, :], in_=c.din["bk_row"]),
             writes=[("s", "bk_bf")], dma="qi")
        p.op("pool", lambda g: g.dma_start(out=c.bv_bf[:, :], in_=c.din["bv_row"]),
             writes=[("s", "bv_bf")], dma="qi")
        p.op("sp", lambda s: s.dma_start(out=c.honehot[:, :], in_=c.din["honehot"]),
             writes=[("s", "honehot")], dma="qi")
        p.op("dve", lambda v: v.tensor_scalar(c.negbn[:, :], c.biases0["bn_pp"][:, :], -1.0, None, ALU.mult),
             reads=[("s", "bn_pp")], writes=[("s", "negbn")])
    if "attn1" in phases:
        for n, t in c.biases1.items():
            p.op("sp", lambda s, t=t, n=n: s.dma_start(out=t[:, :], in_=c.din[n]),
                 writes=[("s", n)], dma="qsx")
        p.op("pool", lambda g: g.dma_start(out=c.Bm[:, :, :], in_=c.din["Bm_r"]),
             writes=[("s", "Bm")], dma="qi")
        p.op("pool", lambda g: g.dma_start(out=c.Cm[:, :, :], in_=c.din["Cm_r"]),
             writes=[("s", "Cm")], dma="qi")
        p.op("sp", lambda s: s.dma_start(out=c.A_sb[:, :], in_=c.din["A_lhs"]),
             writes=[("s", "A_sb")], dma="qsx")
        p.op("sp", lambda s: s.dma_start(out=c.I16f[:, :], in_=c.din["I16f"]),
             writes=[("s", "I16f")], dma="qsx")


def emit_store(p, c):
    for i in range(DCH):
        p.op("sp", lambda s, i=i: s.dma_start(out=c.yout[i * 128:(i + 1) * 128, :], in_=c.x[i][:, :]),
             reads=[("x", i, j) for j in range(4)], dma="qsx")
    p.barrier_waits("sp")



def emit_ln(p, c):
    for j in range(4):
        sl = slice(j * 512, (j + 1) * 512)
        for i in range(DCH):
            xb = scr(c, 40 + i % 2)
            xq = scr(c, 42 + i % 2)
            p.op("dve", lambda v, i=i, xb=xb, sl=sl: v.tensor_copy(xb.v, c.x[i][:, sl]),
                 reads=[("x", i, j)], writes=xb.k)
            p.op("pe", lambda t, xb=xb, i=i: t.matmul(
                c.ps[6][0:1, :], c.ones_col_bf[:, :], xb.v,
                start=(i == 0), stop=(i == DCH - 1)),
                reads=xb.k + [("s", "ones_col_bf")], writes=[("ps", 6)])
            p.op("act", lambda s, i=i, xq=xq, sl=sl: s.activation(xq.v, c.x[i][:, sl], AF.Square, bias=0.0, scale=1.0),
                 reads=[("x", i, j)], writes=xq.k)
            p.op("pe", lambda t, xq=xq, i=i: t.matmul(
                c.ps[7][0:1, :], c.ones_col_bf[:, :], xq.v,
                start=(i == 0), stop=(i == DCH - 1)),
                reads=xq.k + [("s", "ones_col_bf")], writes=[("ps", 7)])
        m_row = c.stat[0:1, 0:512]
        rows = Buf(None, [("s", "lnrows")])
        t0 = c.stat[0:1, 512:1024]
        t1 = c.stat[0:1, 1024:1536]
        p.op("dve", lambda v, m_row=m_row: v.tensor_scalar(m_row, c.ps[6][0:1, :], 1.0 / D, None, ALU.mult),
             reads=[("ps", 6)], writes=[("s", "lnm")])
        p.op("dve", lambda v: v.tensor_scalar(t0, c.ps[7][0:1, :], 1.0 / D, None, ALU.mult),
             reads=[("ps", 7)], writes=rows.k)
        p.op("dve", lambda v, m_row=m_row: v.tensor_tensor(t1, m_row, m_row, ALU.mult),
             reads=[("s", "lnm")], writes=rows.k)
        p.op("dve", lambda v: v.tensor_tensor(t0, t0, t1, ALU.subtract),
             reads=rows.k, writes=rows.k)
        p.op("dve", lambda v: v.tensor_scalar(t0, t0, 1e-5, None, ALU.add),
             reads=rows.k, writes=rows.k)
        p.op("act", lambda s: s.activation(t1, t0, AF.Sqrt, bias=0.0, scale=1.0),
             reads=rows.k, writes=rows.k)
        p.op("dve", lambda v: v.reciprocal(t0, t1), reads=rows.k, writes=rows.k, fence=True)
        p.op("dve", lambda v, m_row=m_row: v.tensor_tensor(t1, m_row, t0, ALU.mult),
             reads=[("s", "lnm")] + rows.k, writes=rows.k)
        p.op("dve", lambda v: v.tensor_scalar(t1, t1, -1.0, None, ALU.mult),
             reads=rows.k, writes=rows.k)
        p.op("pe", lambda t: t.matmul(c.ps[4][:, :], c.ones_row[:, :], t0, start=True, stop=True),
             reads=rows.k + [("s", "ones_row")], writes=[("ps", 4)], fence=True)
        p.op("pe", lambda t: t.matmul(c.ps[5][:, :], c.ones_row[:, :], t1, start=True, stop=True),
             reads=rows.k + [("s", "ones_row")], writes=[("ps", 5)])
        tt = scf(c, 4)
        for i in range(DCH):
            p.op("dve", lambda v, i=i, sl=sl, tt=tt: v.tensor_tensor(tt.v, c.x[i][:, sl], c.ps[4][:, :], ALU.mult),
                 reads=[("x", i, j), ("ps", 4)], writes=tt.k)
            p.op("dve", lambda v, i=i, sl=sl, tt=tt: v.tensor_tensor(c.xn[i][:, sl], tt.v, c.ps[5][:, :], ALU.add),
                 reads=tt.k + [("ps", 5)], writes=[("xn", i, j)])



def emit_moe(p, c, kind):
    shared = kind == "ffn0"
    nexp = 8
    H = 2048 if kind == "ffn0" else 4096
    NGRP = (H // 128) // 16
    FA, FB, FC = {"ffn0": ("F0A", "F0B", "F0C"), "ffn1": ("F1A", "F1B", "F1C")}[kind]
    gw = c.gw[kind]

    for tc in range(16):
        j = tc // 4

        def g_mm(t, tc=tc):
            ins = None
            for kc in range(DCH):
                ins = t.matmul(c.ps[6][:, 0:8], c.xn[kc][:, tc * 128:(tc + 1) * 128],
                               gw[:, kc, :], start=(kc == 0), stop=(kc == DCH - 1))
            return ins
        p.op("pe", g_mm, reads=[("xn", kc, j) for kc in range(DCH)] + [("s", "gw_" + kind)],
             writes=[("ps", 6)])
        r0 = c.rst
        e1, e2, pb, tb = c.r8[0], c.r8[1], c.r8[2], c.r8[3]
        gS = c.gS
        wsl = c.w_tm[:, tc * 8:(tc + 1) * 8]
        p.op("dve", lambda v: v.tensor_copy(gS[:, :], c.ps[6][:, 0:8]),
             reads=[("ps", 6)], writes=[("s", "gS")])
        if kind == "ffn0":
            p.op("dve", lambda v: v.reduce_max(r0[:, 0:1], gS[:, :], axis=AX.X),
                 reads=[("s", "gS")], writes=[("s", "r0")], fence=True)
            p.op("dve", lambda v: v.tensor_scalar(r0[:, 1:2], r0[:, 0:1], -1.0, None, ALU.mult),
                 reads=[("s", "r0")], writes=[("s", "r1")])
            p.op("act", lambda s: s.activation(e1[:, :], gS[:, :], AF.Identity, bias=r0[:, 1:2], scale=1.0),
                 reads=[("s", "gS"), ("s", "r1")], writes=[("s", "e1")])
            p.op("dve", lambda v: v.tensor_scalar(e1[:, :], e1[:, :], 0.0, None, ALU.is_ge),
                 reads=[("s", "e1")], writes=[("s", "e1")])
            p.op("dve", lambda v: v.tensor_scalar(tb[:, :], e1[:, :], BIG_NEG, None, ALU.mult),
                 reads=[("s", "e1")], writes=[("s", "tb")])
            p.op("dve", lambda v: v.tensor_tensor(pb[:, :], gS[:, :], tb[:, :], ALU.add),
                 reads=[("s", "gS"), ("s", "tb")], writes=[("s", "pb")])
            p.op("dve", lambda v: v.reduce_max(r0[:, 2:3], pb[:, :], axis=AX.X),
                 reads=[("s", "pb")], writes=[("s", "r2")], fence=True)
            p.op("dve", lambda v: v.tensor_scalar(r0[:, 3:4], r0[:, 2:3], -1.0, None, ALU.mult),
                 reads=[("s", "r2")], writes=[("s", "r3")])
            p.op("act", lambda s: s.activation(e2[:, :], pb[:, :], AF.Identity, bias=r0[:, 3:4], scale=1.0),
                 reads=[("s", "pb"), ("s", "r3")], writes=[("s", "e2")])
            p.op("dve", lambda v: v.tensor_scalar(e2[:, :], e2[:, :], 0.0, None, ALU.is_ge),
                 reads=[("s", "e2")], writes=[("s", "e2")])
            p.op("dve", lambda v: v.tensor_tensor(r0[:, 4:5], r0[:, 0:1], r0[:, 2:3], ALU.subtract),
                 reads=[("s", "r0"), ("s", "r2")], writes=[("s", "r4")])
            p.op("act", lambda s: s.activation(r0[:, 5:6], r0[:, 4:5], AF.Sigmoid, bias=0.0, scale=1.0),
                 reads=[("s", "r4")], writes=[("s", "r5")])
            p.op("act", lambda s: s.activation(r0[:, 6:7], r0[:, 4:5], AF.Sigmoid, bias=0.0, scale=-1.0),
                 reads=[("s", "r4")], writes=[("s", "r6")])
            p.op("dve", lambda v: v.tensor_scalar(r0[:, 7:8], r0[:, 5:6], 0.5, None, ALU.mult),
                 reads=[("s", "r5")], writes=[("s", "r7")])
            p.op("dve", lambda v: v.tensor_scalar(r0[:, 8:9], r0[:, 6:7], 0.5, None, ALU.mult),
                 reads=[("s", "r6")], writes=[("s", "r8")])
            p.op("dve", lambda v: v.tensor_tensor(e1[:, :], e1[:, :], r0[:, 7:8].broadcast_to([128, 8]), ALU.mult),
                 reads=[("s", "e1"), ("s", "r7")], writes=[("s", "e1")])
            p.op("dve", lambda v: v.tensor_tensor(e2[:, :], e2[:, :], r0[:, 8:9].broadcast_to([128, 8]), ALU.mult),
                 reads=[("s", "e2"), ("s", "r8")], writes=[("s", "e2")])
            p.op("dve", lambda v, wsl=wsl: v.tensor_tensor(wsl, e1[:, :], e2[:, :], ALU.add),
                 reads=[("s", "e1"), ("s", "e2")], writes=[("s", "wtm", tc)])
        else:
            p.op("dve", lambda v: v.reduce_max(r0[:, 0:1], gS[:, :], axis=AX.X),
                 reads=[("s", "gS")], writes=[("s", "r0")], fence=True)
            p.op("dve", lambda v: v.tensor_scalar(r0[:, 1:2], r0[:, 0:1], -1.0, None, ALU.mult),
                 reads=[("s", "r0")], writes=[("s", "r1")])
            p.op("act", lambda s: s.activation(e1[:, :], gS[:, :], AF.Exp, bias=r0[:, 1:2], scale=1.0),
                 reads=[("s", "gS"), ("s", "r1")], writes=[("s", "e1")])
            p.op("dve", lambda v: v.reduce_sum(r0[:, 2:3], e1[:, :], axis=AX.X),
                 reads=[("s", "e1")], writes=[("s", "r2")], fence=True)
            p.op("dve", lambda v: v.reciprocal(r0[:, 3:4], r0[:, 2:3]),
                 reads=[("s", "r2")], writes=[("s", "r3")], fence=True)
            p.op("dve", lambda v: v.tensor_tensor(pb[:, :], e1[:, :], r0[:, 3:4].broadcast_to([128, 8]), ALU.mult),
                 reads=[("s", "e1"), ("s", "r3")], writes=[("s", "pb")])
            p.op("dve", lambda v: v.reduce_max(r0[:, 4:5], pb[:, 1:8], axis=AX.X),
                 reads=[("s", "pb")], writes=[("s", "r4")], fence=True)
            p.op("dve", lambda v: v.tensor_scalar(r0[:, 5:6], r0[:, 4:5], -1.0, None, ALU.mult),
                 reads=[("s", "r4")], writes=[("s", "r5")])
            p.op("act", lambda s: s.activation(e2[:, 1:8], pb[:, 1:8], AF.Identity, bias=r0[:, 5:6], scale=1.0),
                 reads=[("s", "pb"), ("s", "r5")], writes=[("s", "e2")])
            p.op("dve", lambda v: v.tensor_scalar(e2[:, 1:8], e2[:, 1:8], 0.0, None, ALU.is_ge),
                 reads=[("s", "e2")], writes=[("s", "e2")])
            p.op("dve", lambda v: v.tensor_tensor(r0[:, 6:7], pb[:, 0:1], r0[:, 4:5], ALU.add),
                 reads=[("s", "pb"), ("s", "r4")], writes=[("s", "r6")])
            p.op("dve", lambda v: v.tensor_scalar(r0[:, 6:7], r0[:, 6:7], 1e-9, None, ALU.max),
                 reads=[("s", "r6")], writes=[("s", "r6")])
            p.op("dve", lambda v: v.reciprocal(r0[:, 7:8], r0[:, 6:7]),
                 reads=[("s", "r6")], writes=[("s", "r7")], fence=True)
            p.op("dve", lambda v: v.tensor_scalar(r0[:, 8:9], r0[:, 7:8], 0.5, None, ALU.mult),
                 reads=[("s", "r7")], writes=[("s", "r8")])
            p.op("dve", lambda v: v.tensor_tensor(r0[:, 9:10], pb[:, 0:1], r0[:, 8:9], ALU.mult),
                 reads=[("s", "pb"), ("s", "r8")], writes=[("s", "r9")])
            p.op("dve", lambda v, wsl=wsl: v.tensor_copy(wsl[:, 0:1], r0[:, 9:10]),
                 reads=[("s", "r9")], writes=[("s", "wtm", tc)])
            p.op("dve", lambda v: v.tensor_tensor(r0[:, 10:11], r0[:, 4:5], r0[:, 8:9], ALU.mult),
                 reads=[("s", "r4"), ("s", "r8")], writes=[("s", "r10")])
            p.op("dve", lambda v, wsl=wsl: v.tensor_tensor(wsl[:, 1:8], e2[:, 1:8], r0[:, 10:11].broadcast_to([128, 7]), ALU.mult),
                 reads=[("s", "e2"), ("s", "r10")], writes=[("s", "wtm", tc)])
        p.op("pe", lambda t, tc=tc, wsl=wsl: t.transpose(
            c.ps[7][0:8, (tc % 4) * 128:(tc % 4 + 1) * 128], wsl, c.I128[:, :]),
            reads=[("s", "wtm", tc), ("s", "I128")], writes=[("ps", 7)])
        if tc % 4 == 3:
            grp = tc // 4
            p.op("dve", lambda v, grp=grp: v.tensor_copy(
                c.w_fm[:, grp * 512:(grp + 1) * 512], c.ps[7][0:8, :]),
                reads=[("ps", 7)], writes=[("s", "wfm", grp)])

    for half in range(2):
        for e in range(nexp + (1 if shared else 0)):
            is_shared = shared and e == nexp
            if not is_shared:
                for j2 in range(2):
                    j = half * 2 + j2
                    p.op("pe", lambda t, e=e, j=j: t.matmul(
                        c.ps[6][:, :], c.eoneh[:, e * 128:(e + 1) * 128],
                        c.w_fm[:, j * 512:(j + 1) * 512], start=True, stop=True),
                        reads=[("s", "wfm", j), ("s", "eoneh")], writes=[("ps", 6)])
                    p.op("dve", lambda v, j2=j2: v.tensor_copy(c.wb_sb[:, j2 * 512:(j2 + 1) * 512], c.ps[6][:, :]),
                         reads=[("ps", 6)], writes=[("s", "wbsb", j2)])
            for grp in range(NGRP):
                for hc16 in range(16):
                    hc = grp * 16 + hc16
                    bA = c.wbuf[(hc16 % 2) * 2]
                    bB = c.wbuf[(hc16 % 2) * 2 + 1]
                    p.op("pool", lambda g, e=e, hc=hc, bA=bA: g.dma_start(
                        out=bA[:, :].rearrange("p (k m) -> p k m", k=8), in_=c.din[FA][e, hc]),
                        writes=wbk((hc16 % 2) * 2), dma=f"qw{(hc16 % 2) * 2}")
                    p.op("pool", lambda g, e=e, hc=hc, bB=bB: g.dma_start(
                        out=bB[:, :].rearrange("p (k m) -> p k m", k=8), in_=c.din[FB][e, hc]),
                        writes=wbk((hc16 % 2) * 2 + 1), dma=f"qw{(hc16 % 2) * 2 + 1}")
                    for j2 in range(2):
                        j = half * 2 + j2
                        bkA = (hc16 * 2 + j2) % 2
                        bkB = 2 + (hc16 * 2 + j2) % 2

                        def hmm(t, buf, bank, j):
                            ins = None
                            for kc in range(DCH):
                                ins = t.matmul(c.ps[bank][:, :], buf[:, kc * 128:(kc + 1) * 128],
                                               c.xn[kc][:, j * 512:(j + 1) * 512],
                                               start=(kc == 0), stop=(kc == DCH - 1))
                            return ins
                        p.op("pe", lambda t, bA=bA, bkA=bkA, j=j: hmm(t, bA, bkA, j),
                             reads=wbk((hc16 % 2) * 2) + [("xn", kc, j) for kc in range(DCH)],
                             writes=[("ps", bkA)])
                        p.op("pe", lambda t, bB=bB, bkB=bkB, j=j: hmm(t, bB, bkB, j),
                             reads=wbk((hc16 % 2) * 2 + 1) + [("xn", kc, j) for kc in range(DCH)],
                             writes=[("ps", bkB)])
                        sA = scr(c, 44)
                        sB = scr(c, 45)
                        p.op("act", lambda s, bkA=bkA, sA=sA: s.activation(
                            sA.v, c.ps[bkA][:, :], AF.Silu, bias=0.0, scale=1.0),
                            reads=[("ps", bkA)], writes=sA.k)
                        if is_shared:
                            p.op("dve", lambda v, bkB=bkB, sB=sB: v.tensor_scalar(
                                sB.v, c.ps[bkB][:, :], 0.5, None, ALU.mult),
                                reads=[("ps", bkB)], writes=sB.k)
                        else:
                            p.op("dve", lambda v, bkB=bkB, sB=sB, j2=j2: v.tensor_tensor(
                                sB.v, c.ps[bkB][:, :], c.wb_sb[:, j2 * 512:(j2 + 1) * 512], ALU.mult),
                                reads=[("ps", bkB), ("s", "wbsb", j2)], writes=sB.k)
                        hb = scr(c, hc16 * 2 + j2)
                        p.op("dve", lambda v, hb=hb, sA=sA, sB=sB: v.tensor_tensor(hb.v, sA.v, sB.v, ALU.mult),
                             reads=sA.k + sB.k, writes=hb.k)
                for dc in range(DCH):
                    wC = c.w2buf[dc % 2]
                    p.op("pool", lambda g, e=e, dc=dc, grp=grp, wC=wC: g.dma_start(
                        out=wC[:, :].rearrange("p (k m) -> p k m", k=16),
                        in_=c.din[FC][e, grp, dc]),
                        writes=[("s", "w2buf", dc % 2)], dma=f"qc{dc % 2}")
                    for j2 in range(2):
                        j = half * 2 + j2
                        bank = 4 + (dc * 2 + j2) % 2

                        def ymm(t, wC=wC, bank=bank, j2=j2):
                            ins = None
                            for k16 in range(16):
                                ins = t.matmul(c.ps[bank][:, :], wC[:, k16 * 128:(k16 + 1) * 128],
                                               c.SCR[:, (k16 * 2 + j2) * 512:(k16 * 2 + j2 + 1) * 512],
                                               start=(k16 == 0), stop=(k16 == 15))
                            return ins
                        p.op("pe", ymm,
                             reads=[("s", "w2buf", dc % 2)] + [("scr", k16 * 2 + j2) for k16 in range(16)],
                             writes=[("ps", bank)])
                        xv = c.x[dc][:, j * 512:(j + 1) * 512]
                        p.op("dve", lambda v, xv=xv, bank=bank: v.tensor_tensor(xv, xv, c.ps[bank][:, :], ALU.add),
                             reads=[("ps", bank), ("x", dc, j)], writes=[("x", dc, j)])



def proj_ktm_vtm(p, c, sub):
    for which, W, brow, bkey in (("ktm", "Wk_r", c.bk_bf, "bk_bf"),
                                 ("vtm", "Wv_r", c.bv_bf, "bv_bf")):
        gbase = 16 if which == "ktm" else 24
        for kh in range(2):
            for kc in range(DCH):
                u = c.wbuf[kc // 2][:, (kc % 2) * 512:(kc % 2 + 1) * 512]
                p.op("pool", lambda g, W=W, kc=kc, kh=kh, u=u: g.dma_start(
                    out=u, in_=c.din[W][kc, :, kh * 512:(kh + 1) * 512]),
                    writes=wbk(kc // 2, kc % 2), dma=f"qw{kc // 2}")
            for tc in range(4):
                bank = tc % 2
                tok = sub * 512 + tc * 128

                def kmm(t, kh=kh, tok=tok, bank=bank, brow=brow):
                    ins = None
                    for kc in range(DCH):
                        u = c.wbuf[kc // 2][:, (kc % 2) * 512:(kc % 2 + 1) * 512]
                        ins = t.matmul(c.ps[bank][:, :], c.xn[kc][:, tok:tok + 128], u,
                                       start=(kc == 0), stop=False)
                    ins = t.matmul(c.ps[bank][:, :], c.ones_row_bf[0:1, tok % 128:tok % 128 + 128],
                                   brow[0:1, kh * 512:(kh + 1) * 512], start=False, stop=True)
                    return ins
                p.op("pe", kmm,
                     reads=[k for kc in range(DCH) for k in wbk(kc // 2, kc % 2)]
                     + [("xn", kc, sub) for kc in range(DCH)]
                     + [("s", bkey), ("s", "ones_row_bf")],
                     writes=[("ps", bank)])
                dst = scr(c, gbase + tc * 2 + kh)
                p.op("dve", lambda v, dst=dst, bank=bank: v.tensor_copy(dst.v, c.ps[bank][:, :]),
                     reads=[("ps", bank)], writes=dst.k)


def emit_attn0(p, c):
    for sub in range(4):
        proj_ktm_vtm(p, c, sub)
        for tc in range(4):
            gchunk = sub * 4 + tc
            for h in range(16):
                bank = 6 + h // 8
                ktm = scr(c, 16 + tc * 2, 2)
                vtm = scr(c, 24 + tc * 2, 2)
                lv = ktm.v[:, h * 64:(h + 1) * 64]
                rv = vtm.v[:, h * 64:(h + 1) * 64]
                ov = c.ps[bank][0:64, (h % 8) * 64:(h % 8 + 1) * 64]
                st = gchunk == 0 and (h % 8) == 0
                p.op("pe", lambda t, lv=lv, rv=rv, ov=ov, st=st: t.matmul(ov, lv, rv, start=st, stop=True),
                     reads=ktm.k + vtm.k, writes=[("ps", bank)])
    SL = scf(c, 0, 2, parts=64)
    for i2 in range(2):
        p.op("dve", lambda v, i2=i2: v.tensor_copy(SL.v[:, i2 * 512:(i2 + 1) * 512], c.ps[6 + i2][0:64, :]),
             reads=[("ps", 6 + i2)], writes=SL.k)
    p.op("sp", lambda s: s.dma_start(out=c.cinL0.ap(), in_=SL.v),
         reads=SL.k, writes=[("d", "cin0")], dma="qsx")
    p.op("pool", lambda g: g.collective_compute(
        "AllGather", ALU.bypass, replica_groups=[list(range(NCORE))],
        ins=[c.cinL0.ap().opt()], outs=[c.coutL0.ap().opt()]),
        reads=[("d", "cin0")], writes=[("d", "cout0")])
    Sp = scf(c, 0, 2, parts=64)
    Ssb = scr(c, 48)

    def sview(h):
        pb = (h % 2) * 64
        return Ssb.v[pb:pb + 64, (h // 2) * 64:(h // 2) * 64 + 64]

    def emit_sel():
        for rg in range(4):
            G = scf(c, 2, 2)
            for r in range(8):
                p.op("sp", lambda s, rg=rg, G=G, r=r: s.dma_start(
                    out=G.v[r * 16:(r + 1) * 16, :],
                    in_=c.coutL0.ap()[r * 64 + rg * 16:r * 64 + rg * 16 + 16, :]),
                    reads=[("d", "cout0")], writes=G.k, dma="qsx")
            for i2 in range(2):
                p.op("pe", lambda t, G=G, i2=i2: t.matmul(
                    c.ps[4][0:16, :], c.selm[:, :], G.v[:, i2 * 512:(i2 + 1) * 512],
                    start=True, stop=True),
                    reads=G.k + [("s", "selm")], writes=[("ps", 4)])
                stv = scf(c, 6, parts=16)
                p.op("dve", lambda v, stv=stv: v.tensor_copy(stv.v, c.ps[4][0:16, :]),
                     reads=[("ps", 4)], writes=stv.k)
                p.op("sp", lambda s, rg=rg, i2=i2, stv=stv: s.dma_start(
                    out=c.dSp[rg * 16:(rg + 1) * 16, i2 * 512:(i2 + 1) * 512], in_=stv.v),
                    reads=stv.k, writes=[("d", "dSp", rg, i2)], dma="qsx")
        p.op("sp", lambda s: s.dma_start(out=Sp.v.rearrange("p (a f) -> p a f", a=2),
                                         in_=c.dSp.rearrange("p (a f) -> p a f", a=2)),
             reads=[("d", "dSp", rg, i2) for rg in range(4) for i2 in range(2)],
             writes=Sp.k, dma="qsx")
    def emit_ssb_init():
        for par in range(2):
            p.op("dve", lambda v, par=par: v.tensor_copy(
                Ssb.v[par * 64:(par + 1) * 64, :].rearrange("p (a b) -> p a b", b=64),
                Sp.v.rearrange("p (a b) -> p a b", b=64)[:, par::2, :]),
                reads=Sp.k, writes=Ssb.k)

    for sub in range(4):
        tok = slice(sub * 512, (sub + 1) * 512)
        for which, W, bias, bkey, gbase in (("q", "Wq_l", c.biases0["bq_pp"], "bq_pp", 0),
                                            ("k", "Wk_l", c.biases0["bk_pp"], "bk_pp", 8)):
            for qc in range(DCH):
                wb = c.wbuf[qc % 2]
                p.op("pool", lambda g, W=W, qc=qc, wb=wb: g.dma_start(
                    out=wb[:, :].rearrange("p (k m) -> p k m", k=8), in_=c.din[W][qc]),
                    writes=wbk(qc % 2), dma=f"qw{qc % 2}")
                bank = qc % 2

                def pmm(t, wb=wb, bank=bank, tok=tok):
                    ins = None
                    for kc in range(DCH):
                        ins = t.matmul(c.ps[bank][:, :], wb[:, kc * 128:(kc + 1) * 128],
                                       c.xn[kc][:, tok], start=(kc == 0), stop=(kc == DCH - 1))
                    return ins
                p.op("pe", pmm,
                     reads=wbk(qc % 2) + [("xn", kc, sub) for kc in range(DCH)],
                     writes=[("ps", bank)])
                dst = scr(c, gbase + qc)
                p.op("act", lambda s, dst=dst, bank=bank, bias=bias, qc=qc: s.activation(
                    dst.v, c.ps[bank][:, :], AF.Identity, bias=bias[:, qc:qc + 1], scale=1.0),
                    reads=[("ps", bank), ("s", bkey)], writes=dst.k)
        proj_ktm_vtm(p, c, sub)
        def nmm(t, tok=tok):
            ins = None
            for kc in range(DCH):
                ins = t.matmul(c.ps[4][0:16, :], c.Wn[:, kc, :], c.xn[kc][:, tok],
                               start=(kc == 0), stop=(kc == DCH - 1))
            return ins
        p.op("pe", nmm, reads=[("s", "Wn")] + [("xn", kc, sub) for kc in range(DCH)],
             writes=[("ps", 4)])
        nsc = Buf(c.SCR[0:16, 44 * 512:45 * 512], [("scr", 44)])
        p.op("act", lambda s, nsc=nsc: s.activation(nsc.v, c.ps[4][0:16, :], AF.Exp,
                                                    bias=c.negbn[:, :], scale=-1.0),
             reads=[("ps", 4), ("s", "negbn")], writes=nsc.k)
        if sub == 0:
            emit_sel()
            emit_ssb_init()
        for tc in range(4):
            gchunk = sub * 4 + tc
            ktm = scr(c, 16 + tc * 2, 2)
            vtm = scr(c, 24 + tc * 2, 2)
            for h in range(16):
                qb = scr(c, 0 + h // 2)
                kb = scr(c, 8 + h // 2)
                pb = (h % 2) * 64
                sbank = h % 2
                att = scr(c, 46 + h % 2)
                p.op("pe", lambda t, kb=kb, qb=qb, pb=pb, tc=tc, sbank=sbank: t.matmul(
                    c.ps[sbank][:, 0:128], kb.v[pb:pb + 64, tc * 128:(tc + 1) * 128],
                    qb.v[pb:pb + 64, tc * 128:(tc + 1) * 128], start=True, stop=True),
                    reads=kb.k + qb.k, writes=[("ps", sbank)])
                p.op("dve", lambda v, att=att, sbank=sbank: v.tensor_tensor(
                    att.v[:, 0:128], c.ps[sbank][:, 0:128], c.triuT[:, :], ALU.mult),
                    reads=[("ps", sbank), ("s", "triuT")], writes=att.k)
                cbank = 2 + h % 2
                p.op("pe", lambda t, vtm=vtm, att=att, h=h, cbank=cbank: t.matmul(
                    c.ps[cbank][0:64, 0:128], vtm.v[:, h * 64:(h + 1) * 64], att.v[:, 0:128],
                    start=True, stop=False),
                    reads=vtm.k + att.k, writes=[("ps", cbank)])
                p.op("pe", lambda t, qb=qb, pb=pb, tc=tc, h=h, cbank=cbank: t.matmul(
                    c.ps[cbank][0:64, 0:128], sview(h),
                    qb.v[pb:pb + 64, tc * 128:(tc + 1) * 128], start=False, stop=True),
                    reads=Ssb.k + qb.k, writes=[("ps", cbank)])
                ctx = scr(c, 32 + h // 2)
                p.op("dve", lambda v, ctx=ctx, pb=pb, tc=tc, cbank=cbank: v.tensor_copy(
                    ctx.v[pb:pb + 64, tc * 128:(tc + 1) * 128], c.ps[cbank][0:64, 0:128]),
                    reads=[("ps", cbank)], writes=ctx.k)
            for h in range(16):
                bank = 6 + h // 8
                lv = ktm.v[:, h * 64:(h + 1) * 64]
                rv = vtm.v[:, h * 64:(h + 1) * 64]
                ov = c.ps[bank][0:64, (h % 8) * 64:(h % 8 + 1) * 64]
                st = gchunk == 0 and (h % 8) == 0
                p.op("pe", lambda t, lv=lv, rv=rv, ov=ov, st=st: t.matmul(ov, lv, rv, start=st, stop=True),
                     reads=ktm.k + vtm.k, writes=[("ps", bank)])
                p.op("dve", lambda v, h=h, bank=bank: v.tensor_tensor(
                    sview(h), c.ps[bank][0:64, (h % 8) * 64:(h % 8 + 1) * 64],
                    Sp.v[:, h * 64:(h + 1) * 64], ALU.add),
                    reads=[("ps", bank)] + Sp.k, writes=Ssb.k)
        for h in range(16):
            p.op("pe", lambda t, h=h, nsc=nsc: t.matmul(
                c.ps[4][0:64, :], c.honehot[:, h * 64:(h + 1) * 64], nsc.v,
                start=True, stop=True),
                reads=nsc.k + [("s", "honehot")], writes=[("ps", 4)])
            ctx = scr(c, 32 + h // 2)
            pb = (h % 2) * 64
            p.op("dve", lambda v, ctx=ctx, pb=pb: v.tensor_tensor(
                ctx.v[pb:pb + 64, :], ctx.v[pb:pb + 64, :], c.ps[4][0:64, :], ALU.mult),
                reads=ctx.k + [("ps", 4)], writes=ctx.k)
        for dc in range(DCH):
            wb = c.wbuf[2 + dc % 2]
            p.op("pool", lambda g, dc=dc, wb=wb: g.dma_start(
                out=wb[:, :].rearrange("p (k m) -> p k m", k=8), in_=c.din["Wo_l"][dc]),
                writes=wbk(2 + dc % 2), dma=f"qw{2 + dc % 2}")

            def omm(t, wb=wb):
                ins = None
                for cc in range(DCH):
                    ctx = scr(c, 32 + cc)
                    ins = t.matmul(c.ps[5][:, :], wb[:, cc * 128:(cc + 1) * 128], ctx.v,
                                   start=(cc == 0), stop=(cc == DCH - 1))
                return ins
            p.op("pe", omm,
                 reads=wbk(2 + dc % 2) + [("scr", 32 + cc) for cc in range(DCH)],
                 writes=[("ps", 5)])
            otmp = scf(c, 5)
            p.op("act", lambda s, otmp=otmp, dc=dc: s.activation(
                otmp.v, c.ps[5][:, :], AF.Identity, bias=c.biases0["bo_pp"][:, dc:dc + 1], scale=1.0),
                reads=[("ps", 5), ("s", "bo_pp")], writes=otmp.k)
            xv = c.x[dc][:, tok]
            p.op("dve", lambda v, xv=xv, otmp=otmp: v.tensor_tensor(xv, xv, otmp.v, ALU.add),
                 reads=otmp.k + [("x", dc, sub)], writes=[("x", dc, sub)])



SER_SEGS = set()


def emit_attn1(p, c):
    def seg(name):
        p.scope_ser = name in SER_SEGS
    uBx = Buf(c.SCR[0:16, 0:2112].rearrange("p (a b) -> p a b", b=33),
              [("scr", g) for g in range(5)])
    hsT = Buf(c.SCR[0:16, 2560:4608], [("scr", g) for g in range(5, 9)])
    PowsBf = Buf(c.SCR[0:16, 4608:5120], [("scr", 9)])
    WCT = Buf(c.SCR[0:16, 5632:6656], [("scr", 11), ("scr", 12)])
    Pows = Buf(c.SCF[0:16, 0:512], [("scf", 0)])
    echn = Buf(c.SCF[0:16, 1024:1088], [("scf", 2)])
    hmid = Buf(c.SCF[0:16, 1536:1537], [("scf", 3)])
    seg("uproj")
    for sub in range(4):
        tok = slice(sub * 512, (sub + 1) * 512)
        for ec in range(DCH):
            wbi = c.wbuf[0 + ec % 2]
            wbg = c.wbuf[2 + ec % 2]
            p.op("pool", lambda g, ec=ec, wbi=wbi: g.dma_start(
                out=wbi[:, :].rearrange("p (k m) -> p k m", k=8), in_=c.din["Wi_l"][ec]),
                writes=wbk(ec % 2), dma=f"qw{ec % 2}")
            p.op("pool", lambda g, ec=ec, wbg=wbg: g.dma_start(
                out=wbg[:, :].rearrange("p (k m) -> p k m", k=8), in_=c.din["Wg_l"][ec]),
                writes=wbk(2 + ec % 2), dma=f"qw{2 + ec % 2}")

            def prmm(t, wb, bank, tok=tok):
                ins = None
                for kc in range(DCH):
                    ins = t.matmul(c.ps[bank][:, :], wb[:, kc * 128:(kc + 1) * 128],
                                   c.xn[kc][:, tok], start=(kc == 0), stop=(kc == DCH - 1))
                return ins
            bank = ec % 2
            p.op("pe", lambda t, wbi=wbi, bank=bank: prmm(t, wbi, bank),
                 reads=wbk(ec % 2) + [("xn", kc, sub) for kc in range(DCH)],
                 writes=[("ps", bank)])
            p.op("pe", lambda t, wbg=wbg, bank=bank: prmm(t, wbg, 2 + bank),
                 reads=wbk(2 + ec % 2) + [("xn", kc, sub) for kc in range(DCH)],
                 writes=[("ps", 2 + bank)])
            xi = scf(c, 5)
            gt = scf(c, 2)
            p.op("act", lambda s, xi=xi, bank=bank, ec=ec: s.activation(
                xi.v, c.ps[bank][:, :], AF.Identity, bias=c.biases1["bi_pp"][:, ec:ec + 1], scale=1.0),
                reads=[("ps", bank), ("s", "bi_pp")], writes=xi.k)
            p.op("act", lambda s, gt=gt, bank=bank, ec=ec: s.activation(
                gt.v, c.ps[2 + bank][:, :], AF.Sigmoid, bias=c.biases1["bg_pp"][:, ec:ec + 1], scale=1.0),
                reads=[("ps", 2 + bank), ("s", "bg_pp")], writes=gt.k)
            ub = scr(c, 16 + ec)
            p.op("dve", lambda v, ub=ub, xi=xi, gt=gt: v.tensor_tensor(ub.v, xi.v, gt.v, ALU.mult),
                 reads=xi.k + gt.k, writes=ub.k)

        def ubmm(t):
            ins = None
            for kc in range(DCH):
                ub = scr(c, 16 + kc)
                ins = t.matmul(c.ps[4][0:16, :], c.Bm[:, kc, :], ub.v,
                               start=(kc == 0), stop=(kc == DCH - 1))
            return ins
        p.op("pe", ubmm, reads=[("s", "Bm")] + [("scr", 16 + kc) for kc in range(DCH)],
             writes=[("ps", 4)])
        p.op("dve", lambda v, sub=sub: v.tensor_copy(
            uBx.v[:, sub * 16:(sub + 1) * 16, 1:33],
            c.ps[4][0:16, :].rearrange("p (a b) -> p a b", b=32)),
            reads=[("ps", 4)], writes=uBx.k)
    seg("powers")
    p.op("dve", lambda v: v.memset(uBx.v[:, :, 0:1], 0.0), writes=uBx.k)
    p.op("sp", lambda s: s.dma_start(out=Pows.v[:, 0:16], in_=c.din["AT"]),
         writes=Pows.k, dma="qsx")
    p.op("dve", lambda v: v.tensor_copy(PowsBf.v[:, 0:16], Pows.v[:, 0:16]),
         reads=Pows.k, writes=PowsBf.k)
    for l in range(2, 33):
        p.op("pe", lambda t, l=l: t.matmul(c.ps[5][0:16, 0:16], c.A_sb[:, :],
                                           Pows.v[:, (l - 2) * 16:(l - 1) * 16],
                                           start=True, stop=True),
             reads=Pows.k + [("s", "A_sb")], writes=[("ps", 5)])
        p.op("dve", lambda v, l=l: v.tensor_copy(Pows.v[:, (l - 1) * 16:l * 16], c.ps[5][0:16, 0:16]),
             reads=[("ps", 5)], writes=Pows.k)
        p.op("dve", lambda v, l=l: v.tensor_copy(PowsBf.v[:, (l - 1) * 16:l * 16], c.ps[5][0:16, 0:16]),
             reads=[("ps", 5)], writes=PowsBf.k)
    seg("lag")
    CG = [(0, 15), (15, 15), (30, 15), (45, 15), (60, 4)]
    for l in range(33):
        for cgi, (c0, ncg) in enumerate(CG):
            lhs = c.I16bf[:, :] if l == 0 else PowsBf.v[:, (l - 1) * 16:l * 16]
            lk = [("s", "I16bf")] if l == 0 else PowsBf.k
            ov = c.ps[cgi][0:16, 0:ncg * 33].rearrange("p (a b) -> p a b", b=33)[:, :, l:33]
            iv = uBx.v[:, c0:c0 + ncg, 0:33 - l]
            p.op("pe", lambda t, lhs=lhs, ov=ov, iv=iv, l=l: t.matmul(
                ov, lhs, iv, start=(l == 0), stop=False),
                reads=lk + uBx.k, writes=[("ps", cgi)])
    seg("scan")
    for cgi, (c0, ncg) in enumerate(CG):
        p.op("dve", lambda v, cgi=cgi, c0=c0, ncg=ncg: v.tensor_copy(
            echn.v[:, c0:c0 + ncg],
            c.ps[cgi][0:16, 0:ncg * 33].rearrange("p (a b) -> p a b", b=33)[:, :, 32]),
            reads=[("ps", cgi)], writes=echn.k)
    p.op("dve", lambda v: v.memset(c.svec[0][:, :], 0.0), writes=[("s", "sv0")])
    for cc in range(64):
        cur, nxt = c.svec[cc % 2], c.svec[(cc + 1) % 2]
        ck, nk = [("s", f"sv{cc % 2}")], [("s", f"sv{(cc + 1) % 2}")]

        def smm(t, cur=cur, cc=cc):
            t.matmul(c.ps[5][0:16, 0:1], Pows.v[:, 31 * 16:32 * 16], cur[:, :], start=True, stop=False)
            return t.matmul(c.ps[5][0:16, 0:1], c.I16f[:, :], echn.v[:, cc:cc + 1], start=False, stop=True)
        p.op("pe", smm, reads=Pows.k + echn.k + ck + [("s", "I16f")], writes=[("ps", 5)])
        p.op("dve", lambda v, nxt=nxt: v.tensor_copy(nxt[:, :], c.ps[5][0:16, 0:1]),
             reads=[("ps", 5)], writes=nk)
    p.op("sp", lambda s: s.dma_start(out=c.cinL1.ap(), in_=c.svec[0][:, :]),
         reads=[("s", "sv0")], writes=[("d", "cin1")], dma="qsx")
    p.op("pool", lambda g: g.collective_compute(
        "AllGather", ALU.bypass, replica_groups=[list(range(NCORE))],
        ins=[c.cinL1.ap().opt()], outs=[c.coutL1.ap().opt()]),
        reads=[("d", "cin1")], writes=[("d", "cout1")])
    p.op("sp", lambda s: s.dma_start(out=c.Gl1[:, :], in_=c.coutL1.ap()),
         reads=[("d", "cout1")], writes=[("s", "Gl1")], dma="qsx")
    p.op("pe", lambda t: t.matmul(c.ps[5][0:16, 0:1], c.selm[:, :], c.Gl1[:, 0:1],
                                  start=True, stop=True),
         reads=[("s", "Gl1"), ("s", "selm")], writes=[("ps", 5)])
    p.op("dve", lambda v: v.tensor_copy(hmid.v, c.ps[5][0:16, 0:1]),
         reads=[("ps", 5)], writes=hmid.k)
    p.op("dve", lambda v: v.tensor_copy(c.svec[0][:, :], hmid.v),
         reads=hmid.k, writes=[("s", "sv0")])
    for cc in range(64):
        cur, nxt = c.svec[cc % 2], c.svec[(cc + 1) % 2]
        ck, nk = [("s", f"sv{cc % 2}")], [("s", f"sv{(cc + 1) % 2}")]
        p.op("dve", lambda v, cur=cur, cc=cc: v.tensor_copy(uBx.v[:, cc, 0:1], cur[:, :]),
             reads=ck, writes=uBx.k)

        def smm2(t, cur=cur, cc=cc):
            t.matmul(c.ps[5][0:16, 0:1], Pows.v[:, 31 * 16:32 * 16], cur[:, :], start=True, stop=False)
            return t.matmul(c.ps[5][0:16, 0:1], c.I16f[:, :], echn.v[:, cc:cc + 1], start=False, stop=True)
        p.op("pe", smm2, reads=Pows.k + echn.k + ck + [("s", "I16f")], writes=[("ps", 5)])
        p.op("dve", lambda v, nxt=nxt: v.tensor_copy(nxt[:, :], c.ps[5][0:16, 0:1]),
             reads=[("ps", 5)], writes=nk)
    seg("impulse")
    for l in range(1, 33):
        for cgi, (c0, ncg) in enumerate(CG):
            ov = c.ps[cgi][0:16, 0:ncg * 33].rearrange("p (a b) -> p a b", b=33)[:, :, l:l + 1]
            iv = uBx.v[:, c0:c0 + ncg, 0:1]
            p.op("pe", lambda t, l=l, ov=ov, iv=iv: t.matmul(
                ov, PowsBf.v[:, (l - 1) * 16:l * 16], iv, start=False, stop=(l == 32)),
                reads=PowsBf.k + uBx.k, writes=[("ps", cgi)])
    for cgi, (c0, ncg) in enumerate(CG):
        p.op("dve", lambda v, cgi=cgi, c0=c0, ncg=ncg: v.tensor_copy(
            hsT.v.rearrange("p (a b) -> p a b", b=32)[:, c0:c0 + ncg, :],
            c.ps[cgi][0:16, 0:ncg * 33].rearrange("p (a b) -> p a b", b=33)[:, :, 1:33]),
            reads=[("ps", cgi)], writes=hsT.k)
    seg("out")
    for kc in range(DCH):
        wb = c.wbuf[kc % 2]
        p.op("pool", lambda g, kc=kc, wb=wb: g.dma_start(out=wb[:, :], in_=c.din["WoT_r"][kc]),
             writes=wbk(kc % 2), dma=f"qw{kc % 2}")
        for i2 in range(2):
            p.op("pe", lambda t, kc=kc, wb=wb, i2=i2: t.matmul(
                c.ps[i2][0:16, :], c.Cm[:, kc, :], wb[:, i2 * 512:(i2 + 1) * 512],
                start=(kc == 0), stop=(kc == DCH - 1)),
                reads=wbk(kc % 2) + [("s", "Cm")], writes=[("ps", i2)])
    for i2 in range(2):
        p.op("dve", lambda v, i2=i2: v.tensor_copy(WCT.v[:, i2 * 512:(i2 + 1) * 512], c.ps[i2][0:16, :]),
             reads=[("ps", i2)], writes=WCT.k)
    for ec in range(DCH):
        for j in range(4):
            bank = 2 + (ec * 4 + j) % 2
            p.op("pe", lambda t, ec=ec, j=j, bank=bank: t.matmul(
                c.ps[bank][:, :], WCT.v[:, ec * 128:(ec + 1) * 128],
                hsT.v[:, j * 512:(j + 1) * 512], start=True, stop=True),
                reads=WCT.k + hsT.k, writes=[("ps", bank)])
            ot = scf(c, 5)
            p.op("act", lambda s, ot=ot, bank=bank, ec=ec: s.activation(
                ot.v, c.ps[bank][:, :], AF.Identity, bias=c.biases1["bout_pp"][:, ec:ec + 1], scale=1.0),
                reads=[("ps", bank), ("s", "bout_pp")], writes=ot.k)
            xv = c.x[ec][:, j * 512:(j + 1) * 512]
            p.op("dve", lambda v, xv=xv, ot=ot: v.tensor_tensor(xv, xv, ot.v, ALU.add),
                 reads=ot.k + [("x", ec, j)], writes=[("x", ec, j)])
    p.scope_ser = False



DEBUG_DUMPS = []


def build(phases, host):
    p = Prog()
    c = alloc(p, phases, host)
    emit_load_common(p, c, phases)
    for ph in phases:
        emit_ln(p, c)
        if ph in ("ffn0", "ffn1"):
            emit_moe(p, c, ph)
        elif ph == "attn0":
            emit_attn0(p, c)
        elif ph == "attn1":
            emit_attn1(p, c)
    for name, getter, shape, keys in DEBUG_DUMPS:
        ap = getter(c)
        dt = BF16 if "bfloat" in str(ap.dtype) else F32
        dout = p.nc.dram_tensor("dbg_" + name, shape, dt, kind="ExternalOutput").ap()
        p.op("sp", lambda s, dout=dout, ap=ap: s.dma_start(out=dout, in_=ap),
             reads=keys, dma="qsx")
    emit_store(p, c)
    return p.finalize(), c.innames


def run_phases(host, PC, groups, xcur):
    built = {}
    for phases in groups:
        key = tuple(phases)
        if key not in built:
            built[key] = build(phases, host)
        nc, names = built[key]
        in_maps = []
        for cid in range(NCORE):
            m = {}
            for n in names:
                if n == "xin":
                    m[n] = xcur[cid]
                elif n == "selmat":
                    m[n] = PC[cid]["selmat"]
                else:
                    m[n] = host[n]
            in_maps.append(m)
        res = run_bass_kernel_spmd(nc, in_maps, core_ids=list(range(NCORE)))
        xcur = [np.ascontiguousarray(res.results[cid]["y"]) for cid in range(NCORE)]
    return xcur


def kernel(**inputs):
    host, PC = prep_host(inputs)
    groups = [ALL_PHASES] if FUSED else [(ph,) for ph in ALL_PHASES]
    xcur = [PC[cid]["xin"] for cid in range(NCORE)]
    xcur = run_phases(host, PC, groups, xcur)
    out = np.empty((4, 4096, 1024), np.float32)
    for cid in range(NCORE):
        b, hh = divmod(cid, 2)
        out[b, hh * T:(hh + 1) * T, :] = xcur[cid].T
    return out


# version 04
# speedup vs baseline: 1.0328x; 1.0077x over previous
import contextlib
import numpy as np
import ml_dtypes
import concourse.bass as bass
import concourse.mybir as mybir
from concourse.bass_utils import run_bass_kernel_spmd

F32 = mybir.dt.float32
BF16 = mybir.dt.bfloat16
AF = mybir.ActivationFunctionType
ALU = mybir.AluOpType
AX = mybir.AxisListType

NCORE = 8
T = 2048
D = 1024
DCH = 8
BIG_NEG = -1.0e30

ALL_PHASES = ("ffn0", "attn0", "ffn0", "ffn1", "attn1", "ffn1")
FUSED = True
SERIALIZE = False

ENG2SECTION = {"pe": "tensor", "act": "scalar", "dve": "vector",
               "pool": "gpsimd", "sp": "sync"}
DMA_ROLES = ("qi", "qw0", "qw1", "qw2", "qw3", "qc0", "qc1", "qsx")
QUEUES = tuple(f"{r}_{i}" for r in DMA_ROLES for i in range(2))


class Prog:
    def __init__(self):
        self.nc = bass.Bass("TRN2", target_bir_lowering=False)
        self.st = contextlib.ExitStack()
        self.ops = {e: [] for e in ENG2SECTION}
        self.count = {s: 0 for s in list(ENG2SECTION) + list(QUEUES)}
        self.writer = {}
        self.readers = {}
        self.marks = {}
        self.scope_ser = False
        self.role_rot = {}

    def sbuf(self, name, shape, dt):
        return self.st.enter_context(self.nc.sbuf_tensor(name, shape, dt))

    def psum(self, name, shape, dt=F32):
        return self.st.enter_context(self.nc.psum_tensor(name, shape, dt))

    def op(self, eng, emit, reads=(), writes=(), dma=None, sreads=(), fence=False):
        if dma is not None:
            i = self.role_rot.get(dma, 0)
            self.role_rot[dma] = i ^ 1
            dma = f"{dma}_{i}"
        stream = dma if dma else eng
        if dma is None and eng in ("dve", "act", "pool"):
            fence = True
        waits = {}
        if SERIALIZE or self.scope_ser:
            for s, cv in self.count.items():
                if s == eng or cv == 0:
                    continue
                w = self.marks.get((eng, s), 0)
                if cv > w:
                    waits[s] = cv
                    self.marks[(eng, s)] = cv

        def need(pstream, cnt, force=False):
            if cnt <= 0:
                return
            if pstream == eng and dma is None and not force:
                return
            if pstream in QUEUES:
                cnt = self.count[pstream]
            w = self.marks.get((eng, pstream), 0)
            if cnt > w:
                waits[pstream] = max(waits.get(pstream, 0), cnt)
                self.marks[(eng, pstream)] = cnt

        for k in reads:
            lw = self.writer.get(k)
            if lw:
                need(lw[0], lw[1], force=lw[2])
        for k in sreads:
            lw = self.writer.get(k)
            if lw:
                need(lw[0], lw[1], force=True)
        for k in writes:
            lw = self.writer.get(k)
            if lw:
                need(lw[0], lw[1], force=lw[2])
            for r in self.readers.get(k, ()):
                need(*r)
        if dma is not None and self.count[stream] > 0:
            w = self.marks.get((eng, stream), 0)
            if self.count[stream] > w:
                waits[stream] = max(waits.get(stream, 0), self.count[stream])
                self.marks[(eng, stream)] = self.count[stream]
        self.count[stream] += 16 if dma else 1
        cnt = self.count[stream]
        for k in list(reads) + list(sreads):
            self.readers.setdefault(k, []).append((stream, cnt))
        for k in writes:
            self.writer[k] = (stream, cnt, fence)
            self.readers[k] = []
        self.ops[eng].append((sorted(waits.items()), emit, stream, 16 if dma else 1))

    def barrier_waits(self, eng, streams=None):
        waits = []
        for s in (streams or list(self.count)):
            if s == eng or self.count[s] == 0:
                continue
            w = self.marks.get((eng, s), 0)
            if self.count[s] > w:
                waits.append((s, self.count[s]))
                self.marks[(eng, s)] = self.count[s]
        if waits:
            self.ops[eng].append((waits, None, None, 0))

    def finalize(self):
        used = [s for s in self.count if self.count[s] > 0]
        sems = {s: self.st.enter_context(self.nc.semaphore(s)) for s in used}
        blk = self.st.enter_context(self.nc.Block())
        for eng, section_name in ENG2SECTION.items():
            if not self.ops[eng]:
                continue
            section = getattr(blk, section_name)
            eng_ops = self.ops[eng]

            def body(engobj, eng_ops=eng_ops):
                for waits, emit, stream, incv in eng_ops:
                    for s, cval in waits:
                        engobj.wait_ge(sems[s], cval)
                    if emit is not None:
                        ins = emit(engobj)
                        ins.then_inc(sems[stream], incv)
            section(body)
        self.st.close()
        return self.nc


class Buf:
    def __init__(self, v, keys):
        self.v = v
        self.k = list(keys)



def pretile_lhsT(W):
    K, M = W.shape
    return np.ascontiguousarray(
        W.reshape(K // 128, 128, M // 128, 128).transpose(2, 1, 0, 3))


def pretile_rhs(W):
    K, N = W.shape
    return np.ascontiguousarray(W.reshape(K // 128, 128, N))


def resident_rhs(W):
    K, n = W.shape
    return np.ascontiguousarray(W.reshape(K // 128, 128, n).transpose(1, 0, 2))


def bias_pp(b):
    return np.ascontiguousarray(b.reshape(-1, 128).T)


def prep_host(inputs):
    g = {k: np.asarray(v) for k, v in inputs.items()}
    H = {}
    w1s = [g["L0_ffn_ew1"][e] for e in range(8)] + [g["L0_ffn_sw1"]]
    w2s = [g["L0_ffn_ew2"][e] for e in range(8)] + [g["L0_ffn_sw2"]]
    w3s = [g["L0_ffn_ew3"][e] for e in range(8)] + [g["L0_ffn_sw3"]]
    H["F0A"] = np.stack([pretile_lhsT(w) for w in w1s])
    H["F0B"] = np.stack([pretile_lhsT(w) for w in w2s])

    def grp_split(arr, ngrp):
        dc, pp, kc, m = arr.shape
        return np.ascontiguousarray(
            arr.reshape(dc, pp, ngrp, kc // ngrp, m).transpose(2, 0, 1, 3, 4))
    H["F0C"] = np.stack([grp_split(pretile_lhsT(w), 1) for w in w3s])
    H["gw0"] = resident_rhs(g["L0_ffn_gate_w"])
    H["F1A"] = np.stack([pretile_lhsT(g["L1_ffn_W1"][e].T) for e in range(8)])
    H["F1B"] = np.stack([pretile_lhsT(g["L1_ffn_W3"][e].T) for e in range(8)])
    H["F1C"] = np.stack([grp_split(pretile_lhsT(g["L1_ffn_W2"][e].T), 2)
                         for e in range(8)])
    H["gw1"] = resident_rhs(np.ascontiguousarray(g["L1_ffn_Wr"].T))
    Wqk = g["L0_attn_Wqk"]
    H["Wq_l"] = pretile_lhsT(np.ascontiguousarray(Wqk[:, :1024]))
    H["Wk_l"] = pretile_lhsT(np.ascontiguousarray(Wqk[:, 1024:]))
    H["Wk_r"] = pretile_rhs(np.ascontiguousarray(Wqk[:, 1024:]))
    H["Wv_r"] = pretile_rhs(g["L0_attn_Wv"])
    H["Wo_l"] = pretile_lhsT(g["L0_attn_Wo"])
    H["Wn_r"] = resident_rhs(g["L0_attn_Wn"])
    H["bq_pp"] = bias_pp(g["L0_attn_bqk"][:1024])
    H["bk_pp"] = bias_pp(g["L0_attn_bqk"][1024:])
    H["bk_row"] = np.ascontiguousarray(g["L0_attn_bqk"][1024:]).reshape(1, 1024)
    H["bv_row"] = np.ascontiguousarray(g["L0_attn_bv"]).reshape(1, 1024)
    H["bn_pp"] = np.ascontiguousarray(g["L0_attn_bn"]).reshape(16, 1)
    H["bo_pp"] = bias_pp(g["L0_attn_bo"])
    H["Wi_l"] = pretile_lhsT(np.ascontiguousarray(g["L1_attn_W_in"][1024:, :].T))
    H["Wg_l"] = pretile_lhsT(np.ascontiguousarray(g["L1_attn_W_gate"].T))
    H["Bm_r"] = resident_rhs(np.ascontiguousarray(g["L1_attn_Bm"].T))
    H["Cm_r"] = resident_rhs(g["L1_attn_Cm"])
    H["WoT_r"] = pretile_rhs(np.ascontiguousarray(g["L1_attn_W_out"].T))
    H["A_lhs"] = np.ascontiguousarray(g["L1_attn_A"])
    H["AT"] = np.ascontiguousarray(g["L1_attn_A"].T)
    H["bi_pp"] = bias_pp(g["L1_attn_b_in"][1024:])
    H["bg_pp"] = bias_pp(g["L1_attn_b_gate"])
    H["bout_pp"] = bias_pp(g["L1_attn_b_out"])
    H["I128"] = np.eye(128, dtype=np.float32)
    H["I16bf"] = np.eye(16, dtype=np.float32).astype(ml_dtypes.bfloat16)
    H["I16f"] = np.eye(16, dtype=np.float32)
    H["ones_col_bf"] = np.ones((128, 1), dtype=np.float32).astype(ml_dtypes.bfloat16)
    H["ones_row"] = np.ones((1, 128), dtype=np.float32)
    H["ones_row_bf"] = np.ones((1, 128), dtype=np.float32).astype(ml_dtypes.bfloat16)
    H["triuT"] = np.triu(np.ones((128, 128), dtype=np.float32))
    eo = np.zeros((8, 8, 128), dtype=np.float32)
    for e in range(8):
        eo[e, e, :] = 1.0
    H["eoneh"] = np.ascontiguousarray(eo.transpose(1, 0, 2).reshape(8, 8 * 128))
    ho = np.zeros((16, 16, 64), dtype=np.float32)
    for h in range(16):
        ho[h, h, :] = 1.0
    H["honehot"] = np.ascontiguousarray(ho.transpose(1, 0, 2).reshape(16, 16 * 64)).astype(ml_dtypes.bfloat16)
    PC = []
    x = g["L0_attn_x"]
    for cid in range(NCORE):
        b, hh = divmod(cid, 2)
        d = {"xin": np.ascontiguousarray(x[b, hh * T:(hh + 1) * T, :].T)}
        sel = np.zeros((128, 16), dtype=np.float32)
        if hh == 1:
            for i in range(16):
                sel[(cid - 1) * 16 + i, i] = 1.0
        d["selmat"] = sel
        PC.append(d)
    return H, PC



PHASE_WEIGHTS = {
    "ffn0": ["F0A", "F0B", "F0C", "gw0"],
    "ffn1": ["F1A", "F1B", "F1C", "gw1"],
    "attn0": ["Wq_l", "Wk_l", "Wk_r", "Wv_r", "Wo_l", "Wn_r", "bq_pp", "bk_pp",
              "bk_row", "bv_row", "bn_pp", "bo_pp", "selmat", "honehot"],
    "attn1": ["Wi_l", "Wg_l", "Bm_r", "Cm_r", "WoT_r", "A_lhs", "AT", "I16f",
              "bi_pp", "bg_pp", "bout_pp", "selmat"],
}
CONST_NAMES = ["I128", "I16bf", "ones_col_bf", "ones_row", "ones_row_bf", "triuT", "eoneh"]

NSCR = 49
NSCF = 7



def wbk(unit, half=None):
    if half is None:
        return [("wb", unit, 0), ("wb", unit, 1)]
    return [("wb", unit, half)]

class Ctx:
    pass


def alloc(p, phases, host):
    c = Ctx()
    nc = p.nc
    c.din = {}
    names = set(CONST_NAMES)
    for ph in phases:
        names.update(PHASE_WEIGHTS[ph])
    names.add("xin")
    for n in sorted(names):
        arr = host.get(n)
        if arr is None:
            if n == "xin":
                arr = np.zeros((D, T), np.float32)
            elif n == "selmat":
                arr = np.zeros((128, 16), np.float32)
        dt = BF16 if arr.dtype == ml_dtypes.bfloat16 else F32
        c.din[n] = nc.dram_tensor(n, list(arr.shape), dt, kind="ExternalInput").ap()
    c.yout = nc.dram_tensor("y", [D, T], F32, kind="ExternalOutput").ap()
    c.innames = names

    c.x = [p.sbuf(f"x{i}", [128, T], F32) for i in range(DCH)]
    c.xn = [p.sbuf(f"xn{i}", [128, T], BF16) for i in range(DCH)]
    c.SCR = p.sbuf("SCR", [128, NSCR * 512], BF16)
    c.SCF = p.sbuf("SCF", [128, NSCF * 512], F32)
    c.wbuf = [p.sbuf(f"wbuf{i}", [128, 1024], BF16) for i in range(4)]
    c.w2buf = [p.sbuf(f"w2buf{i}", [128, 2048], BF16) for i in range(2)]
    c.ps = [p.psum(f"ps{i}", [128, 512]) for i in range(8)]
    c.wb_sb = p.sbuf("wb_sb", [128, 1024], F32)
    c.w_tm = p.sbuf("w_tm", [128, 128], F32)
    c.w_fm = p.sbuf("w_fm", [8, T], F32)
    c.stat = p.sbuf("stat", [1, 1536], F32)
    c.rst = p.sbuf("rst", [128, 64], F32)
    c.rst2 = p.sbuf("rst2", [128, 64], F32)
    c.gS = p.sbuf("gS", [128, 8], F32)
    c.I128 = p.sbuf("cI128", [128, 128], F32)
    c.I16bf = p.sbuf("cI16", [16, 16], BF16)
    c.ones_col_bf = p.sbuf("cOnesCol", [128, 1], BF16)
    c.ones_row = p.sbuf("cOnesRow", [1, 128], F32)
    c.ones_row_bf = p.sbuf("cOnesRowBf", [1, 128], BF16)
    c.triuT = p.sbuf("cTriu", [128, 128], F32)
    c.eoneh = p.sbuf("cEoneh", [8, 8 * 128], F32)
    c.gw = {}
    if "ffn0" in phases:
        c.gw["ffn0"] = p.sbuf("gw0sb", [128, 8, 8], BF16)
    if "ffn1" in phases:
        c.gw["ffn1"] = p.sbuf("gw1sb", [128, 8, 8], BF16)
    c.dwscr = nc.dram_tensor("dwscr", [16, T], F32).ap()
    need_attn = ("attn0" in phases) or ("attn1" in phases)
    if need_attn:
        c.selm = p.sbuf("selm", [128, 16], F32)
    if "attn0" in phases:
        c.biases0 = {n: p.sbuf("sb_" + n, list(host[n].shape), F32)
                     for n in ("bq_pp", "bk_pp", "bo_pp", "bn_pp")}
        c.negbn = p.sbuf("negbn", [16, 1], F32)
        c.Wn = p.sbuf("WnSb", [128, 8, 16], BF16)
        c.bk_bf = p.sbuf("bk_bf", [1, 1024], BF16)
        c.bv_bf = p.sbuf("bv_bf", [1, 1024], BF16)
        c.honehot = p.sbuf("cHoneh", [16, 16 * 64], BF16)
        c.cinL0 = nc.dram_tensor("cinL0", [64, 1024], F32)
        c.coutL0 = nc.dram_tensor("coutL0", [512, 1024], F32)
        c.dSp = nc.dram_tensor("dSp", [64, 1024], F32).ap()
    if "attn1" in phases:
        c.biases1 = {n: p.sbuf("sb_" + n, list(host[n].shape), F32)
                     for n in ("bi_pp", "bg_pp", "bout_pp")}
        c.Bm = p.sbuf("BmSb", [128, 8, 16], BF16)
        c.Cm = p.sbuf("CmSb", [128, 8, 16], BF16)
        c.A_sb = p.sbuf("A_sb", [16, 16], F32)
        c.I16f = p.sbuf("sI16f", [16, 16], F32)
        c.svec = [p.sbuf(f"svec{i}", [16, 1], F32) for i in range(2)]
        c.cinL1 = nc.dram_tensor("cinL1", [16, 1], F32)
        c.coutL1 = nc.dram_tensor("coutL1", [128, 1], F32)
        c.Gl1 = p.sbuf("Gl1", [128, 1], F32)
    return c


def scr(c, g, ngr=1, parts=128, p0=0):
    v = c.SCR[p0:p0 + parts, g * 512:(g + ngr) * 512]
    return Buf(v, [("scr", g + k) for k in range(ngr)])


def scf(c, g, ngr=1, parts=128, p0=0):
    v = c.SCF[p0:p0 + parts, g * 512:(g + ngr) * 512]
    return Buf(v, [("scf", g + k) for k in range(ngr)])


def emit_load_common(p, c, phases):
    for i in range(DCH):
        p.op("sp", lambda s, i=i: s.dma_start(out=c.x[i][:, :], in_=c.din["xin"][i * 128:(i + 1) * 128, :]),
             writes=[("x", i, j) for j in range(4)], dma="qsx")
    loads = [(c.I128, "I128"), (c.I16bf, "I16bf"), (c.ones_col_bf, "ones_col_bf"),
             (c.ones_row, "ones_row"), (c.ones_row_bf, "ones_row_bf"), (c.triuT, "triuT"),
             (c.eoneh, "eoneh")]
    for t, n in loads:
        p.op("sp", lambda s, t=t, n=n: s.dma_start(out=t[:, :], in_=c.din[n]),
             writes=[("s", n)], dma="qsx")
    for ph, t in c.gw.items():
        n = "gw0" if ph == "ffn0" else "gw1"
        p.op("pool", lambda g, t=t, n=n: g.dma_start(out=t[:, :, :], in_=c.din[n]),
             writes=[("s", "gw_" + ph)], dma="qi")
    if hasattr(c, "selm"):
        p.op("sp", lambda s: s.dma_start(out=c.selm[:, :], in_=c.din["selmat"]),
             writes=[("s", "selm")], dma="qsx")
    if "attn0" in phases:
        for n, t in c.biases0.items():
            p.op("sp", lambda s, t=t, n=n: s.dma_start(out=t[:, :], in_=c.din[n]),
                 writes=[("s", n)], dma="qsx")
        p.op("pool", lambda g: g.dma_start(out=c.Wn[:, :, :], in_=c.din["Wn_r"]),
             writes=[("s", "Wn")], dma="qi")
        p.op("pool", lambda g: g.dma_start(out=c.bk_bf[:, :], in_=c.din["bk_row"]),
             writes=[("s", "bk_bf")], dma="qi")
        p.op("pool", lambda g: g.dma_start(out=c.bv_bf[:, :], in_=c.din["bv_row"]),
             writes=[("s", "bv_bf")], dma="qi")
        p.op("sp", lambda s: s.dma_start(out=c.honehot[:, :], in_=c.din["honehot"]),
             writes=[("s", "honehot")], dma="qi")
        p.op("dve", lambda v: v.tensor_scalar(c.negbn[:, :], c.biases0["bn_pp"][:, :], -1.0, None, ALU.mult),
             reads=[("s", "bn_pp")], writes=[("s", "negbn")])
    if "attn1" in phases:
        for n, t in c.biases1.items():
            p.op("sp", lambda s, t=t, n=n: s.dma_start(out=t[:, :], in_=c.din[n]),
                 writes=[("s", n)], dma="qsx")
        p.op("pool", lambda g: g.dma_start(out=c.Bm[:, :, :], in_=c.din["Bm_r"]),
             writes=[("s", "Bm")], dma="qi")
        p.op("pool", lambda g: g.dma_start(out=c.Cm[:, :, :], in_=c.din["Cm_r"]),
             writes=[("s", "Cm")], dma="qi")
        p.op("sp", lambda s: s.dma_start(out=c.A_sb[:, :], in_=c.din["A_lhs"]),
             writes=[("s", "A_sb")], dma="qsx")
        p.op("sp", lambda s: s.dma_start(out=c.I16f[:, :], in_=c.din["I16f"]),
             writes=[("s", "I16f")], dma="qsx")


def emit_store(p, c):
    for i in range(DCH):
        p.op("sp", lambda s, i=i: s.dma_start(out=c.yout[i * 128:(i + 1) * 128, :], in_=c.x[i][:, :]),
             reads=[("x", i, j) for j in range(4)], dma="qsx")
    p.barrier_waits("sp")



def emit_ln(p, c):
    for j in range(4):
        sl = slice(j * 512, (j + 1) * 512)
        for i in range(DCH):
            xb = scr(c, 40 + i % 2)
            xq = scr(c, 42 + i % 2)
            p.op("dve", lambda v, i=i, xb=xb, sl=sl: v.tensor_copy(xb.v, c.x[i][:, sl]),
                 reads=[("x", i, j)], writes=xb.k)
            p.op("pe", lambda t, xb=xb, i=i: t.matmul(
                c.ps[6][0:1, :], c.ones_col_bf[:, :], xb.v,
                start=(i == 0), stop=(i == DCH - 1)),
                reads=xb.k + [("s", "ones_col_bf")], writes=[("ps", 6)])
            p.op("act", lambda s, i=i, xq=xq, sl=sl: s.activation(xq.v, c.x[i][:, sl], AF.Square, bias=0.0, scale=1.0),
                 reads=[("x", i, j)], writes=xq.k)
            p.op("pe", lambda t, xq=xq, i=i: t.matmul(
                c.ps[7][0:1, :], c.ones_col_bf[:, :], xq.v,
                start=(i == 0), stop=(i == DCH - 1)),
                reads=xq.k + [("s", "ones_col_bf")], writes=[("ps", 7)])
        m_row = c.stat[0:1, 0:512]
        rows = Buf(None, [("s", "lnrows")])
        t0 = c.stat[0:1, 512:1024]
        t1 = c.stat[0:1, 1024:1536]
        p.op("dve", lambda v, m_row=m_row: v.tensor_scalar(m_row, c.ps[6][0:1, :], 1.0 / D, None, ALU.mult),
             reads=[("ps", 6)], writes=[("s", "lnm")])
        p.op("dve", lambda v: v.tensor_scalar(t0, c.ps[7][0:1, :], 1.0 / D, None, ALU.mult),
             reads=[("ps", 7)], writes=rows.k)
        p.op("dve", lambda v, m_row=m_row: v.tensor_tensor(t1, m_row, m_row, ALU.mult),
             reads=[("s", "lnm")], writes=rows.k)
        p.op("dve", lambda v: v.tensor_tensor(t0, t0, t1, ALU.subtract),
             reads=rows.k, writes=rows.k)
        p.op("dve", lambda v: v.tensor_scalar(t0, t0, 1e-5, None, ALU.add),
             reads=rows.k, writes=rows.k)
        p.op("act", lambda s: s.activation(t1, t0, AF.Sqrt, bias=0.0, scale=1.0),
             reads=rows.k, writes=rows.k)
        p.op("dve", lambda v: v.reciprocal(t0, t1), reads=rows.k, writes=rows.k, fence=True)
        p.op("dve", lambda v, m_row=m_row: v.tensor_tensor(t1, m_row, t0, ALU.mult),
             reads=[("s", "lnm")] + rows.k, writes=rows.k)
        p.op("dve", lambda v: v.tensor_scalar(t1, t1, -1.0, None, ALU.mult),
             reads=rows.k, writes=rows.k)
        p.op("pe", lambda t: t.matmul(c.ps[4][:, :], c.ones_row[:, :], t0, start=True, stop=True),
             reads=rows.k + [("s", "ones_row")], writes=[("ps", 4)], fence=True)
        p.op("pe", lambda t: t.matmul(c.ps[5][:, :], c.ones_row[:, :], t1, start=True, stop=True),
             reads=rows.k + [("s", "ones_row")], writes=[("ps", 5)])
        tt = scf(c, 4)
        for i in range(DCH):
            p.op("dve", lambda v, i=i, sl=sl, tt=tt: v.tensor_tensor(tt.v, c.x[i][:, sl], c.ps[4][:, :], ALU.mult),
                 reads=[("x", i, j), ("ps", 4)], writes=tt.k)
            p.op("dve", lambda v, i=i, sl=sl, tt=tt: v.tensor_tensor(c.xn[i][:, sl], tt.v, c.ps[5][:, :], ALU.add),
                 reads=tt.k + [("ps", 5)], writes=[("xn", i, j)])



def emit_moe(p, c, kind):
    shared = kind == "ffn0"
    nexp = 8
    H = 2048 if kind == "ffn0" else 4096
    NGRP = (H // 128) // 16
    FA, FB, FC = {"ffn0": ("F0A", "F0B", "F0C"), "ffn1": ("F1A", "F1B", "F1C")}[kind]
    gw = c.gw[kind]

    for tc in range(16):
        j = tc // 4

        def g_mm(t, tc=tc):
            ins = None
            for kc in range(DCH):
                ins = t.matmul(c.ps[6][:, tc * 8:(tc + 1) * 8],
                               c.xn[kc][:, tc * 128:(tc + 1) * 128],
                               gw[:, kc, :], start=(kc == 0), stop=(kc == DCH - 1))
            return ins
        p.op("pe", g_mm, reads=[("xn", kc, j) for kc in range(DCH)] + [("s", "gw_" + kind)],
             writes=[("ps", 6)])
    gA = c.SCF[:, 3 * 512:3 * 512 + 128]
    gv = gA.rearrange("p (t e) -> p t e", e=8)
    m1 = c.rst[:, 0:16]
    m2 = c.rst2[:, 0:16]
    e1 = c.SCF[:, 2 * 512 + 0:2 * 512 + 128]
    e2 = c.SCF[:, 2 * 512 + 128:2 * 512 + 256]
    pb = c.SCF[:, 2 * 512 + 256:2 * 512 + 384]
    tb = c.SCF[:, 2 * 512 + 384:2 * 512 + 512]
    p.op("dve", lambda v: v.tensor_copy(gA, c.ps[6][:, 0:128]),
         reads=[("ps", 6)], writes=[("scf", 3)])
    if kind == "ffn0":
        p.op("dve", lambda v: v.reduce_max(m1, gv, axis=AX.X),
             reads=[("scf", 3)], writes=[("s", "m1")], fence=True)
        p.op("dve", lambda v: v.tensor_tensor(
            e1.rearrange("p (t e) -> p t e", e=8), gv,
            m1.unsqueeze(2).broadcast_to([128, 16, 8]), ALU.subtract),
            reads=[("scf", 3), ("s", "m1")], writes=[("scf", 2)])
        p.op("dve", lambda v: v.tensor_scalar(e1, e1, 0.0, None, ALU.is_ge),
             reads=[("scf", 2)], writes=[("scf", 2)])
        p.op("dve", lambda v: v.tensor_scalar(tb, e1, BIG_NEG, None, ALU.mult),
             reads=[("scf", 2)], writes=[("scf", 2)])
        p.op("dve", lambda v: v.tensor_tensor(pb, gA, tb, ALU.add),
             reads=[("scf", 3), ("scf", 2)], writes=[("scf", 2)])
        p.op("dve", lambda v: v.reduce_max(m2, pb.rearrange("p (t e) -> p t e", e=8), axis=AX.X),
             reads=[("scf", 2)], writes=[("s", "m2")], fence=True)
        p.op("dve", lambda v: v.tensor_tensor(
            e2.rearrange("p (t e) -> p t e", e=8),
            pb.rearrange("p (t e) -> p t e", e=8),
            m2.unsqueeze(2).broadcast_to([128, 16, 8]), ALU.subtract),
            reads=[("scf", 2), ("s", "m2")], writes=[("scf", 2)])
        p.op("dve", lambda v: v.tensor_scalar(e2, e2, 0.0, None, ALU.is_ge),
             reads=[("scf", 2)], writes=[("scf", 2)])
        p.op("dve", lambda v: v.tensor_tensor(c.rst[:, 16:32], m1, m2, ALU.subtract),
             reads=[("s", "m1"), ("s", "m2")], writes=[("s", "dd")])
        p.op("act", lambda s: s.activation(c.rst2[:, 16:32], c.rst[:, 16:32], AF.Sigmoid, bias=0.0, scale=1.0),
             reads=[("s", "dd")], writes=[("s", "p1")])
        p.op("act", lambda s: s.activation(c.rst[:, 32:48], c.rst[:, 16:32], AF.Sigmoid, bias=0.0, scale=-1.0),
             reads=[("s", "dd")], writes=[("s", "p2")])
        p.op("dve", lambda v: v.tensor_scalar(c.rst2[:, 16:32], c.rst2[:, 16:32], 0.5, None, ALU.mult),
             reads=[("s", "p1")], writes=[("s", "p1")])
        p.op("dve", lambda v: v.tensor_scalar(c.rst[:, 32:48], c.rst[:, 32:48], 0.5, None, ALU.mult),
             reads=[("s", "p2")], writes=[("s", "p2")])
        p.op("dve", lambda v: v.tensor_tensor(
            e1.rearrange("p (t e) -> p t e", e=8),
            e1.rearrange("p (t e) -> p t e", e=8),
            c.rst2[:, 16:32].unsqueeze(2).broadcast_to([128, 16, 8]), ALU.mult),
            reads=[("scf", 2), ("s", "p1")], writes=[("scf", 2)])
        p.op("dve", lambda v: v.tensor_tensor(
            e2.rearrange("p (t e) -> p t e", e=8),
            e2.rearrange("p (t e) -> p t e", e=8),
            c.rst[:, 32:48].unsqueeze(2).broadcast_to([128, 16, 8]), ALU.mult),
            reads=[("scf", 2), ("s", "p2")], writes=[("scf", 2)])
        p.op("dve", lambda v: v.tensor_tensor(c.w_tm[:, :], e1, e2, ALU.add),
             reads=[("scf", 2), ("scf", 2)], writes=[("s", "wtm")])
    else:
        p.op("dve", lambda v: v.reduce_max(m1, gv, axis=AX.X),
             reads=[("scf", 3)], writes=[("s", "m1")], fence=True)
        p.op("dve", lambda v: v.tensor_tensor(
            e1.rearrange("p (t e) -> p t e", e=8), gv,
            m1.unsqueeze(2).broadcast_to([128, 16, 8]), ALU.subtract),
            reads=[("scf", 3), ("s", "m1")], writes=[("scf", 2)])
        p.op("act", lambda s: s.activation(e1, e1, AF.Exp, bias=0.0, scale=1.0),
             reads=[("scf", 2)], writes=[("scf", 2)])
        p.op("dve", lambda v: v.reduce_sum(m2, e1.rearrange("p (t e) -> p t e", e=8), axis=AX.X),
             reads=[("scf", 2)], writes=[("s", "m2")], fence=True)
        p.op("dve", lambda v: v.reciprocal(c.rst[:, 16:32], m2),
             reads=[("s", "m2")], writes=[("s", "rs")], fence=True)
        p.op("dve", lambda v: v.tensor_tensor(
            pb.rearrange("p (t e) -> p t e", e=8),
            e1.rearrange("p (t e) -> p t e", e=8),
            c.rst[:, 16:32].unsqueeze(2).broadcast_to([128, 16, 8]), ALU.mult),
            reads=[("scf", 2), ("s", "rs")], writes=[("scf", 2)])
        pv = pb.rearrange("p (t e) -> p t e", e=8)
        p.op("dve", lambda v: v.reduce_max(c.rst[:, 32:48], pv[:, :, 1:8], axis=AX.X),
             reads=[("scf", 2)], writes=[("s", "mr")], fence=True)
        p.op("dve", lambda v: v.tensor_tensor(
            e2.rearrange("p (t e) -> p t e", e=8)[:, :, 1:8], pv[:, :, 1:8],
            c.rst[:, 32:48].unsqueeze(2).broadcast_to([128, 16, 7]), ALU.subtract),
            reads=[("scf", 2), ("s", "mr")], writes=[("scf", 2)])
        p.op("dve", lambda v: v.tensor_scalar(e2, e2, 0.0, None, ALU.is_ge),
             reads=[("scf", 2)], writes=[("scf", 2)])
        p.op("dve", lambda v: v.tensor_tensor(c.rst2[:, 16:32], pv[:, :, 0], c.rst[:, 32:48], ALU.add),
             reads=[("scf", 2), ("s", "mr")], writes=[("s", "den")])
        p.op("dve", lambda v: v.tensor_scalar(c.rst2[:, 16:32], c.rst2[:, 16:32], 1e-9, None, ALU.max),
             reads=[("s", "den")], writes=[("s", "den")])
        p.op("dve", lambda v: v.reciprocal(c.rst2[:, 32:48], c.rst2[:, 16:32]),
             reads=[("s", "den")], writes=[("s", "rden")], fence=True)
        p.op("dve", lambda v: v.tensor_scalar(c.rst2[:, 32:48], c.rst2[:, 32:48], 0.5, None, ALU.mult),
             reads=[("s", "rden")], writes=[("s", "rden")])
        p.op("dve", lambda v: v.tensor_tensor(c.rst[:, 48:64], pv[:, :, 0], c.rst2[:, 32:48], ALU.mult),
             reads=[("scf", 2), ("s", "rden")], writes=[("s", "w0")])
        p.op("dve", lambda v: v.tensor_copy(
            c.w_tm[:, :].rearrange("p (t e) -> p t e", e=8)[:, :, 0],
            c.rst[:, 48:64]),
            reads=[("s", "w0")], writes=[("s", "wtm")])
        p.op("dve", lambda v: v.tensor_tensor(c.rst2[:, 48:64], c.rst[:, 32:48], c.rst2[:, 32:48], ALU.mult),
             reads=[("s", "mr"), ("s", "rden")], writes=[("s", "wr2")])
        p.op("dve", lambda v: v.tensor_tensor(
            c.w_tm[:, :].rearrange("p (t e) -> p t e", e=8)[:, :, 1:8],
            e2.rearrange("p (t e) -> p t e", e=8)[:, :, 1:8],
            c.rst2[:, 48:64].unsqueeze(2).broadcast_to([128, 16, 7]), ALU.mult),
            reads=[("scf", 2), ("s", "wr2")], writes=[("s", "wtm")])
    for tc in range(16):
        p.op("pe", lambda t, tc=tc: t.transpose(
            c.ps[7][0:8, (tc % 4) * 128:(tc % 4 + 1) * 128],
            c.w_tm[:, tc * 8:(tc + 1) * 8], c.I128[:, :]),
            reads=[("s", "wtm"), ("s", "I128")], writes=[("ps", 7)])
        if tc % 4 == 3:
            grp = tc // 4
            p.op("dve", lambda v, grp=grp: v.tensor_copy(
                c.w_fm[:, grp * 512:(grp + 1) * 512], c.ps[7][0:8, :]),
                reads=[("ps", 7)], writes=[("s", "wfm", grp)])

    for half in range(2):
        order = ([nexp] + list(range(nexp))) if shared else list(range(nexp))
        for e in order:
            is_shared = shared and e == nexp
            if not is_shared:
                for j2 in range(2):
                    j = half * 2 + j2
                    p.op("pe", lambda t, e=e, j=j: t.matmul(
                        c.ps[6][:, :], c.eoneh[:, e * 128:(e + 1) * 128],
                        c.w_fm[:, j * 512:(j + 1) * 512], start=True, stop=True),
                        reads=[("s", "wfm", j), ("s", "eoneh")], writes=[("ps", 6)])
                    p.op("dve", lambda v, j2=j2: v.tensor_copy(c.wb_sb[:, j2 * 512:(j2 + 1) * 512], c.ps[6][:, :]),
                         reads=[("ps", 6)], writes=[("s", "wbsb", j2)])
            for grp in range(NGRP):
                for hc16 in range(16):
                    hc = grp * 16 + hc16
                    bA = c.wbuf[(hc16 % 2) * 2]
                    bB = c.wbuf[(hc16 % 2) * 2 + 1]
                    p.op("pool", lambda g, e=e, hc=hc, bA=bA: g.dma_start(
                        out=bA[:, :].rearrange("p (k m) -> p k m", k=8), in_=c.din[FA][e, hc]),
                        writes=wbk((hc16 % 2) * 2), dma=f"qw{(hc16 % 2) * 2}")
                    p.op("pool", lambda g, e=e, hc=hc, bB=bB: g.dma_start(
                        out=bB[:, :].rearrange("p (k m) -> p k m", k=8), in_=c.din[FB][e, hc]),
                        writes=wbk((hc16 % 2) * 2 + 1), dma=f"qw{(hc16 % 2) * 2 + 1}")
                    for j2 in range(2):
                        j = half * 2 + j2
                        bkA = (hc16 * 2 + j2) % 2
                        bkB = 2 + (hc16 * 2 + j2) % 2

                        def hmm(t, buf, bank, j):
                            ins = None
                            for kc in range(DCH):
                                ins = t.matmul(c.ps[bank][:, :], buf[:, kc * 128:(kc + 1) * 128],
                                               c.xn[kc][:, j * 512:(j + 1) * 512],
                                               start=(kc == 0), stop=(kc == DCH - 1))
                            return ins
                        p.op("pe", lambda t, bA=bA, bkA=bkA, j=j: hmm(t, bA, bkA, j),
                             reads=wbk((hc16 % 2) * 2) + [("xn", kc, j) for kc in range(DCH)],
                             writes=[("ps", bkA)])
                        p.op("pe", lambda t, bB=bB, bkB=bkB, j=j: hmm(t, bB, bkB, j),
                             reads=wbk((hc16 % 2) * 2 + 1) + [("xn", kc, j) for kc in range(DCH)],
                             writes=[("ps", bkB)])
                        sA = scr(c, 44)
                        sB = scr(c, 45)
                        p.op("act", lambda s, bkA=bkA, sA=sA: s.activation(
                            sA.v, c.ps[bkA][:, :], AF.Silu, bias=0.0, scale=1.0),
                            reads=[("ps", bkA)], writes=sA.k)
                        if is_shared:
                            p.op("dve", lambda v, bkB=bkB, sB=sB: v.tensor_scalar(
                                sB.v, c.ps[bkB][:, :], 0.5, None, ALU.mult),
                                reads=[("ps", bkB)], writes=sB.k)
                        else:
                            p.op("dve", lambda v, bkB=bkB, sB=sB, j2=j2: v.tensor_tensor(
                                sB.v, c.ps[bkB][:, :], c.wb_sb[:, j2 * 512:(j2 + 1) * 512], ALU.mult),
                                reads=[("ps", bkB), ("s", "wbsb", j2)], writes=sB.k)
                        hb = scr(c, hc16 * 2 + j2)
                        p.op("dve", lambda v, hb=hb, sA=sA, sB=sB: v.tensor_tensor(hb.v, sA.v, sB.v, ALU.mult),
                             reads=sA.k + sB.k, writes=hb.k)
                for dc in range(DCH):
                    wC = c.w2buf[dc % 2]
                    p.op("pool", lambda g, e=e, dc=dc, grp=grp, wC=wC: g.dma_start(
                        out=wC[:, :].rearrange("p (k m) -> p k m", k=16),
                        in_=c.din[FC][e, grp, dc]),
                        writes=[("s", "w2buf", dc % 2)], dma=f"qc{dc % 2}")
                    for j2 in range(2):
                        j = half * 2 + j2
                        bank = 4 + (dc * 2 + j2) % 2

                        def ymm(t, wC=wC, bank=bank, j2=j2):
                            ins = None
                            for k16 in range(16):
                                ins = t.matmul(c.ps[bank][:, :], wC[:, k16 * 128:(k16 + 1) * 128],
                                               c.SCR[:, (k16 * 2 + j2) * 512:(k16 * 2 + j2 + 1) * 512],
                                               start=(k16 == 0), stop=(k16 == 15))
                            return ins
                        p.op("pe", ymm,
                             reads=[("s", "w2buf", dc % 2)] + [("scr", k16 * 2 + j2) for k16 in range(16)],
                             writes=[("ps", bank)])
                        xv = c.x[dc][:, j * 512:(j + 1) * 512]
                        p.op("dve", lambda v, xv=xv, bank=bank: v.tensor_tensor(xv, xv, c.ps[bank][:, :], ALU.add),
                             reads=[("ps", bank), ("x", dc, j)], writes=[("x", dc, j)])



def proj_ktm_vtm(p, c, sub):
    for which, W, brow, bkey in (("ktm", "Wk_r", c.bk_bf, "bk_bf"),
                                 ("vtm", "Wv_r", c.bv_bf, "bv_bf")):
        gbase = 16 if which == "ktm" else 24
        for kh in range(2):
            for kc in range(DCH):
                u = c.wbuf[kc // 2][:, (kc % 2) * 512:(kc % 2 + 1) * 512]
                p.op("pool", lambda g, W=W, kc=kc, kh=kh, u=u: g.dma_start(
                    out=u, in_=c.din[W][kc, :, kh * 512:(kh + 1) * 512]),
                    writes=wbk(kc // 2, kc % 2), dma=f"qw{kc // 2}")
            for tc in range(4):
                bank = tc % 2
                tok = sub * 512 + tc * 128

                def kmm(t, kh=kh, tok=tok, bank=bank, brow=brow):
                    ins = None
                    for kc in range(DCH):
                        u = c.wbuf[kc // 2][:, (kc % 2) * 512:(kc % 2 + 1) * 512]
                        ins = t.matmul(c.ps[bank][:, :], c.xn[kc][:, tok:tok + 128], u,
                                       start=(kc == 0), stop=False)
                    ins = t.matmul(c.ps[bank][:, :], c.ones_row_bf[0:1, tok % 128:tok % 128 + 128],
                                   brow[0:1, kh * 512:(kh + 1) * 512], start=False, stop=True)
                    return ins
                p.op("pe", kmm,
                     reads=[k for kc in range(DCH) for k in wbk(kc // 2, kc % 2)]
                     + [("xn", kc, sub) for kc in range(DCH)]
                     + [("s", bkey), ("s", "ones_row_bf")],
                     writes=[("ps", bank)])
                dst = scr(c, gbase + tc * 2 + kh)
                p.op("dve", lambda v, dst=dst, bank=bank: v.tensor_copy(dst.v, c.ps[bank][:, :]),
                     reads=[("ps", bank)], writes=dst.k)


def emit_attn0(p, c):
    for sub in range(4):
        proj_ktm_vtm(p, c, sub)
        for tc in range(4):
            gchunk = sub * 4 + tc
            for h in range(16):
                bank = 6 + h // 8
                ktm = scr(c, 16 + tc * 2, 2)
                vtm = scr(c, 24 + tc * 2, 2)
                lv = ktm.v[:, h * 64:(h + 1) * 64]
                rv = vtm.v[:, h * 64:(h + 1) * 64]
                ov = c.ps[bank][0:64, (h % 8) * 64:(h % 8 + 1) * 64]
                st = gchunk == 0 and (h % 8) == 0
                p.op("pe", lambda t, lv=lv, rv=rv, ov=ov, st=st: t.matmul(ov, lv, rv, start=st, stop=True),
                     reads=ktm.k + vtm.k, writes=[("ps", bank)])
    SL = scf(c, 0, 2, parts=64)
    for i2 in range(2):
        p.op("dve", lambda v, i2=i2: v.tensor_copy(SL.v[:, i2 * 512:(i2 + 1) * 512], c.ps[6 + i2][0:64, :]),
             reads=[("ps", 6 + i2)], writes=SL.k)
    p.op("sp", lambda s: s.dma_start(out=c.cinL0.ap(), in_=SL.v),
         reads=SL.k, writes=[("d", "cin0")], dma="qsx")
    p.op("pool", lambda g: g.collective_compute(
        "AllGather", ALU.bypass, replica_groups=[list(range(NCORE))],
        ins=[c.cinL0.ap().opt()], outs=[c.coutL0.ap().opt()]),
        reads=[("d", "cin0")], writes=[("d", "cout0")])
    Sp = scf(c, 0, 2, parts=64)
    Ssb = scr(c, 48)

    def sview(h):
        pb = (h % 2) * 64
        return Ssb.v[pb:pb + 64, (h // 2) * 64:(h // 2) * 64 + 64]

    def emit_sel():
        for rg in range(4):
            G = scf(c, 2, 2)
            for r in range(8):
                p.op("sp", lambda s, rg=rg, G=G, r=r: s.dma_start(
                    out=G.v[r * 16:(r + 1) * 16, :],
                    in_=c.coutL0.ap()[r * 64 + rg * 16:r * 64 + rg * 16 + 16, :]),
                    reads=[("d", "cout0")], writes=G.k, dma="qsx")
            for i2 in range(2):
                p.op("pe", lambda t, G=G, i2=i2: t.matmul(
                    c.ps[4][0:16, :], c.selm[:, :], G.v[:, i2 * 512:(i2 + 1) * 512],
                    start=True, stop=True),
                    reads=G.k + [("s", "selm")], writes=[("ps", 4)])
                stv = scf(c, 6, parts=16)
                p.op("dve", lambda v, stv=stv: v.tensor_copy(stv.v, c.ps[4][0:16, :]),
                     reads=[("ps", 4)], writes=stv.k)
                p.op("sp", lambda s, rg=rg, i2=i2, stv=stv: s.dma_start(
                    out=c.dSp[rg * 16:(rg + 1) * 16, i2 * 512:(i2 + 1) * 512], in_=stv.v),
                    reads=stv.k, writes=[("d", "dSp", rg, i2)], dma="qsx")
        p.op("sp", lambda s: s.dma_start(out=Sp.v.rearrange("p (a f) -> p a f", a=2),
                                         in_=c.dSp.rearrange("p (a f) -> p a f", a=2)),
             reads=[("d", "dSp", rg, i2) for rg in range(4) for i2 in range(2)],
             writes=Sp.k, dma="qsx")
    def emit_ssb_init():
        for par in range(2):
            p.op("dve", lambda v, par=par: v.tensor_copy(
                Ssb.v[par * 64:(par + 1) * 64, :].rearrange("p (a b) -> p a b", b=64),
                Sp.v.rearrange("p (a b) -> p a b", b=64)[:, par::2, :]),
                reads=Sp.k, writes=Ssb.k)

    for sub in range(4):
        tok = slice(sub * 512, (sub + 1) * 512)
        for which, W, bias, bkey, gbase in (("q", "Wq_l", c.biases0["bq_pp"], "bq_pp", 0),
                                            ("k", "Wk_l", c.biases0["bk_pp"], "bk_pp", 8)):
            for qc in range(DCH):
                wb = c.wbuf[qc % 2]
                p.op("pool", lambda g, W=W, qc=qc, wb=wb: g.dma_start(
                    out=wb[:, :].rearrange("p (k m) -> p k m", k=8), in_=c.din[W][qc]),
                    writes=wbk(qc % 2), dma=f"qw{qc % 2}")
                bank = qc % 2

                def pmm(t, wb=wb, bank=bank, tok=tok):
                    ins = None
                    for kc in range(DCH):
                        ins = t.matmul(c.ps[bank][:, :], wb[:, kc * 128:(kc + 1) * 128],
                                       c.xn[kc][:, tok], start=(kc == 0), stop=(kc == DCH - 1))
                    return ins
                p.op("pe", pmm,
                     reads=wbk(qc % 2) + [("xn", kc, sub) for kc in range(DCH)],
                     writes=[("ps", bank)])
                dst = scr(c, gbase + qc)
                p.op("act", lambda s, dst=dst, bank=bank, bias=bias, qc=qc: s.activation(
                    dst.v, c.ps[bank][:, :], AF.Identity, bias=bias[:, qc:qc + 1], scale=1.0),
                    reads=[("ps", bank), ("s", bkey)], writes=dst.k)
        proj_ktm_vtm(p, c, sub)
        def nmm(t, tok=tok):
            ins = None
            for kc in range(DCH):
                ins = t.matmul(c.ps[4][0:16, :], c.Wn[:, kc, :], c.xn[kc][:, tok],
                               start=(kc == 0), stop=(kc == DCH - 1))
            return ins
        p.op("pe", nmm, reads=[("s", "Wn")] + [("xn", kc, sub) for kc in range(DCH)],
             writes=[("ps", 4)])
        nsc = Buf(c.SCR[0:16, 44 * 512:45 * 512], [("scr", 44)])
        p.op("act", lambda s, nsc=nsc: s.activation(nsc.v, c.ps[4][0:16, :], AF.Exp,
                                                    bias=c.negbn[:, :], scale=-1.0),
             reads=[("ps", 4), ("s", "negbn")], writes=nsc.k)
        if sub == 0:
            emit_sel()
            emit_ssb_init()
        for tc in range(4):
            gchunk = sub * 4 + tc
            ktm = scr(c, 16 + tc * 2, 2)
            vtm = scr(c, 24 + tc * 2, 2)
            for h in range(16):
                qb = scr(c, 0 + h // 2)
                kb = scr(c, 8 + h // 2)
                pb = (h % 2) * 64
                sbank = h % 2
                att = scr(c, 46 + h % 2)
                p.op("pe", lambda t, kb=kb, qb=qb, pb=pb, tc=tc, sbank=sbank: t.matmul(
                    c.ps[sbank][:, 0:128], kb.v[pb:pb + 64, tc * 128:(tc + 1) * 128],
                    qb.v[pb:pb + 64, tc * 128:(tc + 1) * 128], start=True, stop=True),
                    reads=kb.k + qb.k, writes=[("ps", sbank)])
                p.op("dve", lambda v, att=att, sbank=sbank: v.tensor_tensor(
                    att.v[:, 0:128], c.ps[sbank][:, 0:128], c.triuT[:, :], ALU.mult),
                    reads=[("ps", sbank), ("s", "triuT")], writes=att.k)
                cbank = 2 + h % 2
                p.op("pe", lambda t, vtm=vtm, att=att, h=h, cbank=cbank: t.matmul(
                    c.ps[cbank][0:64, 0:128], vtm.v[:, h * 64:(h + 1) * 64], att.v[:, 0:128],
                    start=True, stop=False),
                    reads=vtm.k + att.k, writes=[("ps", cbank)])
                p.op("pe", lambda t, qb=qb, pb=pb, tc=tc, h=h, cbank=cbank: t.matmul(
                    c.ps[cbank][0:64, 0:128], sview(h),
                    qb.v[pb:pb + 64, tc * 128:(tc + 1) * 128], start=False, stop=True),
                    reads=Ssb.k + qb.k, writes=[("ps", cbank)])
                ctx = scr(c, 32 + h // 2)
                p.op("dve", lambda v, ctx=ctx, pb=pb, tc=tc, cbank=cbank: v.tensor_copy(
                    ctx.v[pb:pb + 64, tc * 128:(tc + 1) * 128], c.ps[cbank][0:64, 0:128]),
                    reads=[("ps", cbank)], writes=ctx.k)
            for h in range(16):
                bank = 6 + h // 8
                lv = ktm.v[:, h * 64:(h + 1) * 64]
                rv = vtm.v[:, h * 64:(h + 1) * 64]
                ov = c.ps[bank][0:64, (h % 8) * 64:(h % 8 + 1) * 64]
                st = gchunk == 0 and (h % 8) == 0
                p.op("pe", lambda t, lv=lv, rv=rv, ov=ov, st=st: t.matmul(ov, lv, rv, start=st, stop=True),
                     reads=ktm.k + vtm.k, writes=[("ps", bank)])
                p.op("dve", lambda v, h=h, bank=bank: v.tensor_tensor(
                    sview(h), c.ps[bank][0:64, (h % 8) * 64:(h % 8 + 1) * 64],
                    Sp.v[:, h * 64:(h + 1) * 64], ALU.add),
                    reads=[("ps", bank)] + Sp.k, writes=Ssb.k)
        for h in range(16):
            p.op("pe", lambda t, h=h, nsc=nsc: t.matmul(
                c.ps[4][0:64, :], c.honehot[:, h * 64:(h + 1) * 64], nsc.v,
                start=True, stop=True),
                reads=nsc.k + [("s", "honehot")], writes=[("ps", 4)])
            ctx = scr(c, 32 + h // 2)
            pb = (h % 2) * 64
            p.op("dve", lambda v, ctx=ctx, pb=pb: v.tensor_tensor(
                ctx.v[pb:pb + 64, :], ctx.v[pb:pb + 64, :], c.ps[4][0:64, :], ALU.mult),
                reads=ctx.k + [("ps", 4)], writes=ctx.k)
        for dc in range(DCH):
            wb = c.wbuf[2 + dc % 2]
            p.op("pool", lambda g, dc=dc, wb=wb: g.dma_start(
                out=wb[:, :].rearrange("p (k m) -> p k m", k=8), in_=c.din["Wo_l"][dc]),
                writes=wbk(2 + dc % 2), dma=f"qw{2 + dc % 2}")

            def omm(t, wb=wb):
                ins = None
                for cc in range(DCH):
                    ctx = scr(c, 32 + cc)
                    ins = t.matmul(c.ps[5][:, :], wb[:, cc * 128:(cc + 1) * 128], ctx.v,
                                   start=(cc == 0), stop=(cc == DCH - 1))
                return ins
            p.op("pe", omm,
                 reads=wbk(2 + dc % 2) + [("scr", 32 + cc) for cc in range(DCH)],
                 writes=[("ps", 5)])
            otmp = scf(c, 5)
            p.op("act", lambda s, otmp=otmp, dc=dc: s.activation(
                otmp.v, c.ps[5][:, :], AF.Identity, bias=c.biases0["bo_pp"][:, dc:dc + 1], scale=1.0),
                reads=[("ps", 5), ("s", "bo_pp")], writes=otmp.k)
            xv = c.x[dc][:, tok]
            p.op("dve", lambda v, xv=xv, otmp=otmp: v.tensor_tensor(xv, xv, otmp.v, ALU.add),
                 reads=otmp.k + [("x", dc, sub)], writes=[("x", dc, sub)])



SER_SEGS = set()


def emit_attn1(p, c):
    def seg(name):
        p.scope_ser = name in SER_SEGS
    uBx = Buf(c.SCR[0:16, 0:2112].rearrange("p (a b) -> p a b", b=33),
              [("scr", g) for g in range(5)])
    hsT = Buf(c.SCR[0:16, 2560:4608], [("scr", g) for g in range(5, 9)])
    PowsBf = Buf(c.SCR[0:16, 4608:5120], [("scr", 9)])
    WCT = Buf(c.SCR[0:16, 5632:6656], [("scr", 11), ("scr", 12)])
    Pows = Buf(c.SCF[0:16, 0:512], [("scf", 0)])
    echn = Buf(c.SCF[0:16, 1024:1088], [("scf", 2)])
    hmid = Buf(c.SCF[0:16, 1536:1537], [("scf", 3)])
    seg("uproj")
    for sub in range(4):
        tok = slice(sub * 512, (sub + 1) * 512)
        for ec in range(DCH):
            wbi = c.wbuf[0 + ec % 2]
            wbg = c.wbuf[2 + ec % 2]
            p.op("pool", lambda g, ec=ec, wbi=wbi: g.dma_start(
                out=wbi[:, :].rearrange("p (k m) -> p k m", k=8), in_=c.din["Wi_l"][ec]),
                writes=wbk(ec % 2), dma=f"qw{ec % 2}")
            p.op("pool", lambda g, ec=ec, wbg=wbg: g.dma_start(
                out=wbg[:, :].rearrange("p (k m) -> p k m", k=8), in_=c.din["Wg_l"][ec]),
                writes=wbk(2 + ec % 2), dma=f"qw{2 + ec % 2}")

            def prmm(t, wb, bank, tok=tok):
                ins = None
                for kc in range(DCH):
                    ins = t.matmul(c.ps[bank][:, :], wb[:, kc * 128:(kc + 1) * 128],
                                   c.xn[kc][:, tok], start=(kc == 0), stop=(kc == DCH - 1))
                return ins
            bank = ec % 2
            p.op("pe", lambda t, wbi=wbi, bank=bank: prmm(t, wbi, bank),
                 reads=wbk(ec % 2) + [("xn", kc, sub) for kc in range(DCH)],
                 writes=[("ps", bank)])
            p.op("pe", lambda t, wbg=wbg, bank=bank: prmm(t, wbg, 2 + bank),
                 reads=wbk(2 + ec % 2) + [("xn", kc, sub) for kc in range(DCH)],
                 writes=[("ps", 2 + bank)])
            xi = scf(c, 5)
            gt = scf(c, 2)
            p.op("act", lambda s, xi=xi, bank=bank, ec=ec: s.activation(
                xi.v, c.ps[bank][:, :], AF.Identity, bias=c.biases1["bi_pp"][:, ec:ec + 1], scale=1.0),
                reads=[("ps", bank), ("s", "bi_pp")], writes=xi.k)
            p.op("act", lambda s, gt=gt, bank=bank, ec=ec: s.activation(
                gt.v, c.ps[2 + bank][:, :], AF.Sigmoid, bias=c.biases1["bg_pp"][:, ec:ec + 1], scale=1.0),
                reads=[("ps", 2 + bank), ("s", "bg_pp")], writes=gt.k)
            ub = scr(c, 16 + ec)
            p.op("dve", lambda v, ub=ub, xi=xi, gt=gt: v.tensor_tensor(ub.v, xi.v, gt.v, ALU.mult),
                 reads=xi.k + gt.k, writes=ub.k)

        def ubmm(t):
            ins = None
            for kc in range(DCH):
                ub = scr(c, 16 + kc)
                ins = t.matmul(c.ps[4][0:16, :], c.Bm[:, kc, :], ub.v,
                               start=(kc == 0), stop=(kc == DCH - 1))
            return ins
        p.op("pe", ubmm, reads=[("s", "Bm")] + [("scr", 16 + kc) for kc in range(DCH)],
             writes=[("ps", 4)])
        p.op("dve", lambda v, sub=sub: v.tensor_copy(
            uBx.v[:, sub * 16:(sub + 1) * 16, 1:33],
            c.ps[4][0:16, :].rearrange("p (a b) -> p a b", b=32)),
            reads=[("ps", 4)], writes=uBx.k)
    seg("powers")
    p.op("dve", lambda v: v.memset(uBx.v[:, :, 0:1], 0.0), writes=uBx.k)
    p.op("sp", lambda s: s.dma_start(out=Pows.v[:, 0:16], in_=c.din["AT"]),
         writes=Pows.k, dma="qsx")
    p.op("dve", lambda v: v.tensor_copy(PowsBf.v[:, 0:16], Pows.v[:, 0:16]),
         reads=Pows.k, writes=PowsBf.k)
    for l in range(2, 33):
        p.op("pe", lambda t, l=l: t.matmul(c.ps[5][0:16, 0:16], c.A_sb[:, :],
                                           Pows.v[:, (l - 2) * 16:(l - 1) * 16],
                                           start=True, stop=True),
             reads=Pows.k + [("s", "A_sb")], writes=[("ps", 5)])
        p.op("dve", lambda v, l=l: v.tensor_copy(Pows.v[:, (l - 1) * 16:l * 16], c.ps[5][0:16, 0:16]),
             reads=[("ps", 5)], writes=Pows.k)
        p.op("dve", lambda v, l=l: v.tensor_copy(PowsBf.v[:, (l - 1) * 16:l * 16], c.ps[5][0:16, 0:16]),
             reads=[("ps", 5)], writes=PowsBf.k)
    seg("lag")
    CG = [(0, 15), (15, 15), (30, 15), (45, 15), (60, 4)]
    for l in range(33):
        for cgi, (c0, ncg) in enumerate(CG):
            lhs = c.I16bf[:, :] if l == 0 else PowsBf.v[:, (l - 1) * 16:l * 16]
            lk = [("s", "I16bf")] if l == 0 else PowsBf.k
            ov = c.ps[cgi][0:16, 0:ncg * 33].rearrange("p (a b) -> p a b", b=33)[:, :, l:33]
            iv = uBx.v[:, c0:c0 + ncg, 0:33 - l]
            p.op("pe", lambda t, lhs=lhs, ov=ov, iv=iv, l=l: t.matmul(
                ov, lhs, iv, start=(l == 0), stop=False),
                reads=lk + uBx.k, writes=[("ps", cgi)])
    seg("scan")
    for cgi, (c0, ncg) in enumerate(CG):
        p.op("dve", lambda v, cgi=cgi, c0=c0, ncg=ncg: v.tensor_copy(
            echn.v[:, c0:c0 + ncg],
            c.ps[cgi][0:16, 0:ncg * 33].rearrange("p (a b) -> p a b", b=33)[:, :, 32]),
            reads=[("ps", cgi)], writes=echn.k)
    p.op("dve", lambda v: v.memset(c.svec[0][:, :], 0.0), writes=[("s", "sv0")])
    for cc in range(64):
        cur, nxt = c.svec[cc % 2], c.svec[(cc + 1) % 2]
        ck, nk = [("s", f"sv{cc % 2}")], [("s", f"sv{(cc + 1) % 2}")]

        def smm(t, cur=cur, cc=cc):
            t.matmul(c.ps[5][0:16, 0:1], Pows.v[:, 31 * 16:32 * 16], cur[:, :], start=True, stop=False)
            return t.matmul(c.ps[5][0:16, 0:1], c.I16f[:, :], echn.v[:, cc:cc + 1], start=False, stop=True)
        p.op("pe", smm, reads=Pows.k + echn.k + ck + [("s", "I16f")], writes=[("ps", 5)])
        p.op("dve", lambda v, nxt=nxt: v.tensor_copy(nxt[:, :], c.ps[5][0:16, 0:1]),
             reads=[("ps", 5)], writes=nk)
    p.op("sp", lambda s: s.dma_start(out=c.cinL1.ap(), in_=c.svec[0][:, :]),
         reads=[("s", "sv0")], writes=[("d", "cin1")], dma="qsx")
    p.op("pool", lambda g: g.collective_compute(
        "AllGather", ALU.bypass, replica_groups=[list(range(NCORE))],
        ins=[c.cinL1.ap().opt()], outs=[c.coutL1.ap().opt()]),
        reads=[("d", "cin1")], writes=[("d", "cout1")])
    p.op("sp", lambda s: s.dma_start(out=c.Gl1[:, :], in_=c.coutL1.ap()),
         reads=[("d", "cout1")], writes=[("s", "Gl1")], dma="qsx")
    p.op("pe", lambda t: t.matmul(c.ps[5][0:16, 0:1], c.selm[:, :], c.Gl1[:, 0:1],
                                  start=True, stop=True),
         reads=[("s", "Gl1"), ("s", "selm")], writes=[("ps", 5)])
    p.op("dve", lambda v: v.tensor_copy(hmid.v, c.ps[5][0:16, 0:1]),
         reads=[("ps", 5)], writes=hmid.k)
    p.op("dve", lambda v: v.tensor_copy(c.svec[0][:, :], hmid.v),
         reads=hmid.k, writes=[("s", "sv0")])
    for cc in range(64):
        cur, nxt = c.svec[cc % 2], c.svec[(cc + 1) % 2]
        ck, nk = [("s", f"sv{cc % 2}")], [("s", f"sv{(cc + 1) % 2}")]
        p.op("dve", lambda v, cur=cur, cc=cc: v.tensor_copy(uBx.v[:, cc, 0:1], cur[:, :]),
             reads=ck, writes=uBx.k)

        def smm2(t, cur=cur, cc=cc):
            t.matmul(c.ps[5][0:16, 0:1], Pows.v[:, 31 * 16:32 * 16], cur[:, :], start=True, stop=False)
            return t.matmul(c.ps[5][0:16, 0:1], c.I16f[:, :], echn.v[:, cc:cc + 1], start=False, stop=True)
        p.op("pe", smm2, reads=Pows.k + echn.k + ck + [("s", "I16f")], writes=[("ps", 5)])
        p.op("dve", lambda v, nxt=nxt: v.tensor_copy(nxt[:, :], c.ps[5][0:16, 0:1]),
             reads=[("ps", 5)], writes=nk)
    seg("impulse")
    for l in range(1, 33):
        for cgi, (c0, ncg) in enumerate(CG):
            ov = c.ps[cgi][0:16, 0:ncg * 33].rearrange("p (a b) -> p a b", b=33)[:, :, l:l + 1]
            iv = uBx.v[:, c0:c0 + ncg, 0:1]
            p.op("pe", lambda t, l=l, ov=ov, iv=iv: t.matmul(
                ov, PowsBf.v[:, (l - 1) * 16:l * 16], iv, start=False, stop=(l == 32)),
                reads=PowsBf.k + uBx.k, writes=[("ps", cgi)])
    for cgi, (c0, ncg) in enumerate(CG):
        p.op("dve", lambda v, cgi=cgi, c0=c0, ncg=ncg: v.tensor_copy(
            hsT.v.rearrange("p (a b) -> p a b", b=32)[:, c0:c0 + ncg, :],
            c.ps[cgi][0:16, 0:ncg * 33].rearrange("p (a b) -> p a b", b=33)[:, :, 1:33]),
            reads=[("ps", cgi)], writes=hsT.k)
    seg("out")
    for kc in range(DCH):
        wb = c.wbuf[kc % 2]
        p.op("pool", lambda g, kc=kc, wb=wb: g.dma_start(out=wb[:, :], in_=c.din["WoT_r"][kc]),
             writes=wbk(kc % 2), dma=f"qw{kc % 2}")
        for i2 in range(2):
            p.op("pe", lambda t, kc=kc, wb=wb, i2=i2: t.matmul(
                c.ps[i2][0:16, :], c.Cm[:, kc, :], wb[:, i2 * 512:(i2 + 1) * 512],
                start=(kc == 0), stop=(kc == DCH - 1)),
                reads=wbk(kc % 2) + [("s", "Cm")], writes=[("ps", i2)])
    for i2 in range(2):
        p.op("dve", lambda v, i2=i2: v.tensor_copy(WCT.v[:, i2 * 512:(i2 + 1) * 512], c.ps[i2][0:16, :]),
             reads=[("ps", i2)], writes=WCT.k)
    for ec in range(DCH):
        for j in range(4):
            bank = 2 + (ec * 4 + j) % 2
            p.op("pe", lambda t, ec=ec, j=j, bank=bank: t.matmul(
                c.ps[bank][:, :], WCT.v[:, ec * 128:(ec + 1) * 128],
                hsT.v[:, j * 512:(j + 1) * 512], start=True, stop=True),
                reads=WCT.k + hsT.k, writes=[("ps", bank)])
            ot = scf(c, 5)
            p.op("act", lambda s, ot=ot, bank=bank, ec=ec: s.activation(
                ot.v, c.ps[bank][:, :], AF.Identity, bias=c.biases1["bout_pp"][:, ec:ec + 1], scale=1.0),
                reads=[("ps", bank), ("s", "bout_pp")], writes=ot.k)
            xv = c.x[ec][:, j * 512:(j + 1) * 512]
            p.op("dve", lambda v, xv=xv, ot=ot: v.tensor_tensor(xv, xv, ot.v, ALU.add),
                 reads=ot.k + [("x", ec, j)], writes=[("x", ec, j)])
    p.scope_ser = False



DEBUG_DUMPS = []


def build(phases, host):
    p = Prog()
    c = alloc(p, phases, host)
    emit_load_common(p, c, phases)
    for ph in phases:
        emit_ln(p, c)
        if ph in ("ffn0", "ffn1"):
            emit_moe(p, c, ph)
        elif ph == "attn0":
            emit_attn0(p, c)
        elif ph == "attn1":
            emit_attn1(p, c)
    for name, getter, shape, keys in DEBUG_DUMPS:
        ap = getter(c)
        dt = BF16 if "bfloat" in str(ap.dtype) else F32
        dout = p.nc.dram_tensor("dbg_" + name, shape, dt, kind="ExternalOutput").ap()
        p.op("sp", lambda s, dout=dout, ap=ap: s.dma_start(out=dout, in_=ap),
             reads=keys, dma="qsx")
    emit_store(p, c)
    return p.finalize(), c.innames


def run_phases(host, PC, groups, xcur):
    built = {}
    for phases in groups:
        key = tuple(phases)
        if key not in built:
            built[key] = build(phases, host)
        nc, names = built[key]
        in_maps = []
        for cid in range(NCORE):
            m = {}
            for n in names:
                if n == "xin":
                    m[n] = xcur[cid]
                elif n == "selmat":
                    m[n] = PC[cid]["selmat"]
                else:
                    m[n] = host[n]
            in_maps.append(m)
        res = run_bass_kernel_spmd(nc, in_maps, core_ids=list(range(NCORE)))
        xcur = [np.ascontiguousarray(res.results[cid]["y"]) for cid in range(NCORE)]
    return xcur


def kernel(**inputs):
    host, PC = prep_host(inputs)
    groups = [ALL_PHASES] if FUSED else [(ph,) for ph in ALL_PHASES]
    xcur = [PC[cid]["xin"] for cid in range(NCORE)]
    xcur = run_phases(host, PC, groups, xcur)
    out = np.empty((4, 4096, 1024), np.float32)
    for cid in range(NCORE):
        b, hh = divmod(cid, 2)
        out[b, hh * T:(hh + 1) * T, :] = xcur[cid].T
    return out


# version 05
# speedup vs baseline: 1.0424x; 1.0093x over previous
import contextlib
import numpy as np
import ml_dtypes
import concourse.bass as bass
import concourse.mybir as mybir
from concourse.bass_utils import run_bass_kernel_spmd

F32 = mybir.dt.float32
BF16 = mybir.dt.bfloat16
AF = mybir.ActivationFunctionType
ALU = mybir.AluOpType
AX = mybir.AxisListType

NCORE = 8
T = 2048
D = 1024
DCH = 8
BIG_NEG = -1.0e30

ALL_PHASES = ("ffn0", "attn0", "ffn0", "ffn1", "attn1", "ffn1")
FUSED = True
SERIALIZE = False

ENG2SECTION = {"pe": "tensor", "act": "scalar", "dve": "vector",
               "pool": "gpsimd", "sp": "sync"}
DMA_ROLES = ("qi", "qw0", "qw1", "qw2", "qw3", "qc0", "qc1", "qsx")
QUEUES = tuple(f"{r}_{i}" for r in DMA_ROLES for i in range(2))


class Prog:
    def __init__(self):
        self.nc = bass.Bass("TRN2", target_bir_lowering=False)
        self.st = contextlib.ExitStack()
        self.ops = {e: [] for e in ENG2SECTION}
        self.count = {s: 0 for s in list(ENG2SECTION) + list(QUEUES)}
        self.writer = {}
        self.readers = {}
        self.marks = {}
        self.scope_ser = False
        self.role_rot = {}

    def sbuf(self, name, shape, dt):
        return self.st.enter_context(self.nc.sbuf_tensor(name, shape, dt))

    def psum(self, name, shape, dt=F32):
        return self.st.enter_context(self.nc.psum_tensor(name, shape, dt))

    def op(self, eng, emit, reads=(), writes=(), dma=None, sreads=(), fence=False):
        if dma is not None:
            i = self.role_rot.get(dma, 0)
            self.role_rot[dma] = i ^ 1
            dma = f"{dma}_{i}"
        stream = dma if dma else eng
        if dma is None and eng in ("dve", "act", "pool"):
            fence = True
        waits = {}
        if SERIALIZE or self.scope_ser:
            for s, cv in self.count.items():
                if s == eng or cv == 0:
                    continue
                w = self.marks.get((eng, s), 0)
                if cv > w:
                    waits[s] = cv
                    self.marks[(eng, s)] = cv

        def need(pstream, cnt, force=False):
            if cnt <= 0:
                return
            if pstream == eng and dma is None and not force:
                return
            if pstream in QUEUES:
                cnt = self.count[pstream]
            w = self.marks.get((eng, pstream), 0)
            if cnt > w:
                waits[pstream] = max(waits.get(pstream, 0), cnt)
                self.marks[(eng, pstream)] = cnt

        for k in reads:
            lw = self.writer.get(k)
            if lw:
                need(lw[0], lw[1], force=lw[2])
        for k in sreads:
            lw = self.writer.get(k)
            if lw:
                need(lw[0], lw[1], force=True)
        for k in writes:
            lw = self.writer.get(k)
            if lw:
                need(lw[0], lw[1], force=lw[2])
            for r in self.readers.get(k, ()):
                need(*r)
        if dma is not None and self.count[stream] > 0:
            w = self.marks.get((eng, stream), 0)
            if self.count[stream] > w:
                waits[stream] = max(waits.get(stream, 0), self.count[stream])
                self.marks[(eng, stream)] = self.count[stream]
        self.count[stream] += 16 if dma else 1
        cnt = self.count[stream]
        for k in list(reads) + list(sreads):
            self.readers.setdefault(k, []).append((stream, cnt))
        for k in writes:
            self.writer[k] = (stream, cnt, fence)
            self.readers[k] = []
        self.ops[eng].append((sorted(waits.items()), emit, stream, 16 if dma else 1))

    def barrier_waits(self, eng, streams=None):
        waits = []
        for s in (streams or list(self.count)):
            if s == eng or self.count[s] == 0:
                continue
            w = self.marks.get((eng, s), 0)
            if self.count[s] > w:
                waits.append((s, self.count[s]))
                self.marks[(eng, s)] = self.count[s]
        if waits:
            self.ops[eng].append((waits, None, None, 0))

    def finalize(self):
        used = [s for s in self.count if self.count[s] > 0]
        sems = {s: self.st.enter_context(self.nc.semaphore(s)) for s in used}
        blk = self.st.enter_context(self.nc.Block())
        for eng, section_name in ENG2SECTION.items():
            if not self.ops[eng]:
                continue
            section = getattr(blk, section_name)
            eng_ops = self.ops[eng]

            def body(engobj, eng_ops=eng_ops):
                for waits, emit, stream, incv in eng_ops:
                    for s, cval in waits:
                        engobj.wait_ge(sems[s], cval)
                    if emit is not None:
                        ins = emit(engobj)
                        ins.then_inc(sems[stream], incv)
            section(body)
        self.st.close()
        return self.nc


class Buf:
    def __init__(self, v, keys):
        self.v = v
        self.k = list(keys)



def pretile_lhsT(W):
    K, M = W.shape
    return np.ascontiguousarray(
        W.reshape(K // 128, 128, M // 128, 128).transpose(2, 1, 0, 3))


def pretile_rhs(W):
    K, N = W.shape
    return np.ascontiguousarray(W.reshape(K // 128, 128, N))


def resident_rhs(W):
    K, n = W.shape
    return np.ascontiguousarray(W.reshape(K // 128, 128, n).transpose(1, 0, 2))


def bias_pp(b):
    return np.ascontiguousarray(b.reshape(-1, 128).T)


def prep_host(inputs):
    g = {k: np.asarray(v) for k, v in inputs.items()}
    H = {}
    w1s = [g["L0_ffn_ew1"][e] for e in range(8)] + [g["L0_ffn_sw1"]]
    w2s = [g["L0_ffn_ew2"][e] for e in range(8)] + [g["L0_ffn_sw2"]]
    w3s = [g["L0_ffn_ew3"][e] for e in range(8)] + [g["L0_ffn_sw3"]]
    H["F0A"] = np.stack([pretile_lhsT(w) for w in w1s])
    H["F0B"] = np.stack([pretile_lhsT(w) for w in w2s])

    def grp_split(arr, ngrp):
        dc, pp, kc, m = arr.shape
        return np.ascontiguousarray(
            arr.reshape(dc, pp, ngrp, kc // ngrp, m).transpose(2, 0, 1, 3, 4))
    H["F0C"] = np.stack([grp_split(pretile_lhsT(w), 1) for w in w3s])
    H["gw0"] = resident_rhs(g["L0_ffn_gate_w"])
    H["F1A"] = np.stack([pretile_lhsT(g["L1_ffn_W1"][e].T) for e in range(8)])
    H["F1B"] = np.stack([pretile_lhsT(g["L1_ffn_W3"][e].T) for e in range(8)])
    H["F1C"] = np.stack([grp_split(pretile_lhsT(g["L1_ffn_W2"][e].T), 2)
                         for e in range(8)])
    H["gw1"] = resident_rhs(np.ascontiguousarray(g["L1_ffn_Wr"].T))
    Wqk = g["L0_attn_Wqk"]
    H["Wq_l"] = pretile_lhsT(np.ascontiguousarray(Wqk[:, :1024]))
    H["Wk_l"] = pretile_lhsT(np.ascontiguousarray(Wqk[:, 1024:]))
    H["Wk_r"] = pretile_rhs(np.ascontiguousarray(Wqk[:, 1024:]))
    H["Wv_r"] = pretile_rhs(g["L0_attn_Wv"])
    H["Wo_l"] = pretile_lhsT(g["L0_attn_Wo"])
    H["Wn_r"] = resident_rhs(g["L0_attn_Wn"])
    H["bq_pp"] = bias_pp(g["L0_attn_bqk"][:1024])
    H["bk_pp"] = bias_pp(g["L0_attn_bqk"][1024:])
    H["bk_row"] = np.ascontiguousarray(g["L0_attn_bqk"][1024:]).reshape(1, 1024)
    H["bv_row"] = np.ascontiguousarray(g["L0_attn_bv"]).reshape(1, 1024)
    H["bn_pp"] = np.ascontiguousarray(g["L0_attn_bn"]).reshape(16, 1)
    H["bo_pp"] = bias_pp(g["L0_attn_bo"])
    H["Wi_l"] = pretile_lhsT(np.ascontiguousarray(g["L1_attn_W_in"][1024:, :].T))
    H["Wg_l"] = pretile_lhsT(np.ascontiguousarray(g["L1_attn_W_gate"].T))
    H["Bm_r"] = resident_rhs(np.ascontiguousarray(g["L1_attn_Bm"].T))
    H["Cm_r"] = resident_rhs(g["L1_attn_Cm"])
    H["WoT_r"] = pretile_rhs(np.ascontiguousarray(g["L1_attn_W_out"].T))
    H["A_lhs"] = np.ascontiguousarray(g["L1_attn_A"])
    H["AT"] = np.ascontiguousarray(g["L1_attn_A"].T)
    H["bi_pp"] = bias_pp(g["L1_attn_b_in"][1024:])
    H["bg_pp"] = bias_pp(g["L1_attn_b_gate"])
    H["bout_pp"] = bias_pp(g["L1_attn_b_out"])
    H["I128"] = np.eye(128, dtype=np.float32)
    H["I16bf"] = np.eye(16, dtype=np.float32).astype(ml_dtypes.bfloat16)
    H["I16f"] = np.eye(16, dtype=np.float32)
    H["ones_col_bf"] = np.ones((128, 1), dtype=np.float32).astype(ml_dtypes.bfloat16)
    H["ones_row"] = np.ones((1, 128), dtype=np.float32)
    H["ones_row_bf"] = np.ones((1, 128), dtype=np.float32).astype(ml_dtypes.bfloat16)
    H["triuT"] = np.triu(np.ones((128, 128), dtype=np.float32))
    eo = np.zeros((8, 8, 128), dtype=np.float32)
    for e in range(8):
        eo[e, e, :] = 1.0
    H["eoneh"] = np.ascontiguousarray(eo.transpose(1, 0, 2).reshape(8, 8 * 128))
    ho = np.zeros((16, 16, 64), dtype=np.float32)
    for h in range(16):
        ho[h, h, :] = 1.0
    H["honehot"] = np.ascontiguousarray(ho.transpose(1, 0, 2).reshape(16, 16 * 64)).astype(ml_dtypes.bfloat16)
    PC = []
    x = g["L0_attn_x"]
    for cid in range(NCORE):
        b, hh = divmod(cid, 2)
        d = {"xin": np.ascontiguousarray(x[b, hh * T:(hh + 1) * T, :].T)}
        sel = np.zeros((128, 16), dtype=np.float32)
        if hh == 1:
            for i in range(16):
                sel[(cid - 1) * 16 + i, i] = 1.0
        d["selmat"] = sel
        PC.append(d)
    return H, PC



PHASE_WEIGHTS = {
    "ffn0": ["F0A", "F0B", "F0C", "gw0"],
    "ffn1": ["F1A", "F1B", "F1C", "gw1"],
    "attn0": ["Wq_l", "Wk_l", "Wk_r", "Wv_r", "Wo_l", "Wn_r", "bq_pp", "bk_pp",
              "bk_row", "bv_row", "bn_pp", "bo_pp", "selmat", "honehot"],
    "attn1": ["Wi_l", "Wg_l", "Bm_r", "Cm_r", "WoT_r", "A_lhs", "AT", "I16f",
              "bi_pp", "bg_pp", "bout_pp", "selmat"],
}
CONST_NAMES = ["I128", "I16bf", "ones_col_bf", "ones_row", "ones_row_bf", "triuT", "eoneh"]

NSCR = 49
NSCF = 7



def wbk(unit, half=None):
    if half is None:
        return [("wb", unit, 0), ("wb", unit, 1)]
    return [("wb", unit, half)]

class Ctx:
    pass


def alloc(p, phases, host):
    c = Ctx()
    nc = p.nc
    c.din = {}
    names = set(CONST_NAMES)
    for ph in phases:
        names.update(PHASE_WEIGHTS[ph])
    names.add("xin")
    for n in sorted(names):
        arr = host.get(n)
        if arr is None:
            if n == "xin":
                arr = np.zeros((D, T), np.float32)
            elif n == "selmat":
                arr = np.zeros((128, 16), np.float32)
        dt = BF16 if arr.dtype == ml_dtypes.bfloat16 else F32
        c.din[n] = nc.dram_tensor(n, list(arr.shape), dt, kind="ExternalInput").ap()
    c.yout = nc.dram_tensor("y", [D, T], F32, kind="ExternalOutput").ap()
    c.innames = names

    c.x = [p.sbuf(f"x{i}", [128, T], F32) for i in range(DCH)]
    c.xn = [p.sbuf(f"xn{i}", [128, T], BF16) for i in range(DCH)]
    c.SCR = p.sbuf("SCR", [128, NSCR * 512], BF16)
    c.SCF = p.sbuf("SCF", [128, NSCF * 512], F32)
    c.wbuf = [p.sbuf(f"wbuf{i}", [128, 1024], BF16) for i in range(4)]
    c.w2buf = [p.sbuf(f"w2buf{i}", [128, 2048], BF16) for i in range(2)]
    c.ps = [p.psum(f"ps{i}", [128, 512]) for i in range(8)]
    c.wb_sb = p.sbuf("wb_sb", [128, 1024], F32)
    c.w_tm = p.sbuf("w_tm", [128, 128], F32)
    c.w_fm = p.sbuf("w_fm", [8, T], F32)
    c.stat = p.sbuf("stat", [1, 1536], F32)
    c.rst = p.sbuf("rst", [128, 64], F32)
    c.rst2 = p.sbuf("rst2", [128, 64], F32)
    c.gS = p.sbuf("gS", [128, 8], F32)
    c.I128 = p.sbuf("cI128", [128, 128], F32)
    c.I16bf = p.sbuf("cI16", [16, 16], BF16)
    c.ones_col_bf = p.sbuf("cOnesCol", [128, 1], BF16)
    c.ones_row = p.sbuf("cOnesRow", [1, 128], F32)
    c.ones_row_bf = p.sbuf("cOnesRowBf", [1, 128], BF16)
    c.triuT = p.sbuf("cTriu", [128, 128], F32)
    c.eoneh = p.sbuf("cEoneh", [8, 8 * 128], F32)
    c.gw = {}
    if "ffn0" in phases:
        c.gw["ffn0"] = p.sbuf("gw0sb", [128, 8, 8], BF16)
    if "ffn1" in phases:
        c.gw["ffn1"] = p.sbuf("gw1sb", [128, 8, 8], BF16)
    c.dwscr = nc.dram_tensor("dwscr", [16, T], F32).ap()
    need_attn = ("attn0" in phases) or ("attn1" in phases)
    if need_attn:
        c.selm = p.sbuf("selm", [128, 16], F32)
    if "attn0" in phases:
        c.biases0 = {n: p.sbuf("sb_" + n, list(host[n].shape), F32)
                     for n in ("bq_pp", "bk_pp", "bo_pp", "bn_pp")}
        c.negbn = p.sbuf("negbn", [16, 1], F32)
        c.Wn = p.sbuf("WnSb", [128, 8, 16], BF16)
        c.bk_bf = p.sbuf("bk_bf", [1, 1024], BF16)
        c.bv_bf = p.sbuf("bv_bf", [1, 1024], BF16)
        c.honehot = p.sbuf("cHoneh", [16, 16 * 64], BF16)
        c.cinL0 = nc.dram_tensor("cinL0", [64, 1024], F32)
        c.coutL0 = nc.dram_tensor("coutL0", [512, 1024], F32)
        c.dSp = nc.dram_tensor("dSp", [64, 1024], F32).ap()
    if "attn1" in phases:
        c.biases1 = {n: p.sbuf("sb_" + n, list(host[n].shape), F32)
                     for n in ("bi_pp", "bg_pp", "bout_pp")}
        c.Bm = p.sbuf("BmSb", [128, 8, 16], BF16)
        c.Cm = p.sbuf("CmSb", [128, 8, 16], BF16)
        c.A_sb = p.sbuf("A_sb", [16, 16], F32)
        c.I16f = p.sbuf("sI16f", [16, 16], F32)
        c.svec = [p.sbuf(f"svec{i}", [16, 1], F32) for i in range(2)]
        c.cinL1 = nc.dram_tensor("cinL1", [16, 1], F32)
        c.coutL1 = nc.dram_tensor("coutL1", [128, 1], F32)
        c.Gl1 = p.sbuf("Gl1", [128, 1], F32)
    return c


def scr(c, g, ngr=1, parts=128, p0=0):
    v = c.SCR[p0:p0 + parts, g * 512:(g + ngr) * 512]
    return Buf(v, [("scr", g + k) for k in range(ngr)])


def scf(c, g, ngr=1, parts=128, p0=0):
    v = c.SCF[p0:p0 + parts, g * 512:(g + ngr) * 512]
    return Buf(v, [("scf", g + k) for k in range(ngr)])


def emit_load_common(p, c, phases):
    for i in range(DCH):
        p.op("sp", lambda s, i=i: s.dma_start(out=c.x[i][:, :], in_=c.din["xin"][i * 128:(i + 1) * 128, :]),
             writes=[("x", i, j) for j in range(4)], dma="qsx")
    loads = [(c.I128, "I128"), (c.I16bf, "I16bf"), (c.ones_col_bf, "ones_col_bf"),
             (c.ones_row, "ones_row"), (c.ones_row_bf, "ones_row_bf"), (c.triuT, "triuT"),
             (c.eoneh, "eoneh")]
    for t, n in loads:
        p.op("sp", lambda s, t=t, n=n: s.dma_start(out=t[:, :], in_=c.din[n]),
             writes=[("s", n)], dma="qsx")
    for ph, t in c.gw.items():
        n = "gw0" if ph == "ffn0" else "gw1"
        p.op("pool", lambda g, t=t, n=n: g.dma_start(out=t[:, :, :], in_=c.din[n]),
             writes=[("s", "gw_" + ph)], dma="qi")
    if hasattr(c, "selm"):
        p.op("sp", lambda s: s.dma_start(out=c.selm[:, :], in_=c.din["selmat"]),
             writes=[("s", "selm")], dma="qsx")
    if "attn0" in phases:
        for n, t in c.biases0.items():
            p.op("sp", lambda s, t=t, n=n: s.dma_start(out=t[:, :], in_=c.din[n]),
                 writes=[("s", n)], dma="qsx")
        p.op("pool", lambda g: g.dma_start(out=c.Wn[:, :, :], in_=c.din["Wn_r"]),
             writes=[("s", "Wn")], dma="qi")
        p.op("pool", lambda g: g.dma_start(out=c.bk_bf[:, :], in_=c.din["bk_row"]),
             writes=[("s", "bk_bf")], dma="qi")
        p.op("pool", lambda g: g.dma_start(out=c.bv_bf[:, :], in_=c.din["bv_row"]),
             writes=[("s", "bv_bf")], dma="qi")
        p.op("sp", lambda s: s.dma_start(out=c.honehot[:, :], in_=c.din["honehot"]),
             writes=[("s", "honehot")], dma="qi")
        p.op("dve", lambda v: v.tensor_scalar(c.negbn[:, :], c.biases0["bn_pp"][:, :], -1.0, None, ALU.mult),
             reads=[("s", "bn_pp")], writes=[("s", "negbn")])
    if "attn1" in phases:
        for n, t in c.biases1.items():
            p.op("sp", lambda s, t=t, n=n: s.dma_start(out=t[:, :], in_=c.din[n]),
                 writes=[("s", n)], dma="qsx")
        p.op("pool", lambda g: g.dma_start(out=c.Bm[:, :, :], in_=c.din["Bm_r"]),
             writes=[("s", "Bm")], dma="qi")
        p.op("pool", lambda g: g.dma_start(out=c.Cm[:, :, :], in_=c.din["Cm_r"]),
             writes=[("s", "Cm")], dma="qi")
        p.op("sp", lambda s: s.dma_start(out=c.A_sb[:, :], in_=c.din["A_lhs"]),
             writes=[("s", "A_sb")], dma="qsx")
        p.op("sp", lambda s: s.dma_start(out=c.I16f[:, :], in_=c.din["I16f"]),
             writes=[("s", "I16f")], dma="qsx")


def emit_store(p, c):
    for i in range(DCH):
        p.op("sp", lambda s, i=i: s.dma_start(out=c.yout[i * 128:(i + 1) * 128, :], in_=c.x[i][:, :]),
             reads=[("x", i, j) for j in range(4)], dma="qsx")
    p.barrier_waits("sp")



def emit_ln(p, c):
    for j in range(4):
        sl = slice(j * 512, (j + 1) * 512)
        for i in range(DCH):
            xb = scr(c, 40 + i % 2)
            xq = scr(c, 42 + i % 2)
            p.op("dve", lambda v, i=i, xb=xb, sl=sl: v.tensor_copy(xb.v, c.x[i][:, sl]),
                 reads=[("x", i, j)], writes=xb.k)
            p.op("pe", lambda t, xb=xb, i=i: t.matmul(
                c.ps[6][0:1, :], c.ones_col_bf[:, :], xb.v,
                start=(i == 0), stop=(i == DCH - 1)),
                reads=xb.k + [("s", "ones_col_bf")], writes=[("ps", 6)])
            p.op("act", lambda s, i=i, xq=xq, sl=sl: s.activation(xq.v, c.x[i][:, sl], AF.Square, bias=0.0, scale=1.0),
                 reads=[("x", i, j)], writes=xq.k)
            p.op("pe", lambda t, xq=xq, i=i: t.matmul(
                c.ps[7][0:1, :], c.ones_col_bf[:, :], xq.v,
                start=(i == 0), stop=(i == DCH - 1)),
                reads=xq.k + [("s", "ones_col_bf")], writes=[("ps", 7)])
        m_row = c.stat[0:1, 0:512]
        rows = Buf(None, [("s", "lnrows")])
        t0 = c.stat[0:1, 512:1024]
        t1 = c.stat[0:1, 1024:1536]
        p.op("dve", lambda v, m_row=m_row: v.tensor_scalar(m_row, c.ps[6][0:1, :], 1.0 / D, None, ALU.mult),
             reads=[("ps", 6)], writes=[("s", "lnm")])
        p.op("dve", lambda v: v.tensor_scalar(t0, c.ps[7][0:1, :], 1.0 / D, None, ALU.mult),
             reads=[("ps", 7)], writes=rows.k)
        p.op("dve", lambda v, m_row=m_row: v.tensor_tensor(t1, m_row, m_row, ALU.mult),
             reads=[("s", "lnm")], writes=rows.k)
        p.op("dve", lambda v: v.tensor_tensor(t0, t0, t1, ALU.subtract),
             reads=rows.k, writes=rows.k)
        p.op("dve", lambda v: v.tensor_scalar(t0, t0, 1e-5, None, ALU.add),
             reads=rows.k, writes=rows.k)
        p.op("act", lambda s: s.activation(t1, t0, AF.Sqrt, bias=0.0, scale=1.0),
             reads=rows.k, writes=rows.k)
        p.op("dve", lambda v: v.reciprocal(t0, t1), reads=rows.k, writes=rows.k, fence=True)
        p.op("dve", lambda v, m_row=m_row: v.tensor_tensor(t1, m_row, t0, ALU.mult),
             reads=[("s", "lnm")] + rows.k, writes=rows.k)
        p.op("dve", lambda v: v.tensor_scalar(t1, t1, -1.0, None, ALU.mult),
             reads=rows.k, writes=rows.k)
        p.op("pe", lambda t: t.matmul(c.ps[4][:, :], c.ones_row[:, :], t0, start=True, stop=True),
             reads=rows.k + [("s", "ones_row")], writes=[("ps", 4)], fence=True)
        p.op("pe", lambda t: t.matmul(c.ps[5][:, :], c.ones_row[:, :], t1, start=True, stop=True),
             reads=rows.k + [("s", "ones_row")], writes=[("ps", 5)])
        tt = scf(c, 4)
        for i in range(DCH):
            p.op("dve", lambda v, i=i, sl=sl, tt=tt: v.tensor_tensor(tt.v, c.x[i][:, sl], c.ps[4][:, :], ALU.mult),
                 reads=[("x", i, j), ("ps", 4)], writes=tt.k)
            p.op("dve", lambda v, i=i, sl=sl, tt=tt: v.tensor_tensor(c.xn[i][:, sl], tt.v, c.ps[5][:, :], ALU.add),
                 reads=tt.k + [("ps", 5)], writes=[("xn", i, j)])



def emit_moe(p, c, kind):
    shared = kind == "ffn0"
    nexp = 8
    H = 2048 if kind == "ffn0" else 4096
    NGRP = (H // 128) // 16
    FA, FB, FC = {"ffn0": ("F0A", "F0B", "F0C"), "ffn1": ("F1A", "F1B", "F1C")}[kind]
    gw = c.gw[kind]

    for tc in range(16):
        j = tc // 4

        def g_mm(t, tc=tc):
            ins = None
            for kc in range(DCH):
                ins = t.matmul(c.ps[6][:, tc * 8:(tc + 1) * 8],
                               c.xn[kc][:, tc * 128:(tc + 1) * 128],
                               gw[:, kc, :], start=(kc == 0), stop=(kc == DCH - 1))
            return ins
        p.op("pe", g_mm, reads=[("xn", kc, j) for kc in range(DCH)] + [("s", "gw_" + kind)],
             writes=[("ps", 6)])
    gA = c.SCF[:, 3 * 512:3 * 512 + 128]
    gv = gA.rearrange("p (t e) -> p t e", e=8)
    m1 = c.rst[:, 0:16]
    m2 = c.rst2[:, 0:16]
    e1 = c.SCF[:, 2 * 512 + 0:2 * 512 + 128]
    e2 = c.SCF[:, 2 * 512 + 128:2 * 512 + 256]
    pb = c.SCF[:, 2 * 512 + 256:2 * 512 + 384]
    tb = c.SCF[:, 2 * 512 + 384:2 * 512 + 512]
    p.op("dve", lambda v: v.tensor_copy(gA, c.ps[6][:, 0:128]),
         reads=[("ps", 6)], writes=[("scf", 3)])
    if kind == "ffn0":
        p.op("dve", lambda v: v.reduce_max(m1, gv, axis=AX.X),
             reads=[("scf", 3)], writes=[("s", "m1")], fence=True)
        p.op("dve", lambda v: v.tensor_tensor(
            e1.rearrange("p (t e) -> p t e", e=8), gv,
            m1.unsqueeze(2).broadcast_to([128, 16, 8]), ALU.subtract),
            reads=[("scf", 3), ("s", "m1")], writes=[("scf", 2)])
        p.op("dve", lambda v: v.tensor_scalar(e1, e1, 0.0, None, ALU.is_ge),
             reads=[("scf", 2)], writes=[("scf", 2)])
        p.op("dve", lambda v: v.tensor_scalar(tb, e1, BIG_NEG, None, ALU.mult),
             reads=[("scf", 2)], writes=[("scf", 2)])
        p.op("dve", lambda v: v.tensor_tensor(pb, gA, tb, ALU.add),
             reads=[("scf", 3), ("scf", 2)], writes=[("scf", 2)])
        p.op("dve", lambda v: v.reduce_max(m2, pb.rearrange("p (t e) -> p t e", e=8), axis=AX.X),
             reads=[("scf", 2)], writes=[("s", "m2")], fence=True)
        p.op("dve", lambda v: v.tensor_tensor(
            e2.rearrange("p (t e) -> p t e", e=8),
            pb.rearrange("p (t e) -> p t e", e=8),
            m2.unsqueeze(2).broadcast_to([128, 16, 8]), ALU.subtract),
            reads=[("scf", 2), ("s", "m2")], writes=[("scf", 2)])
        p.op("dve", lambda v: v.tensor_scalar(e2, e2, 0.0, None, ALU.is_ge),
             reads=[("scf", 2)], writes=[("scf", 2)])
        p.op("dve", lambda v: v.tensor_tensor(c.rst[:, 16:32], m1, m2, ALU.subtract),
             reads=[("s", "m1"), ("s", "m2")], writes=[("s", "dd")])
        p.op("act", lambda s: s.activation(c.rst2[:, 16:32], c.rst[:, 16:32], AF.Sigmoid, bias=0.0, scale=1.0),
             reads=[("s", "dd")], writes=[("s", "p1")])
        p.op("act", lambda s: s.activation(c.rst[:, 32:48], c.rst[:, 16:32], AF.Sigmoid, bias=0.0, scale=-1.0),
             reads=[("s", "dd")], writes=[("s", "p2")])
        p.op("dve", lambda v: v.tensor_scalar(c.rst2[:, 16:32], c.rst2[:, 16:32], 0.5, None, ALU.mult),
             reads=[("s", "p1")], writes=[("s", "p1")])
        p.op("dve", lambda v: v.tensor_scalar(c.rst[:, 32:48], c.rst[:, 32:48], 0.5, None, ALU.mult),
             reads=[("s", "p2")], writes=[("s", "p2")])
        p.op("dve", lambda v: v.tensor_tensor(
            e1.rearrange("p (t e) -> p t e", e=8),
            e1.rearrange("p (t e) -> p t e", e=8),
            c.rst2[:, 16:32].unsqueeze(2).broadcast_to([128, 16, 8]), ALU.mult),
            reads=[("scf", 2), ("s", "p1")], writes=[("scf", 2)])
        p.op("dve", lambda v: v.tensor_tensor(
            e2.rearrange("p (t e) -> p t e", e=8),
            e2.rearrange("p (t e) -> p t e", e=8),
            c.rst[:, 32:48].unsqueeze(2).broadcast_to([128, 16, 8]), ALU.mult),
            reads=[("scf", 2), ("s", "p2")], writes=[("scf", 2)])
        p.op("dve", lambda v: v.tensor_tensor(c.w_tm[:, :], e1, e2, ALU.add),
             reads=[("scf", 2), ("scf", 2)], writes=[("s", "wtm")])
    else:
        p.op("dve", lambda v: v.reduce_max(m1, gv, axis=AX.X),
             reads=[("scf", 3)], writes=[("s", "m1")], fence=True)
        p.op("dve", lambda v: v.tensor_tensor(
            e1.rearrange("p (t e) -> p t e", e=8), gv,
            m1.unsqueeze(2).broadcast_to([128, 16, 8]), ALU.subtract),
            reads=[("scf", 3), ("s", "m1")], writes=[("scf", 2)])
        p.op("act", lambda s: s.activation(e1, e1, AF.Exp, bias=0.0, scale=1.0),
             reads=[("scf", 2)], writes=[("scf", 2)])
        p.op("dve", lambda v: v.reduce_sum(m2, e1.rearrange("p (t e) -> p t e", e=8), axis=AX.X),
             reads=[("scf", 2)], writes=[("s", "m2")], fence=True)
        p.op("dve", lambda v: v.reciprocal(c.rst[:, 16:32], m2),
             reads=[("s", "m2")], writes=[("s", "rs")], fence=True)
        p.op("dve", lambda v: v.tensor_tensor(
            pb.rearrange("p (t e) -> p t e", e=8),
            e1.rearrange("p (t e) -> p t e", e=8),
            c.rst[:, 16:32].unsqueeze(2).broadcast_to([128, 16, 8]), ALU.mult),
            reads=[("scf", 2), ("s", "rs")], writes=[("scf", 2)])
        pv = pb.rearrange("p (t e) -> p t e", e=8)
        p.op("dve", lambda v: v.reduce_max(c.rst[:, 32:48], pv[:, :, 1:8], axis=AX.X),
             reads=[("scf", 2)], writes=[("s", "mr")], fence=True)
        p.op("dve", lambda v: v.tensor_tensor(
            e2.rearrange("p (t e) -> p t e", e=8)[:, :, 1:8], pv[:, :, 1:8],
            c.rst[:, 32:48].unsqueeze(2).broadcast_to([128, 16, 7]), ALU.subtract),
            reads=[("scf", 2), ("s", "mr")], writes=[("scf", 2)])
        p.op("dve", lambda v: v.tensor_scalar(e2, e2, 0.0, None, ALU.is_ge),
             reads=[("scf", 2)], writes=[("scf", 2)])
        p.op("dve", lambda v: v.tensor_tensor(c.rst2[:, 16:32], pv[:, :, 0], c.rst[:, 32:48], ALU.add),
             reads=[("scf", 2), ("s", "mr")], writes=[("s", "den")])
        p.op("dve", lambda v: v.tensor_scalar(c.rst2[:, 16:32], c.rst2[:, 16:32], 1e-9, None, ALU.max),
             reads=[("s", "den")], writes=[("s", "den")])
        p.op("dve", lambda v: v.reciprocal(c.rst2[:, 32:48], c.rst2[:, 16:32]),
             reads=[("s", "den")], writes=[("s", "rden")], fence=True)
        p.op("dve", lambda v: v.tensor_scalar(c.rst2[:, 32:48], c.rst2[:, 32:48], 0.5, None, ALU.mult),
             reads=[("s", "rden")], writes=[("s", "rden")])
        p.op("dve", lambda v: v.tensor_tensor(c.rst[:, 48:64], pv[:, :, 0], c.rst2[:, 32:48], ALU.mult),
             reads=[("scf", 2), ("s", "rden")], writes=[("s", "w0")])
        p.op("dve", lambda v: v.tensor_copy(
            c.w_tm[:, :].rearrange("p (t e) -> p t e", e=8)[:, :, 0],
            c.rst[:, 48:64]),
            reads=[("s", "w0")], writes=[("s", "wtm")])
        p.op("dve", lambda v: v.tensor_tensor(c.rst2[:, 48:64], c.rst[:, 32:48], c.rst2[:, 32:48], ALU.mult),
             reads=[("s", "mr"), ("s", "rden")], writes=[("s", "wr2")])
        p.op("dve", lambda v: v.tensor_tensor(
            c.w_tm[:, :].rearrange("p (t e) -> p t e", e=8)[:, :, 1:8],
            e2.rearrange("p (t e) -> p t e", e=8)[:, :, 1:8],
            c.rst2[:, 48:64].unsqueeze(2).broadcast_to([128, 16, 7]), ALU.mult),
            reads=[("scf", 2), ("s", "wr2")], writes=[("s", "wtm")])
    for tc in range(16):
        p.op("pe", lambda t, tc=tc: t.transpose(
            c.ps[7][0:8, (tc % 4) * 128:(tc % 4 + 1) * 128],
            c.w_tm[:, tc * 8:(tc + 1) * 8], c.I128[:, :]),
            reads=[("s", "wtm"), ("s", "I128")], writes=[("ps", 7)])
        if tc % 4 == 3:
            grp = tc // 4
            p.op("dve", lambda v, grp=grp: v.tensor_copy(
                c.w_fm[:, grp * 512:(grp + 1) * 512], c.ps[7][0:8, :]),
                reads=[("ps", 7)], writes=[("s", "wfm", grp)])

    for half in range(2):
        order = ([nexp] + list(range(nexp))) if shared else list(range(nexp))
        for e in order:
            is_shared = shared and e == nexp
            if not is_shared:
                for j2 in range(2):
                    j = half * 2 + j2
                    p.op("pe", lambda t, e=e, j=j: t.matmul(
                        c.ps[6][:, :], c.eoneh[:, e * 128:(e + 1) * 128],
                        c.w_fm[:, j * 512:(j + 1) * 512], start=True, stop=True),
                        reads=[("s", "wfm", j), ("s", "eoneh")], writes=[("ps", 6)])
                    p.op("dve", lambda v, j2=j2: v.tensor_copy(c.wb_sb[:, j2 * 512:(j2 + 1) * 512], c.ps[6][:, :]),
                         reads=[("ps", 6)], writes=[("s", "wbsb", j2)])
            for grp in range(NGRP):
                for hc16 in range(16):
                    hc = grp * 16 + hc16
                    bA = c.wbuf[(hc16 % 2) * 2]
                    bB = c.wbuf[(hc16 % 2) * 2 + 1]
                    p.op("pool", lambda g, e=e, hc=hc, bA=bA: g.dma_start(
                        out=bA[:, :].rearrange("p (k m) -> p k m", k=8), in_=c.din[FA][e, hc]),
                        writes=wbk((hc16 % 2) * 2), dma=f"qw{(hc16 % 2) * 2}")
                    p.op("pool", lambda g, e=e, hc=hc, bB=bB: g.dma_start(
                        out=bB[:, :].rearrange("p (k m) -> p k m", k=8), in_=c.din[FB][e, hc]),
                        writes=wbk((hc16 % 2) * 2 + 1), dma=f"qw{(hc16 % 2) * 2 + 1}")
                    for j2 in range(2):
                        j = half * 2 + j2
                        bkA = (hc16 * 2 + j2) % 2
                        bkB = 2 + (hc16 * 2 + j2) % 2

                        def hmm(t, buf, bank, j):
                            ins = None
                            for kc in range(DCH):
                                ins = t.matmul(c.ps[bank][:, :], buf[:, kc * 128:(kc + 1) * 128],
                                               c.xn[kc][:, j * 512:(j + 1) * 512],
                                               start=(kc == 0), stop=(kc == DCH - 1))
                            return ins
                        p.op("pe", lambda t, bA=bA, bkA=bkA, j=j: hmm(t, bA, bkA, j),
                             reads=wbk((hc16 % 2) * 2) + [("xn", kc, j) for kc in range(DCH)],
                             writes=[("ps", bkA)])
                        p.op("pe", lambda t, bB=bB, bkB=bkB, j=j: hmm(t, bB, bkB, j),
                             reads=wbk((hc16 % 2) * 2 + 1) + [("xn", kc, j) for kc in range(DCH)],
                             writes=[("ps", bkB)])
                        sA = scr(c, 44)
                        sB = scr(c, 45)
                        p.op("act", lambda s, bkA=bkA, sA=sA: s.activation(
                            sA.v, c.ps[bkA][:, :], AF.Silu, bias=0.0, scale=1.0),
                            reads=[("ps", bkA)], writes=sA.k)
                        if is_shared:
                            p.op("dve", lambda v, bkB=bkB, sB=sB: v.tensor_scalar(
                                sB.v, c.ps[bkB][:, :], 0.5, None, ALU.mult),
                                reads=[("ps", bkB)], writes=sB.k)
                        else:
                            p.op("dve", lambda v, bkB=bkB, sB=sB, j2=j2: v.tensor_tensor(
                                sB.v, c.ps[bkB][:, :], c.wb_sb[:, j2 * 512:(j2 + 1) * 512], ALU.mult),
                                reads=[("ps", bkB), ("s", "wbsb", j2)], writes=sB.k)
                        hb = scr(c, hc16 * 2 + j2)
                        p.op("dve", lambda v, hb=hb, sA=sA, sB=sB: v.tensor_tensor(hb.v, sA.v, sB.v, ALU.mult),
                             reads=sA.k + sB.k, writes=hb.k)
                for dc in range(DCH):
                    wC = c.w2buf[dc % 2]
                    p.op("pool", lambda g, e=e, dc=dc, grp=grp, wC=wC: g.dma_start(
                        out=wC[:, :].rearrange("p (k m) -> p k m", k=16),
                        in_=c.din[FC][e, grp, dc]),
                        writes=[("s", "w2buf", dc % 2)], dma=f"qc{dc % 2}")
                    for j2 in range(2):
                        j = half * 2 + j2
                        bank = 4 + (dc * 2 + j2) % 2

                        def ymm(t, wC=wC, bank=bank, j2=j2):
                            ins = None
                            for k16 in range(16):
                                ins = t.matmul(c.ps[bank][:, :], wC[:, k16 * 128:(k16 + 1) * 128],
                                               c.SCR[:, (k16 * 2 + j2) * 512:(k16 * 2 + j2 + 1) * 512],
                                               start=(k16 == 0), stop=(k16 == 15))
                            return ins
                        p.op("pe", ymm,
                             reads=[("s", "w2buf", dc % 2)] + [("scr", k16 * 2 + j2) for k16 in range(16)],
                             writes=[("ps", bank)])
                        xv = c.x[dc][:, j * 512:(j + 1) * 512]
                        p.op("dve", lambda v, xv=xv, bank=bank: v.tensor_tensor(xv, xv, c.ps[bank][:, :], ALU.add),
                             reads=[("ps", bank), ("x", dc, j)], writes=[("x", dc, j)])



def proj_ktm_vtm(p, c, sub):
    for which, W, brow, bkey in (("ktm", "Wk_r", c.bk_bf, "bk_bf"),
                                 ("vtm", "Wv_r", c.bv_bf, "bv_bf")):
        gbase = 16 if which == "ktm" else 24
        for kh in range(2):
            for kc in range(DCH):
                u = c.wbuf[kc // 2][:, (kc % 2) * 512:(kc % 2 + 1) * 512]
                p.op("pool", lambda g, W=W, kc=kc, kh=kh, u=u: g.dma_start(
                    out=u, in_=c.din[W][kc, :, kh * 512:(kh + 1) * 512]),
                    writes=wbk(kc // 2, kc % 2), dma=f"qw{kc // 2}")
            for tc in range(4):
                bank = tc % 2
                tok = sub * 512 + tc * 128

                def kmm(t, kh=kh, tok=tok, bank=bank, brow=brow):
                    ins = None
                    for kc in range(DCH):
                        u = c.wbuf[kc // 2][:, (kc % 2) * 512:(kc % 2 + 1) * 512]
                        ins = t.matmul(c.ps[bank][:, :], c.xn[kc][:, tok:tok + 128], u,
                                       start=(kc == 0), stop=False)
                    ins = t.matmul(c.ps[bank][:, :], c.ones_row_bf[0:1, tok % 128:tok % 128 + 128],
                                   brow[0:1, kh * 512:(kh + 1) * 512], start=False, stop=True)
                    return ins
                p.op("pe", kmm,
                     reads=[k for kc in range(DCH) for k in wbk(kc // 2, kc % 2)]
                     + [("xn", kc, sub) for kc in range(DCH)]
                     + [("s", bkey), ("s", "ones_row_bf")],
                     writes=[("ps", bank)])
                dst = scr(c, gbase + tc * 2 + kh)
                p.op("dve", lambda v, dst=dst, bank=bank: v.tensor_copy(dst.v, c.ps[bank][:, :]),
                     reads=[("ps", bank)], writes=dst.k)


def emit_attn0(p, c):
    for sub in range(4):
        proj_ktm_vtm(p, c, sub)
        for tc in range(4):
            gchunk = sub * 4 + tc
            for h in range(16):
                bank = 6 + h // 8
                ktm = scr(c, 16 + tc * 2, 2)
                vtm = scr(c, 24 + tc * 2, 2)
                lv = ktm.v[:, h * 64:(h + 1) * 64]
                rv = vtm.v[:, h * 64:(h + 1) * 64]
                ov = c.ps[bank][0:64, (h % 8) * 64:(h % 8 + 1) * 64]
                st = gchunk == 0 and (h % 8) == 0
                p.op("pe", lambda t, lv=lv, rv=rv, ov=ov, st=st: t.matmul(ov, lv, rv, start=st, stop=True),
                     reads=ktm.k + vtm.k, writes=[("ps", bank)])
    SL = scf(c, 0, 2, parts=64)
    for i2 in range(2):
        p.op("dve", lambda v, i2=i2: v.tensor_copy(SL.v[:, i2 * 512:(i2 + 1) * 512], c.ps[6 + i2][0:64, :]),
             reads=[("ps", 6 + i2)], writes=SL.k)
    p.op("sp", lambda s: s.dma_start(out=c.cinL0.ap(), in_=SL.v),
         reads=SL.k, writes=[("d", "cin0")], dma="qsx")
    p.op("pool", lambda g: g.collective_compute(
        "AllGather", ALU.bypass, replica_groups=[list(range(NCORE))],
        ins=[c.cinL0.ap().opt()], outs=[c.coutL0.ap().opt()]),
        reads=[("d", "cin0")], writes=[("d", "cout0")])
    Sp = scf(c, 0, 2, parts=64)
    Ssb = scr(c, 48)

    def sview(h):
        pb = (h % 2) * 64
        return Ssb.v[pb:pb + 64, (h // 2) * 64:(h // 2) * 64 + 64]

    def emit_sel():
        for rg in range(4):
            G = scf(c, 2, 2)
            for r in range(8):
                p.op("sp", lambda s, rg=rg, G=G, r=r: s.dma_start(
                    out=G.v[r * 16:(r + 1) * 16, :],
                    in_=c.coutL0.ap()[r * 64 + rg * 16:r * 64 + rg * 16 + 16, :]),
                    reads=[("d", "cout0")], writes=G.k, dma="qsx")
            for i2 in range(2):
                p.op("pe", lambda t, G=G, i2=i2: t.matmul(
                    c.ps[4][0:16, :], c.selm[:, :], G.v[:, i2 * 512:(i2 + 1) * 512],
                    start=True, stop=True),
                    reads=G.k + [("s", "selm")], writes=[("ps", 4)])
                stv = scf(c, 6, parts=16)
                p.op("dve", lambda v, stv=stv: v.tensor_copy(stv.v, c.ps[4][0:16, :]),
                     reads=[("ps", 4)], writes=stv.k)
                p.op("sp", lambda s, rg=rg, i2=i2, stv=stv: s.dma_start(
                    out=c.dSp[rg * 16:(rg + 1) * 16, i2 * 512:(i2 + 1) * 512], in_=stv.v),
                    reads=stv.k, writes=[("d", "dSp", rg, i2)], dma="qsx")
        p.op("sp", lambda s: s.dma_start(out=Sp.v.rearrange("p (a f) -> p a f", a=2),
                                         in_=c.dSp.rearrange("p (a f) -> p a f", a=2)),
             reads=[("d", "dSp", rg, i2) for rg in range(4) for i2 in range(2)],
             writes=Sp.k, dma="qsx")
    def emit_ssb_init():
        for par in range(2):
            p.op("dve", lambda v, par=par: v.tensor_copy(
                Ssb.v[par * 64:(par + 1) * 64, :].rearrange("p (a b) -> p a b", b=64),
                Sp.v.rearrange("p (a b) -> p a b", b=64)[:, par::2, :]),
                reads=Sp.k, writes=Ssb.k)

    for sub in range(4):
        tok = slice(sub * 512, (sub + 1) * 512)
        for which, W, bias, bkey, gbase in (("q", "Wq_l", c.biases0["bq_pp"], "bq_pp", 0),
                                            ("k", "Wk_l", c.biases0["bk_pp"], "bk_pp", 8)):
            for qc in range(DCH):
                wb = c.wbuf[qc % 2]
                p.op("pool", lambda g, W=W, qc=qc, wb=wb: g.dma_start(
                    out=wb[:, :].rearrange("p (k m) -> p k m", k=8), in_=c.din[W][qc]),
                    writes=wbk(qc % 2), dma=f"qw{qc % 2}")
                bank = qc % 2

                def pmm(t, wb=wb, bank=bank, tok=tok):
                    ins = None
                    for kc in range(DCH):
                        ins = t.matmul(c.ps[bank][:, :], wb[:, kc * 128:(kc + 1) * 128],
                                       c.xn[kc][:, tok], start=(kc == 0), stop=(kc == DCH - 1))
                    return ins
                p.op("pe", pmm,
                     reads=wbk(qc % 2) + [("xn", kc, sub) for kc in range(DCH)],
                     writes=[("ps", bank)])
                dst = scr(c, gbase + qc)
                p.op("act", lambda s, dst=dst, bank=bank, bias=bias, qc=qc: s.activation(
                    dst.v, c.ps[bank][:, :], AF.Identity, bias=bias[:, qc:qc + 1], scale=1.0),
                    reads=[("ps", bank), ("s", bkey)], writes=dst.k)
        proj_ktm_vtm(p, c, sub)
        def nmm(t, tok=tok):
            ins = None
            for kc in range(DCH):
                ins = t.matmul(c.ps[4][0:16, :], c.Wn[:, kc, :], c.xn[kc][:, tok],
                               start=(kc == 0), stop=(kc == DCH - 1))
            return ins
        p.op("pe", nmm, reads=[("s", "Wn")] + [("xn", kc, sub) for kc in range(DCH)],
             writes=[("ps", 4)])
        nsc = Buf(c.SCR[0:16, 44 * 512:45 * 512], [("scr", 44)])
        p.op("act", lambda s, nsc=nsc: s.activation(nsc.v, c.ps[4][0:16, :], AF.Exp,
                                                    bias=c.negbn[:, :], scale=-1.0),
             reads=[("ps", 4), ("s", "negbn")], writes=nsc.k)
        if sub == 0:
            emit_sel()
            emit_ssb_init()
        for tc in range(4):
            gchunk = sub * 4 + tc
            ktm = scr(c, 16 + tc * 2, 2)
            vtm = scr(c, 24 + tc * 2, 2)
            for h in range(16):
                qb = scr(c, 0 + h // 2)
                kb = scr(c, 8 + h // 2)
                pb = (h % 2) * 64
                sbank = h % 2
                att = scr(c, 46 + h % 2)
                p.op("pe", lambda t, kb=kb, qb=qb, pb=pb, tc=tc, sbank=sbank: t.matmul(
                    c.ps[sbank][:, 0:128], kb.v[pb:pb + 64, tc * 128:(tc + 1) * 128],
                    qb.v[pb:pb + 64, tc * 128:(tc + 1) * 128], start=True, stop=True),
                    reads=kb.k + qb.k, writes=[("ps", sbank)])
                p.op("dve", lambda v, att=att, sbank=sbank: v.tensor_tensor(
                    att.v[:, 0:128], c.ps[sbank][:, 0:128], c.triuT[:, :], ALU.mult),
                    reads=[("ps", sbank), ("s", "triuT")], writes=att.k)
                cbank = 2 + h % 2
                p.op("pe", lambda t, vtm=vtm, att=att, h=h, cbank=cbank: t.matmul(
                    c.ps[cbank][0:64, 0:128], vtm.v[:, h * 64:(h + 1) * 64], att.v[:, 0:128],
                    start=True, stop=False),
                    reads=vtm.k + att.k, writes=[("ps", cbank)])
                p.op("pe", lambda t, qb=qb, pb=pb, tc=tc, h=h, cbank=cbank: t.matmul(
                    c.ps[cbank][0:64, 0:128], sview(h),
                    qb.v[pb:pb + 64, tc * 128:(tc + 1) * 128], start=False, stop=True),
                    reads=Ssb.k + qb.k, writes=[("ps", cbank)])
                ctx = scr(c, 32 + h // 2)
                p.op("dve", lambda v, ctx=ctx, pb=pb, tc=tc, cbank=cbank: v.tensor_copy(
                    ctx.v[pb:pb + 64, tc * 128:(tc + 1) * 128], c.ps[cbank][0:64, 0:128]),
                    reads=[("ps", cbank)], writes=ctx.k)
            for h in range(16):
                bank = 6 + h // 8
                lv = ktm.v[:, h * 64:(h + 1) * 64]
                rv = vtm.v[:, h * 64:(h + 1) * 64]
                ov = c.ps[bank][0:64, (h % 8) * 64:(h % 8 + 1) * 64]
                st = gchunk == 0 and (h % 8) == 0
                p.op("pe", lambda t, lv=lv, rv=rv, ov=ov, st=st: t.matmul(ov, lv, rv, start=st, stop=True),
                     reads=ktm.k + vtm.k, writes=[("ps", bank)])
                p.op("dve", lambda v, h=h, bank=bank: v.tensor_tensor(
                    sview(h), c.ps[bank][0:64, (h % 8) * 64:(h % 8 + 1) * 64],
                    Sp.v[:, h * 64:(h + 1) * 64], ALU.add),
                    reads=[("ps", bank)] + Sp.k, writes=Ssb.k)
        for h in range(16):
            p.op("pe", lambda t, h=h, nsc=nsc: t.matmul(
                c.ps[4][0:64, :], c.honehot[:, h * 64:(h + 1) * 64], nsc.v,
                start=True, stop=True),
                reads=nsc.k + [("s", "honehot")], writes=[("ps", 4)])
            ctx = scr(c, 32 + h // 2)
            pb = (h % 2) * 64
            p.op("dve", lambda v, ctx=ctx, pb=pb: v.tensor_tensor(
                ctx.v[pb:pb + 64, :], ctx.v[pb:pb + 64, :], c.ps[4][0:64, :], ALU.mult),
                reads=ctx.k + [("ps", 4)], writes=ctx.k)
        for dc in range(DCH):
            wb = c.wbuf[2 + dc % 2]
            p.op("pool", lambda g, dc=dc, wb=wb: g.dma_start(
                out=wb[:, :].rearrange("p (k m) -> p k m", k=8), in_=c.din["Wo_l"][dc]),
                writes=wbk(2 + dc % 2), dma=f"qw{2 + dc % 2}")

            def omm(t, wb=wb):
                ins = None
                for cc in range(DCH):
                    ctx = scr(c, 32 + cc)
                    ins = t.matmul(c.ps[5][:, :], wb[:, cc * 128:(cc + 1) * 128], ctx.v,
                                   start=(cc == 0), stop=(cc == DCH - 1))
                return ins
            p.op("pe", omm,
                 reads=wbk(2 + dc % 2) + [("scr", 32 + cc) for cc in range(DCH)],
                 writes=[("ps", 5)])
            otmp = scf(c, 5)
            p.op("act", lambda s, otmp=otmp, dc=dc: s.activation(
                otmp.v, c.ps[5][:, :], AF.Identity, bias=c.biases0["bo_pp"][:, dc:dc + 1], scale=1.0),
                reads=[("ps", 5), ("s", "bo_pp")], writes=otmp.k)
            xv = c.x[dc][:, tok]
            p.op("dve", lambda v, xv=xv, otmp=otmp: v.tensor_tensor(xv, xv, otmp.v, ALU.add),
                 reads=otmp.k + [("x", dc, sub)], writes=[("x", dc, sub)])



SER_SEGS = set()


def emit_attn1(p, c):
    def seg(name):
        p.scope_ser = name in SER_SEGS
    uBx = Buf(c.SCR[0:16, 0:2112].rearrange("p (a b) -> p a b", b=33),
              [("scr", g) for g in range(5)])
    hsT = Buf(c.SCR[0:16, 2560:4608], [("scr", g) for g in range(5, 9)])
    PowsBf = Buf(c.SCR[0:16, 4608:5120], [("scr", 9)])
    WCT = Buf(c.SCR[0:16, 5632:6656], [("scr", 11), ("scr", 12)])
    Pows = Buf(c.SCF[0:16, 0:512], [("scf", 0)])
    echn = Buf(c.SCF[0:16, 1024:1088], [("scf", 2)])
    hmid = Buf(c.SCF[0:16, 1536:1537], [("scf", 3)])
    seg("uproj")
    for sub in range(4):
        tok = slice(sub * 512, (sub + 1) * 512)
        for ec in range(DCH):
            wbi = c.wbuf[0 + ec % 2]
            wbg = c.wbuf[2 + ec % 2]
            p.op("pool", lambda g, ec=ec, wbi=wbi: g.dma_start(
                out=wbi[:, :].rearrange("p (k m) -> p k m", k=8), in_=c.din["Wi_l"][ec]),
                writes=wbk(ec % 2), dma=f"qw{ec % 2}")
            p.op("pool", lambda g, ec=ec, wbg=wbg: g.dma_start(
                out=wbg[:, :].rearrange("p (k m) -> p k m", k=8), in_=c.din["Wg_l"][ec]),
                writes=wbk(2 + ec % 2), dma=f"qw{2 + ec % 2}")

            def prmm(t, wb, bank, tok=tok):
                ins = None
                for kc in range(DCH):
                    ins = t.matmul(c.ps[bank][:, :], wb[:, kc * 128:(kc + 1) * 128],
                                   c.xn[kc][:, tok], start=(kc == 0), stop=(kc == DCH - 1))
                return ins
            bank = ec % 2
            p.op("pe", lambda t, wbi=wbi, bank=bank: prmm(t, wbi, bank),
                 reads=wbk(ec % 2) + [("xn", kc, sub) for kc in range(DCH)],
                 writes=[("ps", bank)])
            p.op("pe", lambda t, wbg=wbg, bank=bank: prmm(t, wbg, 2 + bank),
                 reads=wbk(2 + ec % 2) + [("xn", kc, sub) for kc in range(DCH)],
                 writes=[("ps", 2 + bank)])
            xi = scf(c, 5)
            gt = scf(c, 2)
            p.op("act", lambda s, xi=xi, bank=bank, ec=ec: s.activation(
                xi.v, c.ps[bank][:, :], AF.Identity, bias=c.biases1["bi_pp"][:, ec:ec + 1], scale=1.0),
                reads=[("ps", bank), ("s", "bi_pp")], writes=xi.k)
            p.op("act", lambda s, gt=gt, bank=bank, ec=ec: s.activation(
                gt.v, c.ps[2 + bank][:, :], AF.Sigmoid, bias=c.biases1["bg_pp"][:, ec:ec + 1], scale=1.0),
                reads=[("ps", 2 + bank), ("s", "bg_pp")], writes=gt.k)
            ub = scr(c, 16 + ec)
            p.op("dve", lambda v, ub=ub, xi=xi, gt=gt: v.tensor_tensor(ub.v, xi.v, gt.v, ALU.mult),
                 reads=xi.k + gt.k, writes=ub.k)

        def ubmm(t):
            ins = None
            for kc in range(DCH):
                ub = scr(c, 16 + kc)
                ins = t.matmul(c.ps[4][0:16, :], c.Bm[:, kc, :], ub.v,
                               start=(kc == 0), stop=(kc == DCH - 1))
            return ins
        p.op("pe", ubmm, reads=[("s", "Bm")] + [("scr", 16 + kc) for kc in range(DCH)],
             writes=[("ps", 4)])
        p.op("dve", lambda v, sub=sub: v.tensor_copy(
            uBx.v[:, sub * 16:(sub + 1) * 16, 1:33],
            c.ps[4][0:16, :].rearrange("p (a b) -> p a b", b=32)),
            reads=[("ps", 4)], writes=uBx.k)
    seg("powers")
    p.op("dve", lambda v: v.memset(uBx.v[:, :, 0:1], 0.0), writes=uBx.k)
    p.op("sp", lambda s: s.dma_start(out=Pows.v[:, 0:16], in_=c.din["AT"]),
         writes=Pows.k, dma="qsx")
    p.op("dve", lambda v: v.tensor_copy(PowsBf.v[:, 0:16], Pows.v[:, 0:16]),
         reads=Pows.k, writes=PowsBf.k)
    for l in range(2, 33):
        p.op("pe", lambda t, l=l: t.matmul(c.ps[5][0:16, 0:16], c.A_sb[:, :],
                                           Pows.v[:, (l - 2) * 16:(l - 1) * 16],
                                           start=True, stop=True),
             reads=Pows.k + [("s", "A_sb")], writes=[("ps", 5)])
        p.op("dve", lambda v, l=l: v.tensor_copy(Pows.v[:, (l - 1) * 16:l * 16], c.ps[5][0:16, 0:16]),
             reads=[("ps", 5)], writes=Pows.k)
        p.op("dve", lambda v, l=l: v.tensor_copy(PowsBf.v[:, (l - 1) * 16:l * 16], c.ps[5][0:16, 0:16]),
             reads=[("ps", 5)], writes=PowsBf.k)
    CG = [(0, 15), (15, 15), (30, 15), (45, 15), (60, 4)]
    seg("scan")
    for l in range(33):
        lhs = c.I16bf[:, :] if l == 0 else PowsBf.v[:, (l - 1) * 16:l * 16]
        lk = [("s", "I16bf")] if l == 0 else PowsBf.k
        p.op("pe", lambda t, lhs=lhs, l=l: t.matmul(
            c.ps[6][0:16, 0:64], lhs, uBx.v[:, :, 32 - l],
            start=(l == 0), stop=(l == 32)),
            reads=lk + uBx.k, writes=[("ps", 6)])
    p.op("dve", lambda v: v.tensor_copy(echn.v, c.ps[6][0:16, 0:64]),
         reads=[("ps", 6)], writes=echn.k)
    p.op("dve", lambda v: v.memset(c.svec[0][:, :], 0.0), writes=[("s", "sv0")])
    for cc in range(64):
        cur, nxt = c.svec[cc % 2], c.svec[(cc + 1) % 2]
        ck, nk = [("s", f"sv{cc % 2}")], [("s", f"sv{(cc + 1) % 2}")]

        def smm(t, cur=cur, cc=cc):
            t.matmul(c.ps[5][0:16, 0:1], Pows.v[:, 31 * 16:32 * 16], cur[:, :], start=True, stop=False)
            return t.matmul(c.ps[5][0:16, 0:1], c.I16f[:, :], echn.v[:, cc:cc + 1], start=False, stop=True)
        p.op("pe", smm, reads=Pows.k + echn.k + ck + [("s", "I16f")], writes=[("ps", 5)])
        p.op("dve", lambda v, nxt=nxt: v.tensor_copy(nxt[:, :], c.ps[5][0:16, 0:1]),
             reads=[("ps", 5)], writes=nk)
    p.op("sp", lambda s: s.dma_start(out=c.cinL1.ap(), in_=c.svec[0][:, :]),
         reads=[("s", "sv0")], writes=[("d", "cin1")], dma="qsx")
    p.op("pool", lambda g: g.collective_compute(
        "AllGather", ALU.bypass, replica_groups=[list(range(NCORE))],
        ins=[c.cinL1.ap().opt()], outs=[c.coutL1.ap().opt()]),
        reads=[("d", "cin1")], writes=[("d", "cout1")])
    seg("lag")
    for l in range(33):
        for cgi, (c0, ncg) in enumerate(CG):
            lhs = c.I16bf[:, :] if l == 0 else PowsBf.v[:, (l - 1) * 16:l * 16]
            lk = [("s", "I16bf")] if l == 0 else PowsBf.k
            ov = c.ps[cgi][0:16, 0:ncg * 33].rearrange("p (a b) -> p a b", b=33)[:, :, l:33]
            iv = uBx.v[:, c0:c0 + ncg, 0:33 - l]
            p.op("pe", lambda t, lhs=lhs, ov=ov, iv=iv, l=l: t.matmul(
                ov, lhs, iv, start=(l == 0), stop=False),
                reads=lk + uBx.k, writes=[("ps", cgi)])
    seg("out")
    for kc in range(DCH):
        wb = c.wbuf[kc % 2]
        p.op("pool", lambda g, kc=kc, wb=wb: g.dma_start(out=wb[:, :], in_=c.din["WoT_r"][kc]),
             writes=wbk(kc % 2), dma=f"qw{kc % 2}")
        for i2 in range(2):
            p.op("pe", lambda t, kc=kc, wb=wb, i2=i2: t.matmul(
                c.ps[6 + i2][0:16, :], c.Cm[:, kc, :], wb[:, i2 * 512:(i2 + 1) * 512],
                start=(kc == 0), stop=(kc == DCH - 1)),
                reads=wbk(kc % 2) + [("s", "Cm")], writes=[("ps", 6 + i2)])
    for i2 in range(2):
        p.op("dve", lambda v, i2=i2: v.tensor_copy(WCT.v[:, i2 * 512:(i2 + 1) * 512], c.ps[6 + i2][0:16, :]),
             reads=[("ps", 6 + i2)], writes=WCT.k)
    seg("scan")
    p.op("sp", lambda s: s.dma_start(out=c.Gl1[:, :], in_=c.coutL1.ap()),
         reads=[("d", "cout1")], writes=[("s", "Gl1")], dma="qsx")
    p.op("pe", lambda t: t.matmul(c.ps[5][0:16, 0:1], c.selm[:, :], c.Gl1[:, 0:1],
                                  start=True, stop=True),
         reads=[("s", "Gl1"), ("s", "selm")], writes=[("ps", 5)])
    p.op("dve", lambda v: v.tensor_copy(hmid.v, c.ps[5][0:16, 0:1]),
         reads=[("ps", 5)], writes=hmid.k)
    p.op("dve", lambda v: v.tensor_copy(c.svec[0][:, :], hmid.v),
         reads=hmid.k, writes=[("s", "sv0")])
    for cc in range(64):
        cur, nxt = c.svec[cc % 2], c.svec[(cc + 1) % 2]
        ck, nk = [("s", f"sv{cc % 2}")], [("s", f"sv{(cc + 1) % 2}")]
        p.op("dve", lambda v, cur=cur, cc=cc: v.tensor_copy(uBx.v[:, cc, 0:1], cur[:, :]),
             reads=ck, writes=uBx.k)

        def smm2(t, cur=cur, cc=cc):
            t.matmul(c.ps[5][0:16, 0:1], Pows.v[:, 31 * 16:32 * 16], cur[:, :], start=True, stop=False)
            return t.matmul(c.ps[5][0:16, 0:1], c.I16f[:, :], echn.v[:, cc:cc + 1], start=False, stop=True)
        p.op("pe", smm2, reads=Pows.k + echn.k + ck + [("s", "I16f")], writes=[("ps", 5)])
        p.op("dve", lambda v, nxt=nxt: v.tensor_copy(nxt[:, :], c.ps[5][0:16, 0:1]),
             reads=[("ps", 5)], writes=nk)
    seg("impulse")
    for l in range(1, 33):
        for cgi, (c0, ncg) in enumerate(CG):
            ov = c.ps[cgi][0:16, 0:ncg * 33].rearrange("p (a b) -> p a b", b=33)[:, :, l:l + 1]
            iv = uBx.v[:, c0:c0 + ncg, 0:1]
            p.op("pe", lambda t, l=l, ov=ov, iv=iv: t.matmul(
                ov, PowsBf.v[:, (l - 1) * 16:l * 16], iv, start=False, stop=(l == 32)),
                reads=PowsBf.k + uBx.k, writes=[("ps", cgi)])
    for cgi, (c0, ncg) in enumerate(CG):
        p.op("dve", lambda v, cgi=cgi, c0=c0, ncg=ncg: v.tensor_copy(
            hsT.v.rearrange("p (a b) -> p a b", b=32)[:, c0:c0 + ncg, :],
            c.ps[cgi][0:16, 0:ncg * 33].rearrange("p (a b) -> p a b", b=33)[:, :, 1:33]),
            reads=[("ps", cgi)], writes=hsT.k)
    for ec in range(DCH):
        for j in range(4):
            bank = 2 + (ec * 4 + j) % 2
            p.op("pe", lambda t, ec=ec, j=j, bank=bank: t.matmul(
                c.ps[bank][:, :], WCT.v[:, ec * 128:(ec + 1) * 128],
                hsT.v[:, j * 512:(j + 1) * 512], start=True, stop=True),
                reads=WCT.k + hsT.k, writes=[("ps", bank)])
            ot = scf(c, 5)
            p.op("act", lambda s, ot=ot, bank=bank, ec=ec: s.activation(
                ot.v, c.ps[bank][:, :], AF.Identity, bias=c.biases1["bout_pp"][:, ec:ec + 1], scale=1.0),
                reads=[("ps", bank), ("s", "bout_pp")], writes=ot.k)
            xv = c.x[ec][:, j * 512:(j + 1) * 512]
            p.op("dve", lambda v, xv=xv, ot=ot: v.tensor_tensor(xv, xv, ot.v, ALU.add),
                 reads=ot.k + [("x", ec, j)], writes=[("x", ec, j)])
    p.scope_ser = False



DEBUG_DUMPS = []


def build(phases, host):
    p = Prog()
    c = alloc(p, phases, host)
    emit_load_common(p, c, phases)
    for ph in phases:
        emit_ln(p, c)
        if ph in ("ffn0", "ffn1"):
            emit_moe(p, c, ph)
        elif ph == "attn0":
            emit_attn0(p, c)
        elif ph == "attn1":
            emit_attn1(p, c)
    for name, getter, shape, keys in DEBUG_DUMPS:
        ap = getter(c)
        dt = BF16 if "bfloat" in str(ap.dtype) else F32
        dout = p.nc.dram_tensor("dbg_" + name, shape, dt, kind="ExternalOutput").ap()
        p.op("sp", lambda s, dout=dout, ap=ap: s.dma_start(out=dout, in_=ap),
             reads=keys, dma="qsx")
    emit_store(p, c)
    return p.finalize(), c.innames


def run_phases(host, PC, groups, xcur):
    built = {}
    for phases in groups:
        key = tuple(phases)
        if key not in built:
            built[key] = build(phases, host)
        nc, names = built[key]
        in_maps = []
        for cid in range(NCORE):
            m = {}
            for n in names:
                if n == "xin":
                    m[n] = xcur[cid]
                elif n == "selmat":
                    m[n] = PC[cid]["selmat"]
                else:
                    m[n] = host[n]
            in_maps.append(m)
        res = run_bass_kernel_spmd(nc, in_maps, core_ids=list(range(NCORE)))
        xcur = [np.ascontiguousarray(res.results[cid]["y"]) for cid in range(NCORE)]
    return xcur


def kernel(**inputs):
    host, PC = prep_host(inputs)
    groups = [ALL_PHASES] if FUSED else [(ph,) for ph in ALL_PHASES]
    xcur = [PC[cid]["xin"] for cid in range(NCORE)]
    xcur = run_phases(host, PC, groups, xcur)
    out = np.empty((4, 4096, 1024), np.float32)
    for cid in range(NCORE):
        b, hh = divmod(cid, 2)
        out[b, hh * T:(hh + 1) * T, :] = xcur[cid].T
    return out
